# version 12
# baseline (speedup 1.0000x reference)
import contextlib

import numpy as np
import concourse.bass as bass
import concourse.mybir as mybir

F32 = mybir.dt.float32
BF16 = mybir.dt.bfloat16
I32 = mybir.dt.int32
AF = mybir.ActivationFunctionType
ALU = mybir.AluOpType
AX = mybir.AxisListType

NCORE = 8
B, T, D, S, DI = 2, 8192, 1024, 16, 2816
TPC = 2048
QMAX = 127.0
LN_EPS = 1e-5
DT = D // 128
IT = DI // 128
TC = 512
NCH = TPC // TC
TT = TC // 128
TC4 = 256
NCH4 = TPC // TC4
TT4 = TC4 // 128
SC = 1024
NSC = TPC // SC
SC2 = 512


class Ctx:
    def __init__(self, nc, stack):
        self.nc = nc
        self.stack = stack
        self.sems = {}
        self.cnt = {}

    def sem(self, name):
        if name not in self.sems:
            self.sems[name] = self.stack.enter_context(self.nc.semaphore(name))
            self.cnt[name] = 0
        return self.sems[name]

    def inc(self, inst, name, k=1):
        self.sem(name)
        self.cnt[name] += k
        inst.then_inc(self.sems[name], k)
        return inst

    def dma(self, inst, name):
        return self.inc(inst, name, 16)

    def wait(self, eng, name, val=None):
        self.sem(name)
        eng.wait_ge(self.sems[name], self.cnt[name] if val is None else val)

    def vsync(self, eng):
        self.sem("vself")
        self.cnt["vself"] += 1
        eng.engine_nop().then_inc(self.sems["vself"], 1)
        eng.wait_ge(self.sems["vself"], self.cnt["vself"])

    def sb(self, name, shape, dtype=F32):
        return self.stack.enter_context(self.nc.sbuf_tensor(name, shape, dtype))

    def ps(self, name, shape, dtype=F32):
        return self.stack.enter_context(self.nc.psum_tensor(name, shape, dtype))


class Arena:
    def __init__(self, cx, name, cols, dtype):
        self.t = cx.sb(name, [128, cols], dtype)
        self.cols = cols
        self.off = 0

    def reset(self):
        self.off = 0

    def alloc(self, cols):
        assert self.off + cols <= self.cols, (self.off, cols, self.cols)
        sl = self.t[:, self.off:self.off + cols]
        self.off += cols
        return sl


def build_fused():
    nc = bass.Bass("TRN2", target_bir_lowering=False, debug=False,
                   detect_race_conditions=False)
    x_in = nc.dram_tensor("x", [TPC, D], F32, kind="ExternalInput")
    wd_in = nc.dram_tensor("wd", [D, D], F32, kind="ExternalInput")
    bd_in = nc.dram_tensor("bd", [D], F32, kind="ExternalInput")
    alog_in = nc.dram_tensor("alog", [D, S], F32, kind="ExternalInput")
    bpar_in = nc.dram_tensor("bpar", [D, S], F32, kind="ExternalInput")
    wo_in = nc.dram_tensor("wo", [D, D], F32, kind="ExternalInput")
    bo_in = nc.dram_tensor("bo", [D], F32, kind="ExternalInput")
    clipx_in = nc.dram_tensor("clip_x", [1, 1], F32, kind="ExternalInput")
    clipd_in = nc.dram_tensor("clip_down", [1, 1], F32, kind="ExternalInput")
    wg_in = nc.dram_tensor("wg", [DI, D], F32, kind="ExternalInput")
    wu_in = nc.dram_tensor("wu", [DI, D], F32, kind="ExternalInput")
    wdn_in = nc.dram_tensor("wdn", [D, DI], F32, kind="ExternalInput")
    gmask_in = nc.dram_tensor("gmask", [128, NCORE], F32, kind="ExternalInput")
    out_out = nc.dram_tensor("out", [TPC, D], F32, kind="ExternalOutput")

    delta_hb = nc.dram_tensor("delta_hb", [D, TPC], F32, kind="Internal")
    v_hb = nc.dram_tensor("v_hb", [D, TPC], F32, kind="Internal")
    x2_hb = nc.dram_tensor("x2_hb", [TPC, D], F32, kind="Internal")
    ln2_hb = nc.dram_tensor("ln2_hb", [TPC, D], F32, kind="Internal")
    h_hb = nc.dram_tensor("h_hb", [DI, TPC], F32, kind="Internal")
    cum_hb = nc.dram_tensor("cum_hb", [D, TPC], F32, kind="Internal")
    y0_hb = nc.dram_tensor("y0_hb", [D, TPC], F32, kind="Internal")
    cc_sum_in = nc.dram_tensor("cc_sum_in", [128, 2 * DT * S], F32, kind="Internal")
    cc_sum_out = nc.dram_tensor("cc_sum_out", [NCORE * 128, 2 * DT * S], F32,
                                kind="Internal")
    cc_mx_in = nc.dram_tensor("cc_mx_in", [1, 4], F32, kind="Internal")
    cc_mx_out = nc.dram_tensor("cc_mx_out", [1, 4], F32, kind="Internal")
    cc_mx2_in = nc.dram_tensor("cc_mx2_in", [1, 4], F32, kind="Internal")
    cc_mx2_out = nc.dram_tensor("cc_mx2_out", [1, 4], F32, kind="Internal")

    stack = contextlib.ExitStack()
    cx = Ctx(nc, stack)
    rg = [list(range(NCORE))]

    ident = cx.sb("ident", [128, 128], F32)
    identb = cx.sb("identb", [128, 128], BF16)
    apos = cx.sb("apos", [128, DT * S], F32)
    nega = cx.sb("nega", [128, DT * S], F32)
    bpar = cx.sb("bpar_sb", [128, DT * S], F32)
    t0s = cx.sb("t0s", [128, DT * S], F32)
    t1s = cx.sb("t1s", [128, DT * S], F32)
    bd_sb = cx.sb("bd_sb", [128, DT], F32)
    bd_row = cx.sb("bd_row", [1, D], F32)
    bo_row = cx.sb("bo_row", [1, D], F32)
    onesr = cx.sb("onesr", [1, 128], F32)
    dsum = cx.sb("dsum", [128, DT], F32)
    carry = cx.sb("carry", [128, DT * S], F32)
    aend = cx.sb("aend", [128, DT * S], F32)
    sin = cx.sb("sin", [128, DT * S], F32)
    cc = cx.sb("cc", [128, DT * S], F32)
    gmask = cx.sb("gmask_sb", [128, NCORE], F32)
    summ = cx.sb("summ", [128, 2 * DT * S], F32)
    stat = cx.sb("stat", [128, 4], F32)
    amax = cx.sb("amax", [128, 1], F32)
    amaxt = cx.sb("amaxt", [128, 1], F32)
    fin = cx.sb("fin", [1, 128], F32)
    mx4 = cx.sb("mx4", [1, 4], F32)
    clip_sb = cx.sb("clip_sb", [1, 1], F32)
    sxg = cx.sb("sxg", [1, 1], F32)
    rsxg = cx.sb("rsxg", [1, 1], F32)
    rsx_bc = cx.sb("rsx_bc", [128, 1], F32)
    sx_bc = cx.sb("sx_bc", [128, 1], F32)
    rowm = cx.sb("rowm", [128, 1], F32)
    rs_w = cx.sb("rs_w", [128, 1], F32)
    sg = cx.sb("sg", [128, IT], F32)
    su = cx.sb("su", [128, IT], F32)
    sw_col = cx.sb("sw_col", [128, DT], F32)
    swd_row = cx.sb("swd_row", [1, D], F32)
    swd_bc = cx.sb("swd_bc", [128, D], F32)

    AB = Arena(cx, "arena_b", 53504, BF16)
    AFR = Arena(cx, "arena_f", 16384, F32)
    i32b = cx.sb("i32b", [128, 1024], I32)
    i32c = cx.sb("i32c", [128, 1024], I32)
    i32d = cx.sb("i32d", [128, 1024], I32)

    ptr = [cx.ps(f"ptr{i}", [128, 128], F32) for i in range(2)]
    ptb = [cx.ps(f"ptb{i}", [128, 128], BF16) for i in range(2)]
    pmm = [cx.ps(f"pmm{i}", [128, 512], F32) for i in range(4)]

    with nc.Block() as blk:
        @blk.gpsimd
        def _(g):
            cx.inc(g.iota(ident[:], pattern=[[1, 128]], base=0,
                          channel_multiplier=-1,
                          allow_small_or_imprecise_dtypes=True), "idg0")
            cx.wait(g, "idg0")
            cx.inc(g.tensor_single_scalar(ident[:], ident[:], 0.0, ALU.is_equal),
                   "idg")

        @blk.vector
        def _(v):
            cx.wait(v, "idg")
            cx.inc(v.tensor_copy(identb[:], ident[:]), "idb")

        @blk.sync
        def _(sync):
            cx.dma(sync.dma_start(
                t0s.ap().rearrange("p (j s) -> p j s", s=S),
                alog_in.ap().rearrange("(j p) s -> p j s", p=128)), "ld0")
            cx.dma(sync.dma_start(
                bpar.ap().rearrange("p (j s) -> p j s", s=S),
                bpar_in.ap().rearrange("(j p) s -> p j s", p=128)), "ld0")
            cx.dma(sync.dma_start(
                bd_row[0:1, :], bd_in.ap().rearrange("(a d) -> a d", a=1)), "ld0")
            cx.dma(sync.dma_start(
                bo_row[0:1, :], bo_in.ap().rearrange("(a d) -> a d", a=1)), "ld0")
            cx.dma(sync.dma_start(gmask[:, :], gmask_in.ap()), "ld0")

        @blk.scalar
        def _(s):
            cx.wait(s, "ld0")
            s.activation(t1s[:], t0s[:], AF.Abs)
            s.activation(t1s[:], t1s[:], AF.Exp, scale=-1.0)
            cx.inc(s.activation(t1s[:], t1s[:], AF.Ln, bias=1.0), "sps")

        @blk.vector
        def _(v):
            cx.wait(v, "sps")
            v.tensor_scalar_max(apos[:], t0s[:], 0.0)
            cx.vsync(v)
            v.tensor_add(apos[:], apos[:], t1s[:])
            cx.vsync(v)
            v.tensor_scalar_add(apos[:], apos[:], 1e-4)
            cx.vsync(v)
            v.tensor_scalar_mul(nega[:], apos[:], -1.0)
            cx.vsync(v)
            v.memset(onesr[0:1, :], 1.0)
            v.memset(carry[:], 0.0)
            v.memset(dsum[:], 0.0)
            cx.inc(v.memset(amax[:], 0.0), "ini")

        @blk.tensor
        def _(t):
            cx.wait(t, "ini")
            cx.wait(t, "idg")
            cx.sem("bdev")
            for j in range(DT):
                if j >= 2:
                    cx.wait(t, "bdev", j - 1)
                cx.inc(t.transpose(ptr[j % 2][:, 0:1],
                                   bd_row[0:1, j * 128:(j + 1) * 128],
                                   ident[0:1, 0:1]), "bdtr")

        @blk.vector
        def _(v):
            for j in range(DT):
                cx.wait(v, "bdtr", j + 1)
                cx.inc(v.tensor_copy(bd_sb[:, j:j + 1], ptr[j % 2][:, 0:1]), "bdev")

        @blk.tensor
        def _(t):
            cx.wait(t, "bdev")

    wT_h = AB.alloc(DT * 1024)
    wT_l = AB.alloc(DT * 1024)
    wnat_h = AB.alloc(1024)
    wnat_l = AB.alloc(1024)
    AB_base_p1 = AB.off

    def emit_wsplit(blk, w_dram, tag, wstage):
        cx.sem("wev")
        cx.sem("vsp")
        cx.sem("wtr")
        b_vsp = cx.cnt["vsp"]
        b_wtr = cx.cnt["wtr"]
        b_wev = cx.cnt["wev"]
        if True:
            @blk.sync
            def _(sync):
                for j in range(DT):
                    if j >= 1:
                        cx.wait(sync, "vsp", b_vsp + j)
                    cx.dma(sync.dma_start(wstage,
                                          w_dram.ap()[j * 128:(j + 1) * 128, :]),
                           "ldw")

            @blk.vector
            def _(v):
                for j in range(DT):
                    cx.wait(v, "ldw", (cx.cnt["ldw"] - 16 * (DT - 1 - j)))
                    if j >= 1:
                        cx.wait(v, "wtr", b_wtr + 2 * DT * j)
                    v.tensor_copy(wnat_h, wstage)
                    cx.inc(v.tensor_tensor(wnat_l, wstage, wnat_h, ALU.subtract),
                           "vsp")
                    for k in range(DT):
                        cx.wait(v, "wtr", b_wtr + 2 * DT * j + 2 * (k + 1))
                        v.tensor_copy(
                            wT_h[:, k * 1024 + j * 128: k * 1024 + (j + 1) * 128],
                            ptb[0][:, 0:128])
                        cx.inc(v.tensor_copy(
                            wT_l[:, k * 1024 + j * 128: k * 1024 + (j + 1) * 128],
                            ptb[1][:, 0:128]), "wev")

            @blk.tensor
            def _(t):
                for j in range(DT):
                    cx.wait(t, "vsp", b_vsp + j + 1)
                    for k in range(DT):
                        if DT * j + k >= 1:
                            cx.wait(t, "wev", b_wev + DT * j + k)
                        t.transpose(ptb[0][:, 0:128],
                                    wnat_h[:, k * 128:(k + 1) * 128], identb[:])
                        cx.inc(t.transpose(ptb[1][:, 0:128],
                                           wnat_l[:, k * 128:(k + 1) * 128],
                                           identb[:]), "wtr", 2)

    AFR.reset()
    wstage = AFR.alloc(1024)
    xstage = AFR.alloc(TT * 1024)
    xln_tm = AFR.alloc(TT * 1024)
    xlnT = AFR.alloc(DT * TC)
    sqs = AFR.alloc(1024)
    dstage = AFR.alloc(TC)
    vstage = AFR.alloc(TC)
    sp0 = AFR.alloc(TC)
    sp1 = sqs[:, 0:TC]
    lstats1 = AFR.alloc(8 * TT)
    xh_off = AB.off
    xh = AB.alloc(DT * TC)
    xl = AB.alloc(DT * TC)

    with nc.Block() as blk:
        emit_wsplit(blk, wd_in, "d", wstage)

    def emit_ln2(blk, xsrc, lnout, sqs_, stats, lnmxb, mxbase, ready_sem,
                 ready_vals):
        ms = stats[:, 0:TT]
        vs = stats[:, TT:2 * TT]
        t1 = stats[:, 2 * TT:3 * TT]
        t2 = stats[:, 3 * TT:4 * TT]
        rr = stats[:, 4 * TT:5 * TT]
        nb = stats[:, 5 * TT:6 * TT]
        for nm in ("lsq", "lst", "lnq", "lns", "lnr", "lnd", "lnx"):
            cx.sem(nm)
        b_lsq = cx.cnt["lsq"]
        b_lst = cx.cnt["lst"]
        b_lnr = cx.cnt["lnr"]
        b_lnd = cx.cnt["lnd"]

        @blk.scalar
        def _(s):
            for a in range(TT):
                cx.wait(s, ready_sem, ready_vals[a])
                if b_lst + a >= 1:
                    cx.wait(s, "lst", b_lst + a)
                cx.inc(s.activation(sqs_, xsrc[:, a * D:(a + 1) * D],
                                    AF.Square), "lsq")

        @blk.vector
        def _(v):
            for a in range(TT):
                cx.wait(v, "lsq", b_lsq + a + 1)
                v.reduce_sum(ms[:, a:a + 1], xsrc[:, a * D:(a + 1) * D],
                             axis=AX.X)
                cx.inc(v.reduce_sum(vs[:, a:a + 1], sqs_, axis=AX.X), "lst")
            cx.vsync(v)
            v.tensor_scalar_mul(ms, ms, 1.0 / D)
            v.tensor_scalar_mul(vs, vs, 1.0 / D)
            cx.vsync(v)
            v.tensor_tensor(t1, ms, ms, ALU.mult)
            cx.vsync(v)
            v.tensor_tensor(vs, vs, t1, ALU.subtract)
            cx.vsync(v)
            cx.inc(v.tensor_scalar_add(t1, vs, LN_EPS), "lnq")

        @blk.scalar
        def _(s):
            cx.wait(s, "lnq")
            cx.inc(s.activation(t2, t1, AF.Sqrt), "lns")

        @blk.vector
        def _(v):
            cx.wait(v, "lns")
            v.reciprocal(rr, t2)
            cx.vsync(v)
            v.tensor_tensor(t2, rr, rr, ALU.mult)
            cx.vsync(v)
            v.tensor_tensor(t2, t1, t2, ALU.mult)
            cx.vsync(v)
            v.tensor_scalar(t2, t2, -0.5, 1.5, ALU.mult, ALU.add)
            cx.vsync(v)
            v.tensor_tensor(rr, rr, t2, ALU.mult)
            cx.vsync(v)
            v.tensor_tensor(nb, ms, rr, ALU.mult)
            cx.vsync(v)
            cx.inc(v.tensor_scalar_mul(nb, nb, -1.0), "lnr")
            cx.vsync(v)

        @blk.scalar
        def _(s):
            cx.wait(s, "lnr", b_lnr + 1)
            for a in range(TT):
                cx.inc(s.activation(lnout[:, a * D:(a + 1) * D],
                                    xsrc[:, a * D:(a + 1) * D], AF.Identity,
                                    bias=nb[:, a:a + 1],
                                    scale=rr[:, a:a + 1]), "lnd")

        if lnmxb is not None:
            @blk.vector
            def _(v):
                for a in range(TT):
                    cx.wait(v, "lnd", b_lnd + a + 1)
                    cx.inc(v.reduce_max(lnmxb[:, mxbase + a:mxbase + a + 1],
                                        lnout[:, a * D:(a + 1) * D],
                                        axis=AX.X,
                                        apply_absolute_value=True), "lnx")


    for c in range(NCH):
        with nc.Block() as blk:
            @blk.sync
            def _(sync):
                cx.dma(sync.dma_start(
                    xstage.rearrange("p (a d) -> p a d", d=D),
                    x_in.ap()[c * TC:(c + 1) * TC, :].rearrange(
                        "(a p) d -> p a d", p=128)), "ldx")

            emit_ln2(blk, xstage, xln_tm, sqs, lstats1, None, 0, "ldx",
                     [cx.cnt["ldx"]] * TT)

            @blk.tensor
            def _(t):
                cx.sem("xev")
                cx.wait(t, "lnd")
                base = cx.cnt["xev"]
                for k in range(DT):
                    for a in range(TT):
                        g = k * TT + a
                        if base + g >= 2:
                            cx.wait(t, "xev", base + g - 1)
                        cx.inc(t.transpose(
                            ptr[g % 2][:, 0:128],
                            xln_tm[:, a * D + k * 128: a * D + (k + 1) * 128],
                            ident[:]), "xtr")

            @blk.vector
            def _(v):
                base = cx.cnt["xtr"] - DT * TT
                for k in range(DT):
                    for a in range(TT):
                        g = k * TT + a
                        cx.wait(v, "xtr", base + g + 1)
                        cx.inc(v.tensor_copy(
                            xlnT[:, k * TC + a * 128: k * TC + (a + 1) * 128],
                            ptr[g % 2][:, 0:128]), "xev")
                for k in range(DT):
                    v.tensor_copy(xh[:, k * TC:(k + 1) * TC],
                                  xlnT[:, k * TC:(k + 1) * TC])
                    cx.inc(v.tensor_tensor(xl[:, k * TC:(k + 1) * TC],
                                           xlnT[:, k * TC:(k + 1) * TC],
                                           xh[:, k * TC:(k + 1) * TC],
                                           ALU.subtract), "xsp")

            for et in range(DT):
                g = c * DT + et
                pm = pmm[et % 2]

                @blk.tensor
                def _(t, et=et, g=g, pm=pm):
                    if et == 0:
                        cx.wait(t, "xsp")
                    if g >= 2:
                        cx.wait(t, "spev", g - 1)
                    first = True
                    for k in range(DT):
                        lh = wT_h[:, k * 1024 + et * 128: k * 1024 + (et + 1) * 128]
                        ll = wT_l[:, k * 1024 + et * 128: k * 1024 + (et + 1) * 128]
                        rh = xh[:, k * TC:(k + 1) * TC]
                        rl = xl[:, k * TC:(k + 1) * TC]
                        for lhs, rhs in ((lh, rh), (lh, rl), (ll, rh)):
                            last = (k == DT - 1) and (lhs is ll)
                            mi = t.matmul(pm[:, :], lhs, rhs, start=first, stop=last)
                            if last:
                                cx.inc(mi, "mm")
                            first = False

                @blk.scalar
                def _(s, et=et, g=g, pm=pm):
                    cx.wait(s, "mm")
                    if g >= 1:
                        cx.wait(s, "spev", g)
                    bcol = bd_sb[:, et:et + 1]
                    s.activation(sp0, pm[:, :], AF.Abs, bias=bcol)
                    s.activation(sp1, sp0, AF.Exp, scale=-1.0)
                    cx.inc(s.activation(xstage[:, 0:TC], sp1, AF.Ln, bias=1.0),
                           "spl")

                @blk.vector
                def _(v, et=et, g=g, pm=pm):
                    cx.wait(v, "spl")
                    if g >= 1:
                        cx.wait(v, "sto1", 32 * g)
                    bcol = bd_sb[:, et:et + 1]
                    v.scalar_tensor_tensor(dstage, pm[:, :], bcol, sp0,
                                           ALU.add, ALU.add)
                    v.scalar_tensor_tensor(dstage, dstage, 0.5,
                                           xstage[:, 0:TC],
                                           ALU.mult, ALU.add)
                    v.scalar_tensor_tensor(vstage, dstage, 0.0,
                                           xlnT[:, et * TC:(et + 1) * TC],
                                           ALU.add, ALU.mult)
                    v.reduce_sum(amaxt[:], dstage, axis=AX.X)
                    cx.vsync(v)
                    cx.inc(v.tensor_add(dsum[:, et:et + 1], dsum[:, et:et + 1],
                                        amaxt[:]), "spev")

                @blk.sync
                def _(sync, et=et, c=c):
                    cx.wait(sync, "spev")
                    cx.dma(sync.dma_start(
                        delta_hb.ap()[et * 128:(et + 1) * 128, c * TC:(c + 1) * TC],
                        dstage), "sto1")
                    cx.dma(sync.dma_start(
                        v_hb.ap()[et * 128:(et + 1) * 128, c * TC:(c + 1) * TC],
                        vstage), "sto1")

    AFR.reset()
    dj = AFR.alloc(TPC)
    vjb = AFR.alloc(TPC)
    cumj = AFR.alloc(TPC)
    y0j = AFR.alloc(TPC)
    abz = [AFR.alloc(TPC) for _ in range(2)]
    sgz = [AFR.alloc(TPC) for _ in range(2)]

    with nc.Block() as blk:
        @blk.sync
        def _(sync):
            cx.wait(sync, "sto1")

        @blk.vector
        def _(v):
            v.memset(swd_bc[:, :], 1.0)

    for nm in ("zse", "zscs", "zdone", "y0st"):
        cx.sem(nm)
    b_zse = cx.cnt["zse"]
    b_zscs = cx.cnt["zscs"]
    b_zd = cx.cnt["zdone"]
    b_y0 = cx.cnt["y0st"]
    with nc.Block() as blk:
        for j in range(DT):
            @blk.sync
            def _(sync, j=j):
                if j >= 1:
                    cx.wait(sync, "zdone", b_zd + j)
                    cx.dma(sync.dma_start(
                        y0_hb.ap()[(j - 1) * 128:j * 128, :], y0j), "y0st")
                    cx.dma(sync.dma_start(
                        cum_hb.ap()[(j - 1) * 128:j * 128, :], cumj), "y0st")
                    cx.wait(sync, "zse", b_zse + S * j)
                cx.dma(sync.dma_start(
                    dj, delta_hb.ap()[j * 128:(j + 1) * 128, :]), "ldz")
                cx.dma(sync.dma_start(
                    vjb, v_hb.ap()[j * 128:(j + 1) * 128, :]), "ldz")

            @blk.scalar
            def _(s, j=j):
                cx.wait(s, "ldz")
                for si in range(S):
                    gs = j * S + si
                    if gs >= 2:
                        cx.wait(s, "zscs", b_zscs + gs - 1)
                    cx.inc(s.activation(
                        abz[si % 2], dj, AF.Exp,
                        scale=nega[:, j * S + si:j * S + si + 1]), "zse")

            @blk.vector
            def _(v, j=j):
                cx.wait(v, "ldz")
                if j >= 1:
                    cx.wait(v, "y0st", b_y0 + 32 * j)
                half = TPC // 2
                v.tensor_tensor_scan(cumj[:, 0:half], swd_bc[:, 0:half],
                                     dj[:, 0:half], 0.0, ALU.mult, ALU.add)
                for si in range(S):
                    if si >= 2:
                        v.tensor_copy(
                            carry[:, j * S + si - 2:j * S + si - 1],
                            sgz[si % 2][:, TPC - 1:TPC])
                    cx.wait(v, "zse", b_zse + j * S + si + 1)
                    cx.inc(v.tensor_tensor_scan(
                        sgz[si % 2], abz[si % 2], vjb, 0.0,
                        ALU.mult, ALU.add), "zscs")
                    if si == 0:
                        v.tensor_tensor_scan(
                            cumj[:, half:TPC], swd_bc[:, 0:half],
                            dj[:, half:TPC], cumj[:, half - 1:half],
                            ALU.mult, ALU.add)
                        v.scalar_tensor_tensor(
                            y0j, sgz[0], bpar[:, j * S:j * S + 1], sgz[0],
                            ALU.mult, ALU.bypass)
                    else:
                        v.scalar_tensor_tensor(
                            y0j, sgz[si % 2],
                            bpar[:, j * S + si:j * S + si + 1], y0j,
                            ALU.mult, ALU.add)
                cx.vsync(v)
                v.tensor_copy(carry[:, j * S + S - 2:j * S + S - 1],
                              sgz[0][:, TPC - 1:TPC])
                cx.inc(v.tensor_copy(carry[:, j * S + S - 1:j * S + S],
                                     sgz[1][:, TPC - 1:TPC]), "zdone")

        @blk.sync
        def _(sync):
            cx.wait(sync, "zdone", b_zd + DT)
            cx.dma(sync.dma_start(
                y0_hb.ap()[(DT - 1) * 128:DT * 128, :], y0j), "y0st")
            cx.dma(sync.dma_start(
                cum_hb.ap()[(DT - 1) * 128:DT * 128, :], cumj), "y0st")
            cx.wait(sync, "y0st")

    with nc.Block() as blk:
        @blk.vector
        def _(v):
            cx.wait(v, "zdone")
            for j in range(DT):
                cx.inc(v.tensor_scalar_mul(t0s[:, j * S:(j + 1) * S],
                                           nega[:, j * S:(j + 1) * S],
                                           dsum[:, j:j + 1]), "ae0")
            cx.vsync(v)

        @blk.scalar
        def _(s):
            cx.wait(s, "ae0")
            cx.inc(s.activation(aend[:], t0s[:], AF.Exp), "ae1")

        @blk.sync
        def _(sync):
            cx.wait(sync, "ae1")
            cx.dma(sync.dma_start(cc_sum_in.ap()[:, 0:DT * S], carry[:, :]), "cci")
            cx.dma(sync.dma_start(cc_sum_in.ap()[:, DT * S:], aend[:, :]), "cci")

        @blk.gpsimd
        def _(g):
            g.wait_ge(cx.sems["cci"], cx.cnt["cci"])
            cx.inc(g.collective_compute(
                "AllGather", ALU.bypass, replica_groups=rg,
                ins=[cc_sum_in.ap()], outs=[cc_sum_out.ap()]), "cc1")

        @blk.vector
        def _(v):
            cx.wait(v, "cc1")
            cx.inc(v.memset(sin[:], 0.0), "sini")

    for p in range(NCORE):
        with nc.Block() as blk:
            @blk.sync
            def _(sync):
                if p == 0:
                    cx.wait(sync, "sini")
                cx.dma(sync.dma_start(
                    summ[:, :], cc_sum_out.ap()[p * 128:(p + 1) * 128, :]), "lds")

            @blk.vector
            def _(v):
                cx.wait(v, "lds")
                gcol = gmask[:, p:p + 1]
                v.tensor_scalar_mul(t0s[:], summ[:, DT * S:], gcol)
                v.tensor_scalar(amaxt[:], gcol, -1.0, 1.0, ALU.mult, ALU.add)
                cx.vsync(v)
                v.tensor_scalar_add(t0s[:], t0s[:], amaxt[:, 0:1])
                cx.vsync(v)
                v.tensor_mul(sin[:], sin[:], t0s[:])
                v.tensor_scalar_mul(t1s[:], summ[:, 0:DT * S], gcol)
                cx.vsync(v)
                cx.inc(v.tensor_add(sin[:], sin[:], t1s[:]), "scomp")
                cx.vsync(v)

    with nc.Block() as blk:
        @blk.vector
        def _(v):
            cx.wait(v, "scomp")
            cx.inc(v.tensor_tensor(cc[:], bpar[:], sin[:], ALU.mult), "ccd")

    AFR.reset()
    cu2 = [AFR.alloc(TPC) for _ in range(2)]
    yy = [AFR.alloc(TPC) for _ in range(2)]
    pfac = [AFR.alloc(TPC) for _ in range(2)]
    wstage2 = AFR.alloc(1024)
    AB.off = xh_off
    yh = AB.alloc(DT * TPC)
    yl = AB.alloc(DT * TPC)

    for nm in ("pfe", "yco", "ysp2", "ldy"):
        cx.sem(nm)
    b_pfe = cx.cnt["pfe"]
    b_yco = cx.cnt["yco"]
    b_ysp2 = cx.cnt["ysp2"]
    with nc.Block() as blk:
        emit_wsplit(blk, wo_in, "o", wstage2)
        for j in range(DT):
            jb = j % 2

            @blk.sync
            def _(sync, j=j, jb=jb):
                if j >= 2:
                    cx.wait(sync, "pfe", b_pfe + S * (j - 1))
                    cx.wait(sync, "ysp2", b_ysp2 + j - 1)
                cx.dma(sync.dma_start(
                    cu2[jb], cum_hb.ap()[j * 128:(j + 1) * 128, :]), "ldy")
                cx.dma(sync.dma_start(
                    yy[jb], y0_hb.ap()[j * 128:(j + 1) * 128, :]), "ldy")

            @blk.scalar
            def _(s, j=j, jb=jb):
                cx.wait(s, "ldy", 32 * (j + 1))
                if j == 0:
                    cx.wait(s, "ccd")
                for si in range(S):
                    gs = j * S + si
                    if gs >= 2:
                        cx.wait(s, "yco", b_yco + gs - 1)
                    cx.inc(s.activation(
                        pfac[si % 2], cu2[jb], AF.Exp,
                        scale=nega[:, j * S + si:j * S + si + 1]), "pfe")

            @blk.vector
            def _(v, j=j, jb=jb):
                for si in range(S):
                    cx.wait(v, "pfe", b_pfe + j * S + si + 1)
                    cx.inc(v.scalar_tensor_tensor(
                        yy[jb], pfac[si % 2],
                        cc[:, j * S + si:j * S + si + 1], yy[jb],
                        ALU.mult, ALU.add), "yco")
                v.tensor_copy(yh[:, j * TPC:(j + 1) * TPC], yy[jb])
                cx.inc(v.scalar_tensor_tensor(
                    yl[:, j * TPC:(j + 1) * TPC], yy[jb], 0.0,
                    yh[:, j * TPC:(j + 1) * TPC],
                    ALU.add, ALU.subtract), "ysp2")

    AFR.reset()
    x2s = AFR.alloc(TT * 1024)
    ln2s = AFR.alloc(TT * 1024)
    sqs2 = AFR.alloc(1024)
    lstats = AFR.alloc(8 * TT)
    lnmx = AFR.alloc(NCH * TT)

    for c in range(NCH):
        with nc.Block() as blk:
            @blk.sync
            def _(sync):
                cx.dma(sync.dma_start(
                    x2s.rearrange("p (a d) -> p a d", d=D),
                    x_in.ap()[c * TC:(c + 1) * TC, :].rearrange(
                        "(a p) d -> p a d", p=128)), "ldx2")

            for a in range(TT):
                gt = c * TT + a

                @blk.tensor
                def _(t, a=a, gt=gt):
                    for eb in range(2):
                        gg = 2 * gt + eb
                        if gg >= 2:
                            cx.wait(t, "x2ev", gg - 1)
                        t.matmul(pmm[eb][:, 0:512], onesr[0:1, :],
                                 bo_row[0:1, eb * 512:(eb + 1) * 512],
                                 start=True, stop=False)
                        for k in range(DT):
                            lh = yh[:, k * TPC + c * TC + a * 128:
                                    k * TPC + c * TC + (a + 1) * 128]
                            ll = yl[:, k * TPC + c * TC + a * 128:
                                    k * TPC + c * TC + (a + 1) * 128]
                            rh = wT_h[:, k * 1024 + eb * 512: k * 1024 + (eb + 1) * 512]
                            rl = wT_l[:, k * 1024 + eb * 512: k * 1024 + (eb + 1) * 512]
                            for lhs, rhs in ((lh, rh), (lh, rl), (ll, rh)):
                                last = (k == DT - 1) and (lhs is ll)
                                mi = t.matmul(pmm[eb][:, 0:512], lhs, rhs,
                                              start=False, stop=last)
                                if last:
                                    cx.inc(mi, "mmo")

                @blk.vector
                def _(v, a=a):
                    cx.wait(v, "ldx2")
                    xrow = x2s[:, a * D:(a + 1) * D]
                    for eb in range(2):
                        cx.wait(v, "mmo", cx.cnt["mmo"] - (2 - 1 - eb))
                        sl = slice(eb * 512, (eb + 1) * 512)
                        cx.inc(v.scalar_tensor_tensor(
                            xrow[:, sl], pmm[eb][:, 0:512], 0.0, xrow[:, sl],
                            ALU.add, ALU.add), "x2ev")

            x2ev_now = cx.cnt["x2ev"]
            emit_ln2(blk, x2s, ln2s, sqs2, lstats, lnmx, c * TT,
                     "x2ev", [x2ev_now - 2 * (TT - 1 - a) for a in range(TT)])

            @blk.sync
            def _(sync):
                cx.wait(sync, "lnd")
                cx.dma(sync.dma_start(
                    x2_hb.ap()[c * TC:(c + 1) * TC, :].rearrange(
                        "(a p) d -> p a d", p=128),
                    x2s.rearrange("p (a d) -> p a d", d=D)), "sto2")
                cx.dma(sync.dma_start(
                    ln2_hb.ap()[c * TC:(c + 1) * TC, :].rearrange(
                        "(a p) d -> p a d", p=128),
                    ln2s.rearrange("p (a d) -> p a d", d=D)), "sto2")
                cx.wait(sync, "sto2")

    with nc.Block() as blk:
        @blk.vector
        def _(v):
            cx.wait(v, "lnx")
            cx.inc(v.reduce_max(amax[:], lnmx[:, 0:NCH * TT], axis=AX.X),
                   "lfm")

        @blk.tensor
        def _(t):
            cx.wait(t, "lfm")

    def emit_allreduce_max(cc_in_d, cc_out_d, tag):
        with nc.Block() as blk:
            @blk.tensor
            def _(t):
                cx.inc(t.transpose(ptr[0][0:1, 0:128], amax[:, 0:1], ident[:]),
                       "ftr")

            @blk.vector
            def _(v):
                cx.wait(v, "ftr")
                v.tensor_copy(fin[0:1, :], ptr[0][0:1, 0:128])
                cx.vsync(v)
                v.reduce_max(mx4[0:1, 0:1], fin[0:1, :], axis=AX.X)
                cx.vsync(v)
                v.tensor_copy(mx4[0:1, 1:2], mx4[0:1, 0:1])
                v.tensor_copy(mx4[0:1, 2:3], mx4[0:1, 0:1])
                cx.inc(v.tensor_copy(mx4[0:1, 3:4], mx4[0:1, 0:1]), "fmx")

            @blk.sync
            def _(sync):
                cx.wait(sync, "fmx")
                cx.dma(sync.dma_start(cc_in_d.ap(), mx4[0:1, :]), "ccm")

            @blk.gpsimd
            def _(g):
                g.wait_ge(cx.sems["ccm"], cx.cnt["ccm"])
                cx.inc(g.collective_compute(
                    "AllReduce", ALU.max, replica_groups=rg,
                    ins=[cc_in_d.ap()], outs=[cc_out_d.ap()]), "ccr")

            @blk.sync
            def _(sync):
                cx.wait(sync, "ccr")
                cx.dma(sync.dma_start(mx4[0:1, :], cc_out_d.ap()), "ccl")

    def emit_scale_prep(clip_dram, tag):
        with nc.Block() as blk:
            @blk.sync
            def _(sync):
                cx.dma(sync.dma_start(clip_sb[0:1, :], clip_dram.ap()),
                       "ldc")

            @blk.scalar
            def _(s):
                cx.wait(s, "ldc")
                cx.inc(s.activation(sxg[0:1, :], clip_sb[0:1, :], AF.Sigmoid),
                       "sg")

            @blk.vector
            def _(v):
                cx.wait(v, "sg")
                cx.wait(v, "ccl")
                v.tensor_mul(sxg[0:1, :], sxg[0:1, :], mx4[0:1, 0:1])
                cx.vsync(v)
                v.tensor_scalar_mul(sxg[0:1, :], sxg[0:1, :], 1.0 / QMAX)
                cx.vsync(v)
                cx.inc(v.reciprocal(rsxg[0:1, :], sxg[0:1, :]), "rs")
                cx.vsync(v)

            @blk.tensor
            def _(t):
                cx.wait(t, "rs")
                t.matmul(ptr[0][:, 0:1], onesr[0:1, :], rsxg[0:1, 0:1],
                         start=True, stop=True)
                cx.inc(t.matmul(ptr[1][:, 0:1], onesr[0:1, :], sxg[0:1, 0:1],
                                start=True, stop=True), "rb")

            @blk.vector
            def _(v):
                cx.wait(v, "rb")
                v.tensor_copy(rsx_bc[:, 0:1], ptr[0][:, 0:1])
                cx.inc(v.tensor_copy(sx_bc[:, 0:1], ptr[1][:, 0:1]), "rb2")

            @blk.tensor
            def _(t):
                cx.wait(t, "rb2")

    emit_allreduce_max(cc_mx_in, cc_mx_out, "1")
    emit_scale_prep(clipx_in, "1")

    AB.reset()
    wqgT = AB.alloc(DT * DI)
    wquT = AB.alloc(DT * DI)
    wq_nat = AB.alloc(1024)
    wq_natB = AB.alloc(1024)
    xq_tm = AB.alloc(1024)
    xqT = AB.alloc(DT * TC)
    AFR.reset()
    wsf = [AFR.alloc(1024) for _ in range(2)]
    lstage = AFR.alloc(TT * 1024)
    tq = AFR.alloc(1024)
    dbq = AFR.alloc(1024)
    dbqB = AFR.alloc(1024)
    c1q = AFR.alloc(1024)
    silu = [AFR.alloc(TC) for _ in range(2)]
    hst = [AFR.alloc(TC) for _ in range(2)]
    hmx = AFR.alloc(NCH * IT)
    rowcat = AFR.alloc(2 * IT)
    rs52 = AFR.alloc(2 * IT)

    strips = [(wg_in, wqgT, it) for it in range(IT)] + \
             [(wu_in, wquT, it) for it in range(IT)]
    NS2 = len(strips)
    for nm in ("rmx", "lwq", "lwq2", "wt0", "wrq", "wtt", "wtv", "wsc2",
               "sev"):
        cx.sem(nm)
    b_rmx = cx.cnt["rmx"]
    b_wt0 = cx.cnt["wt0"]
    b_wrq = cx.cnt["wrq"]
    b_wtt = cx.cnt["wtt"]
    with nc.Block() as blk:
        for t_, (w_dram, wqt, row) in enumerate(strips):
            pb = t_ % 2

            @blk.sync
            def _(sync, w_dram=w_dram, row=row, pb=pb, t_=t_):
                if t_ >= 2:
                    cx.wait(sync, "rmx", b_rmx + t_ - 1)
                cx.dma(sync.dma_start(
                    wsf[pb], w_dram.ap()[row * 128:(row + 1) * 128, :]),
                    "lwq")

            @blk.vector
            def _(v, pb=pb, t_=t_):
                cx.wait(v, "lwq")
                cx.inc(v.reduce_max(rowcat[:, t_:t_ + 1], wsf[pb], axis=AX.X,
                                    apply_absolute_value=True), "rmx")

        @blk.vector
        def _(v):
            cx.wait(v, "rmx")
            cx.vsync(v)
            v.reciprocal(rs52[:, 0:NS2], rowcat[:, 0:NS2])
            cx.vsync(v)
            cx.inc(v.tensor_scalar_mul(rs52[:, 0:NS2], rs52[:, 0:NS2], QMAX),
                   "wsc2")
            v.tensor_scalar_mul(sg[:, :], rowcat[:, 0:IT], 1.0 / QMAX)
            cx.inc(v.tensor_scalar_mul(su[:, :], rowcat[:, IT:2 * IT],
                                       1.0 / QMAX), "wsc2")
            cx.wait(v, "rb2")
            cx.vsync(v)
            v.scalar_tensor_tensor(sg[:, :], sg[:, :], sx_bc[:, 0:1],
                                   sg[:, :], ALU.mult, ALU.bypass)
            cx.inc(v.scalar_tensor_tensor(su[:, :], su[:, :], sx_bc[:, 0:1],
                                          su[:, :], ALU.mult, ALU.bypass),
                   "sev")
            cx.vsync(v)

        for t_, (w_dram, wqt, row) in enumerate(strips):
            pb = t_ % 2
            dq = dbq if pb == 0 else dbqB
            iq = i32b[:, :] if pb == 0 else i32c[:, :]
            wqn = wq_nat if pb == 0 else wq_natB

            @blk.sync
            def _(sync, w_dram=w_dram, row=row, pb=pb, t_=t_):
                if t_ >= 2:
                    cx.wait(sync, "wt0", b_wt0 + t_ - 1)
                cx.dma(sync.dma_start(
                    wsf[pb], w_dram.ap()[row * 128:(row + 1) * 128, :]),
                    "lwq2")

            @blk.scalar
            def _(s, pb=pb, t_=t_, dq=dq, iq=iq):
                cx.wait(s, "lwq2")
                if t_ == 0:
                    cx.wait(s, "wsc2", 1)
                if t_ >= 2:
                    cx.wait(s, "wrq", b_wrq + t_ - 1)
                s.activation(dq, wsf[pb], AF.Identity,
                             scale=rs52[:, t_:t_ + 1])
                cx.inc(s.activation(iq, wsf[pb], AF.Identity,
                                    scale=rs52[:, t_:t_ + 1]), "wt0")

            if t_ >= 1:
                @blk.vector
                def _(v, t_=t_):
                    pv = (t_ - 1) % 2
                    wqtp, rowp = strips[t_ - 1][1], strips[t_ - 1][2]
                    for jj in range(DT):
                        cx.wait(v, "wtt", b_wtt + (t_ - 1) * DT + jj + 1)
                        cx.inc(v.tensor_copy(
                            wqtp[:, jj * DI + rowp * 128:
                                 jj * DI + (rowp + 1) * 128],
                            ptb[jj % 2][:, 0:128]), "wtv")

            @blk.vector
            def _(v, pb=pb, t_=t_, dq=dq, iq=iq, wqn=wqn):
                cx.wait(v, "wt0", b_wt0 + t_ + 1)
                if t_ >= 2:
                    cx.wait(v, "wtt", b_wtt + (t_ - 1) * DT)
                v.scalar_tensor_tensor(c1q, dq, 0.0, iq,
                                       ALU.add, ALU.subtract)
                v.scalar_tensor_tensor(dq, c1q, -0.5, iq,
                                       ALU.is_lt, ALU.subtract)
                cx.inc(v.scalar_tensor_tensor(wqn, c1q, 0.5, dq,
                                              ALU.is_gt, ALU.subtract),
                       "wrq")

            @blk.tensor
            def _(t, t_=t_, wqn=wqn):
                cx.wait(t, "wrq", b_wrq + t_ + 1)
                base = cx.cnt["wtv"]
                for jj in range(DT):
                    if base + jj >= 2:
                        cx.wait(t, "wtv", base + jj - 1)
                    cx.inc(t.transpose(ptb[jj % 2][:, 0:128],
                                       wqn[:, jj * 128:(jj + 1) * 128],
                                       identb[:]), "wtt")

        @blk.vector
        def _(v):
            t_ = NS2
            wqtp, rowp = strips[t_ - 1][1], strips[t_ - 1][2]
            for jj in range(DT):
                cx.wait(v, "wtt", b_wtt + (t_ - 1) * DT + jj + 1)
                cx.inc(v.tensor_copy(
                    wqtp[:, jj * DI + rowp * 128:
                         jj * DI + (rowp + 1) * 128],
                    ptb[jj % 2][:, 0:128]), "wtv")

        @blk.tensor
        def _(t):
            cx.wait(t, "wtv")

    for c in range(NCH):
        with nc.Block() as blk:
            @blk.sync
            def _(sync):
                if c > 0:
                    cx.wait(sync, "ch3", c)
                cx.dma(sync.dma_start(
                    lstage.rearrange("p (a d) -> p a d", d=D),
                    ln2_hb.ap()[c * TC:(c + 1) * TC, :].rearrange(
                        "(a p) d -> p a d", p=128)), "ldl")

            @blk.scalar
            def _(s):
                cx.wait(s, "ldl")
                if c == 0:
                    cx.wait(s, "sev")
                for a in range(TT):
                    g = c * TT + a
                    if g >= 1:
                        cx.wait(s, "xqr", g)
                    s.activation(tq, lstage[:, a * D:(a + 1) * D],
                                 AF.Identity, scale=rsx_bc[:, 0:1])
                    cx.inc(s.activation(i32d[:, :],
                                        lstage[:, a * D:(a + 1) * D],
                                        AF.Identity, scale=rsx_bc[:, 0:1]),
                           "xqs")

            @blk.vector
            def _(v):
                cx.sem("xqtr")
                for a in range(TT):
                    g = c * TT + a
                    cx.wait(v, "xqs", g + 1)
                    if g >= 1:
                        cx.wait(v, "xqtr", g * DT)
                    v.scalar_tensor_tensor(dbq, tq, 0.0, i32d[:, :],
                                           ALU.add, ALU.subtract)
                    v.scalar_tensor_tensor(c1q, dbq, -0.5, i32d[:, :],
                                           ALU.is_lt, ALU.subtract)
                    v.scalar_tensor_tensor(tq, dbq, 0.5, c1q,
                                           ALU.is_gt, ALU.subtract)
                    cx.inc(v.tensor_scalar(xq_tm, tq, QMAX, -QMAX - 1.0,
                                           ALU.min, ALU.max), "xqr")
                    for k in range(DT):
                        gg = g * DT + k
                        cx.wait(v, "xqtr", gg + 1)
                        cx.inc(v.tensor_copy(
                            xqT[:, k * TC + a * 128: k * TC + (a + 1) * 128],
                            ptb[k % 2][:, 0:128]), "xqev")

            @blk.tensor
            def _(t):
                for a in range(TT):
                    g = c * TT + a
                    cx.wait(t, "xqr", g + 1)
                    for k in range(DT):
                        gg = g * DT + k
                        if gg >= 2:
                            cx.wait(t, "xqev", gg - 1)
                        cx.inc(t.transpose(ptb[k % 2][:, 0:128],
                                           xq_tm[:, k * 128:(k + 1) * 128],
                                           identb[:]), "xqtr")

            for it in range(IT):
                gi = c * IT + it
                pg = pmm[2 * (it % 2)]
                pu = pmm[2 * (it % 2) + 1]

                @blk.tensor
                def _(t, it=it, gi=gi, pg=pg, pu=pu):
                    if it == 0:
                        cx.wait(t, "xqev")
                    if gi >= 2:
                        cx.wait(t, "hev", gi - 1)
                    for pi, (pm, wqt) in enumerate(((pg, wqgT), (pu, wquT))):
                        first = True
                        for k in range(DT):
                            lhs = wqt[:, k * DI + it * 128: k * DI + (it + 1) * 128]
                            rhs = xqT[:, k * TC:(k + 1) * TC]
                            mi = t.matmul(pm[:, 0:512], lhs, rhs, start=first,
                                          stop=(k == DT - 1))
                            if pi == 1 and k == DT - 1:
                                cx.inc(mi, "mg")
                            first = False

                @blk.scalar
                def _(s, it=it, gi=gi, pg=pg):
                    cx.wait(s, "mg")
                    if gi >= 2:
                        cx.wait(s, "hev", gi - 1)
                    cx.inc(s.activation(silu[gi % 2], pg[:, 0:512], AF.Silu,
                                        scale=sg[:, it:it + 1]), "sil")

                @blk.vector
                def _(v, it=it, gi=gi, pu=pu):
                    cx.wait(v, "sil")
                    if gi >= 2:
                        cx.wait(v, "sth", 16 * (gi - 1))
                    v.scalar_tensor_tensor(hst[gi % 2], pu[:, 0:512],
                                           su[:, it:it + 1],
                                           silu[gi % 2], ALU.mult, ALU.mult)
                    cx.inc(v.reduce_max(hmx[:, gi:gi + 1], hst[gi % 2],
                                        axis=AX.X,
                                        apply_absolute_value=True), "hev")

                @blk.sync
                def _(sync, it=it, gi=gi, c=c):
                    cx.wait(sync, "hev")
                    cx.dma(sync.dma_start(
                        h_hb.ap()[it * 128:(it + 1) * 128, c * TC:(c + 1) * TC],
                        hst[gi % 2]), "sth")

            @blk.sync
            def _(sync):
                cx.wait(sync, "sth")
                cx.inc(sync.nop(), "ch3")

    with nc.Block() as blk:
        @blk.vector
        def _(v):
            cx.inc(v.reduce_max(amax[:], hmx[:, 0:NCH * IT], axis=AX.X), "hfm")

        @blk.tensor
        def _(t):
            cx.wait(t, "hfm")

    emit_allreduce_max(cc_mx2_in, cc_mx2_out, "2")
    emit_scale_prep(clipd_in, "2")

    AB.reset()
    wdnT = AB.alloc(IT * 1024)
    wq_nat4 = AB.alloc(DI)
    hq = [AB.alloc(IT * TC4) for _ in range(2)]
    AFR.reset()
    wstrip4 = AFR.alloc(DI)
    dbq4 = AFR.alloc(1024)
    c1q4 = AFR.alloc(1024)
    hs2 = [AFR.alloc(TC4) for _ in range(2)]
    x2stage = [AFR.alloc(TT4 * 1024) for _ in range(2)]
    outst = [AFR.alloc(TT4 * 1024) for _ in range(2)]
    tq4 = [AFR.alloc(TC4) for _ in range(2)]
    d4 = AFR.alloc(TC4)
    c4 = AFR.alloc(TC4)

    for nm in ("wseg", "wt0", "wtt", "wtv"):
        cx.sem(nm)
    with nc.Block() as blk:
        for jt in range(DT):
            b_wt0 = cx.cnt["wt0"]
            b_wseg = cx.cnt["wseg"]
            b_wtt = cx.cnt["wtt"]

            @blk.sync
            def _(sync, jt=jt, b_wt0=b_wt0):
                if jt >= 1:
                    cx.wait(sync, "wt0", b_wt0)
                cx.dma(sync.dma_start(wstrip4,
                                      wdn_in.ap()[jt * 128:(jt + 1) * 128, :]),
                       "lwd")

            @blk.vector
            def _(v, jt=jt, b_wt0=b_wt0):
                cx.wait(v, "lwd")
                v.reduce_max(rowm[:], wstrip4, axis=AX.X,
                             apply_absolute_value=True)
                cx.vsync(v)
                if jt >= 1:
                    cx.wait(v, "wt0", b_wt0)
                v.reciprocal(rs_w[:], rowm[:])
                cx.vsync(v)
                cx.inc(v.tensor_scalar_mul(rs_w[:], rs_w[:], QMAX), "wrs")
                cx.inc(v.tensor_scalar_mul(sw_col[:, jt:jt + 1], rowm[:],
                                           1.0 / QMAX), "wsc")

            @blk.scalar
            def _(s, b_wseg=b_wseg):
                cx.wait(s, "wrs")
                for seg in range(3):
                    lo = seg * 1024
                    hi = min(DI, lo + 1024)
                    if b_wseg + seg >= 1:
                        cx.wait(s, "wseg", b_wseg + seg)
                    s.activation(dbq4[:, 0:hi - lo], wstrip4[:, lo:hi],
                                 AF.Identity, scale=rs_w[:, 0:1])
                    cx.inc(s.activation(i32b[:, 0:hi - lo], wstrip4[:, lo:hi],
                                        AF.Identity, scale=rs_w[:, 0:1]), "wt0")

            @blk.vector
            def _(v, jt=jt, b_wtt=b_wtt):
                if jt >= 1:
                    cx.wait(v, "wtt", b_wtt)
                for seg in range(3):
                    lo = seg * 1024
                    hi = min(DI, lo + 1024)
                    w_ = hi - lo
                    cx.wait(v, "wt0", cx.cnt["wt0"] - (3 - 1 - seg))
                    v.scalar_tensor_tensor(c1q4[:, 0:w_], dbq4[:, 0:w_], 0.0,
                                           i32b[:, 0:w_], ALU.add,
                                           ALU.subtract)
                    v.scalar_tensor_tensor(dbq4[:, 0:w_], c1q4[:, 0:w_], -0.5,
                                           i32b[:, 0:w_], ALU.is_lt,
                                           ALU.subtract)
                    cx.inc(v.scalar_tensor_tensor(wq_nat4[:, lo:hi], c1q4[:, 0:w_],
                                                  0.5, dbq4[:, 0:w_], ALU.is_gt,
                                                  ALU.subtract), "wseg")
                cx.inc(v.memset(stat[:, 0:1], 0.0), "wrq")

            @blk.tensor
            def _(t):
                cx.wait(t, "wrq")
                base = cx.cnt["wtv"]
                for jj in range(IT):
                    if base + jj >= 2:
                        cx.wait(t, "wtv", base + jj - 1)
                    cx.inc(t.transpose(ptb[jj % 2][:, 0:128],
                                       wq_nat4[:, jj * 128:(jj + 1) * 128],
                                       identb[:]), "wtt")

            @blk.vector
            def _(v, jt=jt):
                for jj in range(IT):
                    cx.wait(v, "wtt", cx.cnt["wtt"] - (IT - 1 - jj))
                    cx.inc(v.tensor_copy(
                        wdnT[:, jj * 1024 + jt * 128: jj * 1024 + (jt + 1) * 128],
                        ptb[jj % 2][:, 0:128]), "wtv")

        @blk.tensor
        def _(t):
            cx.wait(t, "wtv")

    with nc.Block() as blk:
        @blk.tensor
        def _(t):
            cx.wait(t, "wsc")
            cx.sem("srev")
            base = cx.cnt["srev"]
            for jt in range(DT):
                if jt >= 2:
                    cx.wait(t, "srev", base + jt - 1)
                cx.inc(t.transpose(ptr[jt % 2][0:1, 0:128], sw_col[:, jt:jt + 1],
                                   ident[:]), "str")

        @blk.vector
        def _(v):
            for jt in range(DT):
                cx.wait(v, "str", cx.cnt["str"] - (DT - 1 - jt))
                cx.inc(v.tensor_copy(swd_row[0:1, jt * 128:(jt + 1) * 128],
                                     ptr[jt % 2][0:1, 0:128]), "srev")
            cx.wait(v, "rs")
            cx.vsync(v)
            cx.inc(v.tensor_scalar_mul(swd_row[0:1, :], swd_row[0:1, :],
                                       sxg[0:1, 0:1]), "srow")
            cx.vsync(v)

        @blk.tensor
        def _(t):
            cx.wait(t, "srow")
            for eb in range(2):
                cx.inc(t.matmul(pmm[eb][:, 0:512], onesr[0:1, :],
                                swd_row[0:1, eb * 512:(eb + 1) * 512],
                                start=True, stop=True), "sbc")

        @blk.vector
        def _(v):
            cx.wait(v, "sbc")
            for eb in range(2):
                cx.inc(v.tensor_copy(swd_bc[:, eb * 512:(eb + 1) * 512],
                                     pmm[eb][:, 0:512]), "sbv")

        @blk.tensor
        def _(t):
            cx.wait(t, "sbv")

    for nm in ("ldh", "ldx4", "hqs", "hqr", "md", "oev", "sto4"):
        cx.sem(nm)
    b_oev = cx.cnt["oev"]
    b_md = cx.cnt["md"]
    b_sto4 = cx.cnt["sto4"]
    b_hqs = cx.cnt["hqs"]
    b_hqr = cx.cnt["hqr"]
    with nc.Block() as blk:
        for c in range(NCH4):
            cb = c % 2

            @blk.sync
            def _(sync, c=c, cb=cb):
                if c >= 2:
                    cx.wait(sync, "oev", b_oev + 4 * (c - 1))
                    cx.wait(sync, "sto4", b_sto4 + 16 * (c - 1))
                cx.dma(sync.dma_start(
                    x2stage[cb].rearrange("p (a d) -> p a d", d=D),
                    x2_hb.ap()[c * TC4:(c + 1) * TC4, :].rearrange(
                        "(a p) d -> p a d", p=128)), "ldx4")

            for k in range(IT):
                gk = c * IT + k
                pb = gk % 2

                @blk.sync
                def _(sync, c=c, k=k, gk=gk, pb=pb):
                    if gk >= 2:
                        cx.wait(sync, "hqs", b_hqs + gk - 1)
                    cx.dma(sync.dma_start(
                        hs2[pb],
                        h_hb.ap()[k * 128:(k + 1) * 128,
                                  c * TC4:(c + 1) * TC4]), "ldh")

                @blk.scalar
                def _(s, c=c, k=k, gk=gk, pb=pb):
                    cx.wait(s, "ldh")
                    if c == 0 and k == 0:
                        cx.wait(s, "rb2")
                    if gk >= 2:
                        cx.wait(s, "hqr", b_hqr + gk - 1)
                    s.activation(tq4[pb], hs2[pb],
                                 AF.Identity, scale=rsx_bc[:, 0:1])
                    cx.inc(s.activation(
                        i32b[:, pb * TC4:(pb + 1) * TC4], hs2[pb],
                        AF.Identity, scale=rsx_bc[:, 0:1]), "hqs")

                @blk.vector
                def _(v, c=c, k=k, gk=gk, pb=pb, cb=cb):
                    i4 = i32b[:, pb * TC4:(pb + 1) * TC4]
                    cx.wait(v, "hqs", b_hqs + gk + 1)
                    if k == 0 and c >= 2:
                        cx.wait(v, "md", b_md + 4 * (c - 1))
                    v.scalar_tensor_tensor(d4, tq4[pb], 0.0, i4,
                                           ALU.add, ALU.subtract)
                    v.scalar_tensor_tensor(c4, d4, -0.5, i4,
                                           ALU.is_lt, ALU.subtract)
                    v.scalar_tensor_tensor(tq4[pb], d4, 0.5, c4, ALU.is_gt,
                                           ALU.subtract)
                    cx.inc(v.tensor_scalar(hq[cb][:, k * TC4:(k + 1) * TC4],
                                           tq4[pb],
                                           QMAX, -QMAX - 1.0, ALU.min,
                                           ALU.max), "hqr")

            for a in range(TT4):
                gt = c * TT4 + a

                @blk.tensor
                def _(t, c=c, a=a, gt=gt, cb=cb):
                    if a == 0:
                        cx.wait(t, "hqr")
                    for eb in range(2):
                        gg = 2 * gt + eb
                        if gg >= 2:
                            cx.wait(t, "oev", b_oev + gg - 1)
                        first = True
                        for k in range(IT):
                            lhs = hq[cb][:, k * TC4 + a * 128:
                                         k * TC4 + (a + 1) * 128]
                            rhs = wdnT[:, k * 1024 + eb * 512:
                                       k * 1024 + (eb + 1) * 512]
                            mi = t.matmul(pmm[eb][:, 0:512], lhs, rhs,
                                          start=first, stop=(k == IT - 1))
                            if k == IT - 1:
                                cx.inc(mi, "md")
                            first = False

                @blk.vector
                def _(v, c=c, a=a, gt=gt, cb=cb):
                    cx.wait(v, "ldx4", 16 * (c + 1))
                    x2row = x2stage[cb][:, a * D:(a + 1) * D]
                    orow = outst[cb][:, a * D:(a + 1) * D]
                    for eb in range(2):
                        cx.wait(v, "md", cx.cnt["md"] - (2 - 1 - eb))
                        sl = slice(eb * 512, (eb + 1) * 512)
                        v.scalar_tensor_tensor(orow[:, sl], pmm[eb][:, 0:512],
                                               0.0, swd_bc[:, sl],
                                               ALU.add, ALU.mult)
                        cx.inc(v.scalar_tensor_tensor(orow[:, sl],
                                                      x2row[:, sl], 0.0,
                                                      orow[:, sl],
                                                      ALU.add, ALU.add),
                               "oev")

            @blk.sync
            def _(sync, c=c, cb=cb):
                cx.wait(sync, "oev")
                cx.dma(sync.dma_start(
                    out_out.ap()[c * TC4:(c + 1) * TC4, :].rearrange(
                        "(a p) d -> p a d", p=128),
                    outst[cb].rearrange("p (a d) -> p a d", d=D)), "sto4")

    with nc.Block() as blk:
        @blk.sync
        def _(sync):
            cx.wait(sync, "sto4")
    stack.close()
    return nc


from concourse.bass_utils import run_bass_kernel_spmd

_CACHE = {}


def kernel(L0_attn_x, L0_attn_Wd, L0_attn_bd, L0_attn_a_log, L0_attn_b_param,
           L0_attn_Wo, L0_attn_bo, L0_ffn_clip_x, L0_ffn_clip_down,
           L0_ffn_w_gate, L0_ffn_w_up, L0_ffn_w_down):
    x = np.ascontiguousarray(np.asarray(L0_attn_x, dtype=np.float32))
    wd = np.ascontiguousarray(np.asarray(L0_attn_Wd, dtype=np.float32))
    bd = np.ascontiguousarray(np.asarray(L0_attn_bd, dtype=np.float32))
    alog = np.ascontiguousarray(np.asarray(L0_attn_a_log, dtype=np.float32))
    bpar = np.ascontiguousarray(np.asarray(L0_attn_b_param, dtype=np.float32))
    wo = np.ascontiguousarray(np.asarray(L0_attn_Wo, dtype=np.float32))
    bo = np.ascontiguousarray(np.asarray(L0_attn_bo, dtype=np.float32))
    clip_x = np.asarray(L0_ffn_clip_x, dtype=np.float32).reshape(1, 1)
    clip_dn = np.asarray(L0_ffn_clip_down, dtype=np.float32).reshape(1, 1)
    wg = np.ascontiguousarray(np.asarray(L0_ffn_w_gate, dtype=np.float32))
    wu = np.ascontiguousarray(np.asarray(L0_ffn_w_up, dtype=np.float32))
    wdn = np.ascontiguousarray(np.asarray(L0_ffn_w_down, dtype=np.float32))

    if "nc" not in _CACHE:
        _CACHE["nc"] = build_fused()
    nc = _CACHE["nc"]
    in_maps = []
    for i in range(NCORE):
        b, q = divmod(i, 4)
        gm = np.zeros((128, NCORE), dtype=np.float32)
        gm[:, b * 4:b * 4 + q] = 1.0
        in_maps.append({
            "x": np.ascontiguousarray(x[b, q * TPC:(q + 1) * TPC]),
            "wd": wd, "bd": bd, "alog": alog, "bpar": bpar, "wo": wo, "bo": bo,
            "clip_x": clip_x, "clip_down": clip_dn, "wg": wg, "wu": wu,
            "wdn": wdn, "gmask": gm})
    res = run_bass_kernel_spmd(nc, in_maps, core_ids=list(range(NCORE)))
    out = np.empty((B, T, D), dtype=np.float32)
    for i in range(NCORE):
        b, q = divmod(i, 4)
        out[b, q * TPC:(q + 1) * TPC] = res.results[i]["out"]
    return out


# version 13
# speedup vs baseline: 1.0143x; 1.0143x over previous
import contextlib

import numpy as np
import concourse.bass as bass
import concourse.mybir as mybir

F32 = mybir.dt.float32
BF16 = mybir.dt.bfloat16
I32 = mybir.dt.int32
AF = mybir.ActivationFunctionType
ALU = mybir.AluOpType
AX = mybir.AxisListType

NCORE = 8
B, T, D, S, DI = 2, 8192, 1024, 16, 2816
TPC = 2048
QMAX = 127.0
LN_EPS = 1e-5
DT = D // 128
IT = DI // 128
TC = 512
NCH = TPC // TC
TT = TC // 128
TC4 = 256
NCH4 = TPC // TC4
TT4 = TC4 // 128
SC = 1024
NSC = TPC // SC
SC2 = 512


class Ctx:
    def __init__(self, nc, stack):
        self.nc = nc
        self.stack = stack
        self.sems = {}
        self.cnt = {}

    def sem(self, name):
        if name not in self.sems:
            self.sems[name] = self.stack.enter_context(self.nc.semaphore(name))
            self.cnt[name] = 0
        return self.sems[name]

    def inc(self, inst, name, k=1):
        self.sem(name)
        self.cnt[name] += k
        inst.then_inc(self.sems[name], k)
        return inst

    def dma(self, inst, name):
        return self.inc(inst, name, 16)

    def wait(self, eng, name, val=None):
        self.sem(name)
        eng.wait_ge(self.sems[name], self.cnt[name] if val is None else val)

    def vsync(self, eng):
        self.sem("vself")
        self.cnt["vself"] += 1
        eng.engine_nop().then_inc(self.sems["vself"], 1)
        eng.wait_ge(self.sems["vself"], self.cnt["vself"])

    def sb(self, name, shape, dtype=F32):
        return self.stack.enter_context(self.nc.sbuf_tensor(name, shape, dtype))

    def ps(self, name, shape, dtype=F32):
        return self.stack.enter_context(self.nc.psum_tensor(name, shape, dtype))


class Arena:
    def __init__(self, cx, name, cols, dtype):
        self.t = cx.sb(name, [128, cols], dtype)
        self.cols = cols
        self.off = 0

    def reset(self):
        self.off = 0

    def alloc(self, cols):
        assert self.off + cols <= self.cols, (self.off, cols, self.cols)
        sl = self.t[:, self.off:self.off + cols]
        self.off += cols
        return sl


def build_fused():
    nc = bass.Bass("TRN2", target_bir_lowering=False, debug=False,
                   detect_race_conditions=False)
    x_in = nc.dram_tensor("x", [TPC, D], F32, kind="ExternalInput")
    wd_in = nc.dram_tensor("wd", [D, D], F32, kind="ExternalInput")
    bd_in = nc.dram_tensor("bd", [D], F32, kind="ExternalInput")
    alog_in = nc.dram_tensor("alog", [D, S], F32, kind="ExternalInput")
    bpar_in = nc.dram_tensor("bpar", [D, S], F32, kind="ExternalInput")
    wo_in = nc.dram_tensor("wo", [D, D], F32, kind="ExternalInput")
    bo_in = nc.dram_tensor("bo", [D], F32, kind="ExternalInput")
    clipx_in = nc.dram_tensor("clip_x", [1, 1], F32, kind="ExternalInput")
    clipd_in = nc.dram_tensor("clip_down", [1, 1], F32, kind="ExternalInput")
    wg_in = nc.dram_tensor("wg", [DI, D], F32, kind="ExternalInput")
    wu_in = nc.dram_tensor("wu", [DI, D], F32, kind="ExternalInput")
    wdn_in = nc.dram_tensor("wdn", [D, DI], F32, kind="ExternalInput")
    gmask_in = nc.dram_tensor("gmask", [128, NCORE], F32, kind="ExternalInput")
    out_out = nc.dram_tensor("out", [TPC, D], F32, kind="ExternalOutput")

    delta_hb = nc.dram_tensor("delta_hb", [D, TPC], F32, kind="Internal")
    v_hb = nc.dram_tensor("v_hb", [D, TPC], F32, kind="Internal")
    x2_hb = nc.dram_tensor("x2_hb", [TPC, D], F32, kind="Internal")
    ln2_hb = nc.dram_tensor("ln2_hb", [TPC, D], F32, kind="Internal")
    h_hb = nc.dram_tensor("h_hb", [DI, TPC], F32, kind="Internal")
    cum_hb = nc.dram_tensor("cum_hb", [D, TPC], F32, kind="Internal")
    y0_hb = nc.dram_tensor("y0_hb", [D, TPC], F32, kind="Internal")
    cc_sum_in = nc.dram_tensor("cc_sum_in", [128, 2 * DT * S], F32, kind="Internal")
    cc_sum_out = nc.dram_tensor("cc_sum_out", [NCORE * 128, 2 * DT * S], F32,
                                kind="Internal")
    cc_mx_in = nc.dram_tensor("cc_mx_in", [1, 4], F32, kind="Internal")
    cc_mx_out = nc.dram_tensor("cc_mx_out", [1, 4], F32, kind="Internal")
    cc_mx2_in = nc.dram_tensor("cc_mx2_in", [1, 4], F32, kind="Internal")
    cc_mx2_out = nc.dram_tensor("cc_mx2_out", [1, 4], F32, kind="Internal")

    stack = contextlib.ExitStack()
    cx = Ctx(nc, stack)
    rg = [list(range(NCORE))]

    ident = cx.sb("ident", [128, 128], F32)
    identb = cx.sb("identb", [128, 128], BF16)
    apos = cx.sb("apos", [128, DT * S], F32)
    nega = cx.sb("nega", [128, DT * S], F32)
    bpar = cx.sb("bpar_sb", [128, DT * S], F32)
    t0s = cx.sb("t0s", [128, DT * S], F32)
    t1s = cx.sb("t1s", [128, DT * S], F32)
    bd_sb = cx.sb("bd_sb", [128, DT], F32)
    bd_row = cx.sb("bd_row", [1, D], F32)
    bo_row = cx.sb("bo_row", [1, D], F32)
    onesr = cx.sb("onesr", [1, 128], F32)
    dsum = cx.sb("dsum", [128, DT], F32)
    carry = cx.sb("carry", [128, DT * S], F32)
    aend = cx.sb("aend", [128, DT * S], F32)
    sin = cx.sb("sin", [128, DT * S], F32)
    cc = cx.sb("cc", [128, DT * S], F32)
    gmask = cx.sb("gmask_sb", [128, NCORE], F32)
    summ = cx.sb("summ", [128, 2 * DT * S], F32)
    stat = cx.sb("stat", [128, 4], F32)
    amax = cx.sb("amax", [128, 1], F32)
    amaxt = cx.sb("amaxt", [128, 1], F32)
    fin = cx.sb("fin", [1, 128], F32)
    mx4 = cx.sb("mx4", [1, 4], F32)
    clip_sb = cx.sb("clip_sb", [1, 1], F32)
    sxg = cx.sb("sxg", [1, 1], F32)
    rsxg = cx.sb("rsxg", [1, 1], F32)
    rsx_bc = cx.sb("rsx_bc", [128, 1], F32)
    sx_bc = cx.sb("sx_bc", [128, 1], F32)
    rowm = cx.sb("rowm", [128, 1], F32)
    rs_w = cx.sb("rs_w", [128, 1], F32)
    sg = cx.sb("sg", [128, IT], F32)
    su = cx.sb("su", [128, IT], F32)
    sw_col = cx.sb("sw_col", [128, DT], F32)
    swd_row = cx.sb("swd_row", [1, D], F32)
    swd_bc = cx.sb("swd_bc", [128, D], F32)

    AB = Arena(cx, "arena_b", 53504, BF16)
    AFR = Arena(cx, "arena_f", 16384, F32)
    i32b = cx.sb("i32b", [128, 1024], I32)
    i32c = cx.sb("i32c", [128, 1024], I32)
    i32d = cx.sb("i32d", [128, 1024], I32)

    ptr = [cx.ps(f"ptr{i}", [128, 128], F32) for i in range(2)]
    ptb = [cx.ps(f"ptb{i}", [128, 128], BF16) for i in range(2)]
    pmm = [cx.ps(f"pmm{i}", [128, 512], F32) for i in range(4)]

    with nc.Block() as blk:
        @blk.gpsimd
        def _(g):
            cx.inc(g.iota(ident[:], pattern=[[1, 128]], base=0,
                          channel_multiplier=-1,
                          allow_small_or_imprecise_dtypes=True), "idg0")
            cx.wait(g, "idg0")
            cx.inc(g.tensor_single_scalar(ident[:], ident[:], 0.0, ALU.is_equal),
                   "idg")

        @blk.vector
        def _(v):
            cx.wait(v, "idg")
            cx.inc(v.tensor_copy(identb[:], ident[:]), "idb")

        @blk.sync
        def _(sync):
            cx.dma(sync.dma_start(
                t0s.ap().rearrange("p (j s) -> p j s", s=S),
                alog_in.ap().rearrange("(j p) s -> p j s", p=128)), "ld0")
            cx.dma(sync.dma_start(
                bpar.ap().rearrange("p (j s) -> p j s", s=S),
                bpar_in.ap().rearrange("(j p) s -> p j s", p=128)), "ld0")
            cx.dma(sync.dma_start(
                bd_row[0:1, :], bd_in.ap().rearrange("(a d) -> a d", a=1)), "ld0")
            cx.dma(sync.dma_start(
                bo_row[0:1, :], bo_in.ap().rearrange("(a d) -> a d", a=1)), "ld0")
            cx.dma(sync.dma_start(gmask[:, :], gmask_in.ap()), "ld0")

        @blk.scalar
        def _(s):
            cx.wait(s, "ld0")
            s.activation(t1s[:], t0s[:], AF.Abs)
            s.activation(t1s[:], t1s[:], AF.Exp, scale=-1.0)
            cx.inc(s.activation(t1s[:], t1s[:], AF.Ln, bias=1.0), "sps")

        @blk.vector
        def _(v):
            cx.wait(v, "sps")
            v.tensor_scalar_max(apos[:], t0s[:], 0.0)
            cx.vsync(v)
            v.tensor_add(apos[:], apos[:], t1s[:])
            cx.vsync(v)
            v.tensor_scalar_add(apos[:], apos[:], 1e-4)
            cx.vsync(v)
            v.tensor_scalar_mul(nega[:], apos[:], -1.0)
            cx.vsync(v)
            v.memset(onesr[0:1, :], 1.0)
            v.memset(carry[:], 0.0)
            v.memset(dsum[:], 0.0)
            cx.inc(v.memset(amax[:], 0.0), "ini")

        @blk.tensor
        def _(t):
            cx.wait(t, "ini")
            cx.wait(t, "idg")
            cx.sem("bdev")
            for j in range(DT):
                if j >= 2:
                    cx.wait(t, "bdev", j - 1)
                cx.inc(t.transpose(ptr[j % 2][:, 0:1],
                                   bd_row[0:1, j * 128:(j + 1) * 128],
                                   ident[0:1, 0:1]), "bdtr")

        @blk.vector
        def _(v):
            for j in range(DT):
                cx.wait(v, "bdtr", j + 1)
                cx.inc(v.tensor_copy(bd_sb[:, j:j + 1], ptr[j % 2][:, 0:1]), "bdev")

        @blk.tensor
        def _(t):
            cx.wait(t, "bdev")

    wT_h = AB.alloc(DT * 1024)
    wT_l = AB.alloc(DT * 1024)
    wnat_h = AB.alloc(1024)
    wnat_l = AB.alloc(1024)
    AB_base_p1 = AB.off

    def emit_wsplit(blk, w_dram, tag, wstage):
        cx.sem("wev")
        cx.sem("vsp")
        cx.sem("wtr")
        b_vsp = cx.cnt["vsp"]
        b_wtr = cx.cnt["wtr"]
        b_wev = cx.cnt["wev"]
        if True:
            @blk.sync
            def _(sync):
                for j in range(DT):
                    if j >= 1:
                        cx.wait(sync, "vsp", b_vsp + j)
                    cx.dma(sync.dma_start(wstage,
                                          w_dram.ap()[j * 128:(j + 1) * 128, :]),
                           "ldw")

            @blk.vector
            def _(v):
                for j in range(DT):
                    cx.wait(v, "ldw", (cx.cnt["ldw"] - 16 * (DT - 1 - j)))
                    if j >= 1:
                        cx.wait(v, "wtr", b_wtr + 2 * DT * j)
                    v.tensor_copy(wnat_h, wstage)
                    cx.inc(v.tensor_tensor(wnat_l, wstage, wnat_h, ALU.subtract),
                           "vsp")
                    for k in range(DT):
                        cx.wait(v, "wtr", b_wtr + 2 * DT * j + 2 * (k + 1))
                        v.tensor_copy(
                            wT_h[:, k * 1024 + j * 128: k * 1024 + (j + 1) * 128],
                            ptb[0][:, 0:128])
                        cx.inc(v.tensor_copy(
                            wT_l[:, k * 1024 + j * 128: k * 1024 + (j + 1) * 128],
                            ptb[1][:, 0:128]), "wev")

            @blk.tensor
            def _(t):
                for j in range(DT):
                    cx.wait(t, "vsp", b_vsp + j + 1)
                    for k in range(DT):
                        if DT * j + k >= 1:
                            cx.wait(t, "wev", b_wev + DT * j + k)
                        t.transpose(ptb[0][:, 0:128],
                                    wnat_h[:, k * 128:(k + 1) * 128], identb[:])
                        cx.inc(t.transpose(ptb[1][:, 0:128],
                                           wnat_l[:, k * 128:(k + 1) * 128],
                                           identb[:]), "wtr", 2)

    AFR.reset()
    wstage = AFR.alloc(1024)
    xstage = AFR.alloc(TT * 1024)
    xln_tm = AFR.alloc(TT * 1024)
    xlnT = AFR.alloc(DT * TC)
    sqs = AFR.alloc(1024)
    dstage = AFR.alloc(TC)
    vstage = AFR.alloc(TC)
    sp0 = AFR.alloc(TC)
    sp1 = sqs[:, 0:TC]
    lstats1 = AFR.alloc(8 * TT)
    xh_off = AB.off
    xh = AB.alloc(DT * TC)
    xl = AB.alloc(DT * TC)

    with nc.Block() as blk:
        emit_wsplit(blk, wd_in, "d", wstage)

    def emit_ln2(blk, xsrc, lnout, sqs_, stats, lnmxb, mxbase, ready_sem,
                 ready_vals):
        ms = stats[:, 0:TT]
        vs = stats[:, TT:2 * TT]
        t1 = stats[:, 2 * TT:3 * TT]
        t2 = stats[:, 3 * TT:4 * TT]
        rr = stats[:, 4 * TT:5 * TT]
        nb = stats[:, 5 * TT:6 * TT]
        for nm in ("lsq", "lst", "lnq", "lns", "lnr", "lnd", "lnx"):
            cx.sem(nm)
        b_lsq = cx.cnt["lsq"]
        b_lst = cx.cnt["lst"]
        b_lnr = cx.cnt["lnr"]
        b_lnd = cx.cnt["lnd"]

        @blk.scalar
        def _(s):
            for a in range(TT):
                cx.wait(s, ready_sem, ready_vals[a])
                if b_lst + a >= 1:
                    cx.wait(s, "lst", b_lst + a)
                cx.inc(s.activation(sqs_, xsrc[:, a * D:(a + 1) * D],
                                    AF.Square), "lsq")

        @blk.vector
        def _(v):
            for a in range(TT):
                cx.wait(v, "lsq", b_lsq + a + 1)
                v.reduce_sum(ms[:, a:a + 1], xsrc[:, a * D:(a + 1) * D],
                             axis=AX.X)
                cx.inc(v.reduce_sum(vs[:, a:a + 1], sqs_, axis=AX.X), "lst")
            cx.vsync(v)
            v.tensor_scalar_mul(ms, ms, 1.0 / D)
            v.tensor_scalar_mul(vs, vs, 1.0 / D)
            cx.vsync(v)
            v.tensor_tensor(t1, ms, ms, ALU.mult)
            cx.vsync(v)
            v.tensor_tensor(vs, vs, t1, ALU.subtract)
            cx.vsync(v)
            cx.inc(v.tensor_scalar_add(t1, vs, LN_EPS), "lnq")

        @blk.scalar
        def _(s):
            cx.wait(s, "lnq")
            cx.inc(s.activation(t2, t1, AF.Sqrt), "lns")

        @blk.vector
        def _(v):
            cx.wait(v, "lns")
            v.reciprocal(rr, t2)
            cx.vsync(v)
            v.tensor_tensor(t2, rr, rr, ALU.mult)
            cx.vsync(v)
            v.tensor_tensor(t2, t1, t2, ALU.mult)
            cx.vsync(v)
            v.tensor_scalar(t2, t2, -0.5, 1.5, ALU.mult, ALU.add)
            cx.vsync(v)
            v.tensor_tensor(rr, rr, t2, ALU.mult)
            cx.vsync(v)
            v.tensor_tensor(nb, ms, rr, ALU.mult)
            cx.vsync(v)
            cx.inc(v.tensor_scalar_mul(nb, nb, -1.0), "lnr")
            cx.vsync(v)

        @blk.scalar
        def _(s):
            cx.wait(s, "lnr", b_lnr + 1)
            for a in range(TT):
                cx.inc(s.activation(lnout[:, a * D:(a + 1) * D],
                                    xsrc[:, a * D:(a + 1) * D], AF.Identity,
                                    bias=nb[:, a:a + 1],
                                    scale=rr[:, a:a + 1]), "lnd")

        if lnmxb is not None:
            @blk.vector
            def _(v):
                for a in range(TT):
                    cx.wait(v, "lnd", b_lnd + a + 1)
                    cx.inc(v.reduce_max(lnmxb[:, mxbase + a:mxbase + a + 1],
                                        lnout[:, a * D:(a + 1) * D],
                                        axis=AX.X,
                                        apply_absolute_value=True), "lnx")


    for c in range(NCH):
        with nc.Block() as blk:
            @blk.sync
            def _(sync):
                cx.dma(sync.dma_start(
                    xstage.rearrange("p (a d) -> p a d", d=D),
                    x_in.ap()[c * TC:(c + 1) * TC, :].rearrange(
                        "(a p) d -> p a d", p=128)), "ldx")

            emit_ln2(blk, xstage, xln_tm, sqs, lstats1, None, 0, "ldx",
                     [cx.cnt["ldx"]] * TT)

            @blk.tensor
            def _(t):
                cx.sem("xev")
                cx.wait(t, "lnd")
                base = cx.cnt["xev"]
                for k in range(DT):
                    for a in range(TT):
                        g = k * TT + a
                        if base + g >= 2:
                            cx.wait(t, "xev", base + g - 1)
                        cx.inc(t.transpose(
                            ptr[g % 2][:, 0:128],
                            xln_tm[:, a * D + k * 128: a * D + (k + 1) * 128],
                            ident[:]), "xtr")

            @blk.vector
            def _(v):
                base = cx.cnt["xtr"] - DT * TT
                for k in range(DT):
                    for a in range(TT):
                        g = k * TT + a
                        cx.wait(v, "xtr", base + g + 1)
                        cx.inc(v.tensor_copy(
                            xlnT[:, k * TC + a * 128: k * TC + (a + 1) * 128],
                            ptr[g % 2][:, 0:128]), "xev")
                for k in range(DT):
                    v.tensor_copy(xh[:, k * TC:(k + 1) * TC],
                                  xlnT[:, k * TC:(k + 1) * TC])
                    cx.inc(v.tensor_tensor(xl[:, k * TC:(k + 1) * TC],
                                           xlnT[:, k * TC:(k + 1) * TC],
                                           xh[:, k * TC:(k + 1) * TC],
                                           ALU.subtract), "xsp")

            for et in range(DT):
                g = c * DT + et
                pm = pmm[et % 2]

                @blk.tensor
                def _(t, et=et, g=g, pm=pm):
                    if et == 0:
                        cx.wait(t, "xsp")
                    if g >= 2:
                        cx.wait(t, "spev", g - 1)
                    first = True
                    for k in range(DT):
                        lh = wT_h[:, k * 1024 + et * 128: k * 1024 + (et + 1) * 128]
                        ll = wT_l[:, k * 1024 + et * 128: k * 1024 + (et + 1) * 128]
                        rh = xh[:, k * TC:(k + 1) * TC]
                        rl = xl[:, k * TC:(k + 1) * TC]
                        for lhs, rhs in ((lh, rh), (lh, rl), (ll, rh)):
                            last = (k == DT - 1) and (lhs is ll)
                            mi = t.matmul(pm[:, :], lhs, rhs, start=first, stop=last)
                            if last:
                                cx.inc(mi, "mm")
                            first = False

                @blk.scalar
                def _(s, et=et, g=g, pm=pm):
                    cx.wait(s, "mm")
                    if g >= 1:
                        cx.wait(s, "spev", g)
                    bcol = bd_sb[:, et:et + 1]
                    s.activation(sp0, pm[:, :], AF.Abs, bias=bcol)
                    s.activation(sp1, sp0, AF.Exp, scale=-1.0)
                    cx.inc(s.activation(xstage[:, 0:TC], sp1, AF.Ln, bias=1.0),
                           "spl")

                @blk.vector
                def _(v, et=et, g=g, pm=pm):
                    cx.wait(v, "spl")
                    if g >= 1:
                        cx.wait(v, "sto1", 32 * g)
                    bcol = bd_sb[:, et:et + 1]
                    v.scalar_tensor_tensor(dstage, pm[:, :], bcol, sp0,
                                           ALU.add, ALU.add)
                    v.scalar_tensor_tensor(dstage, dstage, 0.5,
                                           xstage[:, 0:TC],
                                           ALU.mult, ALU.add)
                    v.scalar_tensor_tensor(vstage, dstage, 0.0,
                                           xlnT[:, et * TC:(et + 1) * TC],
                                           ALU.add, ALU.mult)
                    v.reduce_sum(amaxt[:], dstage, axis=AX.X)
                    cx.vsync(v)
                    cx.inc(v.tensor_add(dsum[:, et:et + 1], dsum[:, et:et + 1],
                                        amaxt[:]), "spev")

                @blk.sync
                def _(sync, et=et, c=c):
                    cx.wait(sync, "spev")
                    cx.dma(sync.dma_start(
                        delta_hb.ap()[et * 128:(et + 1) * 128, c * TC:(c + 1) * TC],
                        dstage), "sto1")
                    cx.dma(sync.dma_start(
                        v_hb.ap()[et * 128:(et + 1) * 128, c * TC:(c + 1) * TC],
                        vstage), "sto1")

    AFR.reset()
    dj = AFR.alloc(TPC)
    vjb = AFR.alloc(TPC)
    cumj = AFR.alloc(TPC)
    y0j = AFR.alloc(TPC)
    abz = [AFR.alloc(TPC) for _ in range(2)]
    sgz = [AFR.alloc(TPC) for _ in range(2)]

    with nc.Block() as blk:
        @blk.sync
        def _(sync):
            cx.wait(sync, "sto1")

        @blk.vector
        def _(v):
            v.memset(swd_bc[:, :], 1.0)

    for nm in ("zse", "zscs", "zdone", "y0st"):
        cx.sem(nm)
    b_zse = cx.cnt["zse"]
    b_zscs = cx.cnt["zscs"]
    b_zd = cx.cnt["zdone"]
    b_y0 = cx.cnt["y0st"]
    with nc.Block() as blk:
        for j in range(DT):
            @blk.sync
            def _(sync, j=j):
                if j >= 1:
                    cx.wait(sync, "zdone", b_zd + j)
                    cx.dma(sync.dma_start(
                        y0_hb.ap()[(j - 1) * 128:j * 128, :], y0j), "y0st")
                    cx.dma(sync.dma_start(
                        cum_hb.ap()[(j - 1) * 128:j * 128, :], cumj), "y0st")
                    cx.wait(sync, "zse", b_zse + S * j)
                cx.dma(sync.dma_start(
                    dj, delta_hb.ap()[j * 128:(j + 1) * 128, :]), "ldz")
                cx.dma(sync.dma_start(
                    vjb, v_hb.ap()[j * 128:(j + 1) * 128, :]), "ldz")

            @blk.scalar
            def _(s, j=j):
                cx.wait(s, "ldz")
                for si in range(S):
                    gs = j * S + si
                    if gs >= 2:
                        cx.wait(s, "zscs", b_zscs + gs - 1)
                    cx.inc(s.activation(
                        abz[si % 2], dj, AF.Exp,
                        scale=nega[:, j * S + si:j * S + si + 1]), "zse")

            @blk.vector
            def _(v, j=j):
                cx.wait(v, "ldz")
                if j >= 1:
                    cx.wait(v, "y0st", b_y0 + 32 * j)
                half = TPC // 2
                v.tensor_tensor_scan(cumj[:, 0:half], swd_bc[:, 0:half],
                                     dj[:, 0:half], 0.0, ALU.mult, ALU.add)
                for si in range(S):
                    if si >= 2:
                        v.tensor_copy(
                            carry[:, j * S + si - 2:j * S + si - 1],
                            sgz[si % 2][:, TPC - 1:TPC])
                    cx.wait(v, "zse", b_zse + j * S + si + 1)
                    cx.inc(v.tensor_tensor_scan(
                        sgz[si % 2], abz[si % 2], vjb, 0.0,
                        ALU.mult, ALU.add), "zscs")
                    if si == 0:
                        v.tensor_tensor_scan(
                            cumj[:, half:TPC], swd_bc[:, 0:half],
                            dj[:, half:TPC], cumj[:, half - 1:half],
                            ALU.mult, ALU.add)
                        v.scalar_tensor_tensor(
                            y0j, sgz[0], bpar[:, j * S:j * S + 1], sgz[0],
                            ALU.mult, ALU.bypass)
                    else:
                        v.scalar_tensor_tensor(
                            y0j, sgz[si % 2],
                            bpar[:, j * S + si:j * S + si + 1], y0j,
                            ALU.mult, ALU.add)
                cx.vsync(v)
                v.tensor_copy(carry[:, j * S + S - 2:j * S + S - 1],
                              sgz[0][:, TPC - 1:TPC])
                cx.inc(v.tensor_copy(carry[:, j * S + S - 1:j * S + S],
                                     sgz[1][:, TPC - 1:TPC]), "zdone")

        @blk.sync
        def _(sync):
            cx.wait(sync, "zdone", b_zd + DT)
            cx.dma(sync.dma_start(
                y0_hb.ap()[(DT - 1) * 128:DT * 128, :], y0j), "y0st")
            cx.dma(sync.dma_start(
                cum_hb.ap()[(DT - 1) * 128:DT * 128, :], cumj), "y0st")
            cx.wait(sync, "y0st")

    with nc.Block() as blk:
        @blk.vector
        def _(v):
            cx.wait(v, "zdone")
            for j in range(DT):
                cx.inc(v.tensor_scalar_mul(t0s[:, j * S:(j + 1) * S],
                                           nega[:, j * S:(j + 1) * S],
                                           dsum[:, j:j + 1]), "ae0")
            cx.vsync(v)

        @blk.scalar
        def _(s):
            cx.wait(s, "ae0")
            cx.inc(s.activation(aend[:], t0s[:], AF.Exp), "ae1")

        @blk.sync
        def _(sync):
            cx.wait(sync, "ae1")
            cx.dma(sync.dma_start(cc_sum_in.ap()[:, 0:DT * S], carry[:, :]), "cci")
            cx.dma(sync.dma_start(cc_sum_in.ap()[:, DT * S:], aend[:, :]), "cci")

        @blk.gpsimd
        def _(g):
            g.wait_ge(cx.sems["cci"], cx.cnt["cci"])
            cx.inc(g.collective_compute(
                "AllGather", ALU.bypass, replica_groups=rg,
                ins=[cc_sum_in.ap()], outs=[cc_sum_out.ap()]), "cc1")

        @blk.vector
        def _(v):
            cx.wait(v, "cc1")
            cx.inc(v.memset(sin[:], 0.0), "sini")

    for p in range(NCORE):
        with nc.Block() as blk:
            @blk.sync
            def _(sync):
                if p == 0:
                    cx.wait(sync, "sini")
                cx.dma(sync.dma_start(
                    summ[:, :], cc_sum_out.ap()[p * 128:(p + 1) * 128, :]), "lds")

            @blk.vector
            def _(v):
                cx.wait(v, "lds")
                gcol = gmask[:, p:p + 1]
                v.tensor_scalar_mul(t0s[:], summ[:, DT * S:], gcol)
                v.tensor_scalar(amaxt[:], gcol, -1.0, 1.0, ALU.mult, ALU.add)
                cx.vsync(v)
                v.tensor_scalar_add(t0s[:], t0s[:], amaxt[:, 0:1])
                cx.vsync(v)
                v.tensor_mul(sin[:], sin[:], t0s[:])
                v.tensor_scalar_mul(t1s[:], summ[:, 0:DT * S], gcol)
                cx.vsync(v)
                cx.inc(v.tensor_add(sin[:], sin[:], t1s[:]), "scomp")
                cx.vsync(v)

    with nc.Block() as blk:
        @blk.vector
        def _(v):
            cx.wait(v, "scomp")
            cx.inc(v.tensor_tensor(cc[:], bpar[:], sin[:], ALU.mult), "ccd")

    AFR.reset()
    cu2 = [AFR.alloc(TPC) for _ in range(2)]
    yy = [AFR.alloc(TPC) for _ in range(2)]
    pfac = [AFR.alloc(TPC) for _ in range(2)]
    wstage2 = AFR.alloc(1024)
    AB.off = xh_off
    yh = AB.alloc(DT * TPC)
    yl = AB.alloc(DT * TPC)

    for nm in ("pfe", "yco", "ysp2", "ldy"):
        cx.sem(nm)
    b_pfe = cx.cnt["pfe"]
    b_yco = cx.cnt["yco"]
    b_ysp2 = cx.cnt["ysp2"]
    with nc.Block() as blk:
        emit_wsplit(blk, wo_in, "o", wstage2)
        for j in range(DT):
            jb = j % 2

            @blk.sync
            def _(sync, j=j, jb=jb):
                if j >= 2:
                    cx.wait(sync, "pfe", b_pfe + S * (j - 1))
                    cx.wait(sync, "ysp2", b_ysp2 + j - 1)
                cx.dma(sync.dma_start(
                    cu2[jb], cum_hb.ap()[j * 128:(j + 1) * 128, :]), "ldy")
                cx.dma(sync.dma_start(
                    yy[jb], y0_hb.ap()[j * 128:(j + 1) * 128, :]), "ldy")

            @blk.scalar
            def _(s, j=j, jb=jb):
                cx.wait(s, "ldy", 32 * (j + 1))
                if j == 0:
                    cx.wait(s, "ccd")
                for si in range(S):
                    gs = j * S + si
                    if gs >= 2:
                        cx.wait(s, "yco", b_yco + gs - 1)
                    cx.inc(s.activation(
                        pfac[si % 2], cu2[jb], AF.Exp,
                        scale=nega[:, j * S + si:j * S + si + 1]), "pfe")

            @blk.vector
            def _(v, j=j, jb=jb):
                for si in range(S):
                    cx.wait(v, "pfe", b_pfe + j * S + si + 1)
                    cx.inc(v.scalar_tensor_tensor(
                        yy[jb], pfac[si % 2],
                        cc[:, j * S + si:j * S + si + 1], yy[jb],
                        ALU.mult, ALU.add), "yco")
                v.tensor_copy(yh[:, j * TPC:(j + 1) * TPC], yy[jb])
                cx.inc(v.scalar_tensor_tensor(
                    yl[:, j * TPC:(j + 1) * TPC], yy[jb], 0.0,
                    yh[:, j * TPC:(j + 1) * TPC],
                    ALU.add, ALU.subtract), "ysp2")

    AFR.reset()
    x2s = AFR.alloc(TT * 1024)
    ln2s = AFR.alloc(TT * 1024)
    sqs2 = AFR.alloc(1024)
    lstats = AFR.alloc(8 * TT)
    lnmx = AFR.alloc(NCH * TT)

    for c in range(NCH):
        with nc.Block() as blk:
            @blk.sync
            def _(sync):
                cx.dma(sync.dma_start(
                    x2s.rearrange("p (a d) -> p a d", d=D),
                    x_in.ap()[c * TC:(c + 1) * TC, :].rearrange(
                        "(a p) d -> p a d", p=128)), "ldx2")

            for a in range(TT):
                gt = c * TT + a

                @blk.tensor
                def _(t, a=a, gt=gt):
                    for eb in range(2):
                        gg = 2 * gt + eb
                        if gg >= 2:
                            cx.wait(t, "x2ev", gg - 1)
                        t.matmul(pmm[eb][:, 0:512], onesr[0:1, :],
                                 bo_row[0:1, eb * 512:(eb + 1) * 512],
                                 start=True, stop=False)
                        for k in range(DT):
                            lh = yh[:, k * TPC + c * TC + a * 128:
                                    k * TPC + c * TC + (a + 1) * 128]
                            ll = yl[:, k * TPC + c * TC + a * 128:
                                    k * TPC + c * TC + (a + 1) * 128]
                            rh = wT_h[:, k * 1024 + eb * 512: k * 1024 + (eb + 1) * 512]
                            rl = wT_l[:, k * 1024 + eb * 512: k * 1024 + (eb + 1) * 512]
                            for lhs, rhs in ((lh, rh), (lh, rl), (ll, rh)):
                                last = (k == DT - 1) and (lhs is ll)
                                mi = t.matmul(pmm[eb][:, 0:512], lhs, rhs,
                                              start=False, stop=last)
                                if last:
                                    cx.inc(mi, "mmo")

                @blk.vector
                def _(v, a=a):
                    cx.wait(v, "ldx2")
                    xrow = x2s[:, a * D:(a + 1) * D]
                    for eb in range(2):
                        cx.wait(v, "mmo", cx.cnt["mmo"] - (2 - 1 - eb))
                        sl = slice(eb * 512, (eb + 1) * 512)
                        cx.inc(v.scalar_tensor_tensor(
                            xrow[:, sl], pmm[eb][:, 0:512], 0.0, xrow[:, sl],
                            ALU.add, ALU.add), "x2ev")

            x2ev_now = cx.cnt["x2ev"]
            emit_ln2(blk, x2s, ln2s, sqs2, lstats, lnmx, c * TT,
                     "x2ev", [x2ev_now - 2 * (TT - 1 - a) for a in range(TT)])

            @blk.sync
            def _(sync):
                cx.wait(sync, "lnd")
                cx.dma(sync.dma_start(
                    x2_hb.ap()[c * TC:(c + 1) * TC, :].rearrange(
                        "(a p) d -> p a d", p=128),
                    x2s.rearrange("p (a d) -> p a d", d=D)), "sto2")
                cx.dma(sync.dma_start(
                    ln2_hb.ap()[c * TC:(c + 1) * TC, :].rearrange(
                        "(a p) d -> p a d", p=128),
                    ln2s.rearrange("p (a d) -> p a d", d=D)), "sto2")
                cx.wait(sync, "sto2")

    with nc.Block() as blk:
        @blk.vector
        def _(v):
            cx.wait(v, "lnx")
            cx.inc(v.reduce_max(amax[:], lnmx[:, 0:NCH * TT], axis=AX.X),
                   "lfm")

        @blk.tensor
        def _(t):
            cx.wait(t, "lfm")

    def emit_allreduce_max(cc_in_d, cc_out_d, tag):
        with nc.Block() as blk:
            @blk.tensor
            def _(t):
                cx.inc(t.transpose(ptr[0][0:1, 0:128], amax[:, 0:1], ident[:]),
                       "ftr")

            @blk.vector
            def _(v):
                cx.wait(v, "ftr")
                v.tensor_copy(fin[0:1, :], ptr[0][0:1, 0:128])
                cx.vsync(v)
                v.reduce_max(mx4[0:1, 0:1], fin[0:1, :], axis=AX.X)
                cx.vsync(v)
                v.tensor_copy(mx4[0:1, 1:2], mx4[0:1, 0:1])
                v.tensor_copy(mx4[0:1, 2:3], mx4[0:1, 0:1])
                cx.inc(v.tensor_copy(mx4[0:1, 3:4], mx4[0:1, 0:1]), "fmx")

            @blk.sync
            def _(sync):
                cx.wait(sync, "fmx")
                cx.dma(sync.dma_start(cc_in_d.ap(), mx4[0:1, :]), "ccm")

            @blk.gpsimd
            def _(g):
                g.wait_ge(cx.sems["ccm"], cx.cnt["ccm"])
                cx.inc(g.collective_compute(
                    "AllReduce", ALU.max, replica_groups=rg,
                    ins=[cc_in_d.ap()], outs=[cc_out_d.ap()]), "ccr")

            @blk.sync
            def _(sync):
                cx.wait(sync, "ccr")
                cx.dma(sync.dma_start(mx4[0:1, :], cc_out_d.ap()), "ccl")

    def emit_scale_prep(clip_dram, tag):
        with nc.Block() as blk:
            @blk.sync
            def _(sync):
                cx.dma(sync.dma_start(clip_sb[0:1, :], clip_dram.ap()),
                       "ldc")

            @blk.scalar
            def _(s):
                cx.wait(s, "ldc")
                cx.inc(s.activation(sxg[0:1, :], clip_sb[0:1, :], AF.Sigmoid),
                       "sg")

            @blk.vector
            def _(v):
                cx.wait(v, "sg")
                cx.wait(v, "ccl")
                v.tensor_mul(sxg[0:1, :], sxg[0:1, :], mx4[0:1, 0:1])
                cx.vsync(v)
                v.tensor_scalar_mul(sxg[0:1, :], sxg[0:1, :], 1.0 / QMAX)
                cx.vsync(v)
                cx.inc(v.reciprocal(rsxg[0:1, :], sxg[0:1, :]), "rs")
                cx.vsync(v)

            @blk.tensor
            def _(t):
                cx.wait(t, "rs")
                t.matmul(ptr[0][:, 0:1], onesr[0:1, :], rsxg[0:1, 0:1],
                         start=True, stop=True)
                cx.inc(t.matmul(ptr[1][:, 0:1], onesr[0:1, :], sxg[0:1, 0:1],
                                start=True, stop=True), "rb")

            @blk.vector
            def _(v):
                cx.wait(v, "rb")
                v.tensor_copy(rsx_bc[:, 0:1], ptr[0][:, 0:1])
                cx.inc(v.tensor_copy(sx_bc[:, 0:1], ptr[1][:, 0:1]), "rb2")

            @blk.tensor
            def _(t):
                cx.wait(t, "rb2")

    emit_allreduce_max(cc_mx_in, cc_mx_out, "1")
    emit_scale_prep(clipx_in, "1")

    AB.reset()
    wqgT = AB.alloc(DT * DI)
    wquT = AB.alloc(DT * DI)
    wq_nat = AB.alloc(1024)
    wq_natB = AB.alloc(1024)
    xq_tm = AB.alloc(1024)
    xqT = AB.alloc(DT * TC)
    AFR.reset()
    wsf = [AFR.alloc(1024) for _ in range(2)]
    lstage = AFR.alloc(TT * 1024)
    tq = AFR.alloc(1024)
    dbq = AFR.alloc(1024)
    dbqB = AFR.alloc(1024)
    c1q = AFR.alloc(1024)
    silu = [AFR.alloc(TC) for _ in range(2)]
    hst = [AFR.alloc(TC) for _ in range(2)]
    hmx = AFR.alloc(NCH * IT)
    rowcat = AFR.alloc(2 * IT)
    rs52 = AFR.alloc(2 * IT)

    strips = [(wg_in, wqgT, it) for it in range(IT)] + \
             [(wu_in, wquT, it) for it in range(IT)]
    NS2 = len(strips)
    for nm in ("rmx", "lwq", "lwq2", "wt0", "wrq", "wtt", "wtv", "wsc2",
               "sev"):
        cx.sem(nm)
    b_rmx = cx.cnt["rmx"]
    b_wt0 = cx.cnt["wt0"]
    b_wrq = cx.cnt["wrq"]
    b_wtt = cx.cnt["wtt"]
    with nc.Block() as blk:
        for t_, (w_dram, wqt, row) in enumerate(strips):
            pb = t_ % 2

            @blk.sync
            def _(sync, w_dram=w_dram, row=row, pb=pb, t_=t_):
                if t_ >= 2:
                    cx.wait(sync, "rmx", b_rmx + t_ - 1)
                cx.dma(sync.dma_start(
                    wsf[pb], w_dram.ap()[row * 128:(row + 1) * 128, :]),
                    "lwq")

            @blk.vector
            def _(v, pb=pb, t_=t_):
                cx.wait(v, "lwq")
                cx.inc(v.reduce_max(rowcat[:, t_:t_ + 1], wsf[pb], axis=AX.X,
                                    apply_absolute_value=True), "rmx")

        @blk.vector
        def _(v):
            cx.wait(v, "rmx")
            cx.vsync(v)
            v.reciprocal(rs52[:, 0:NS2], rowcat[:, 0:NS2])
            cx.vsync(v)
            cx.inc(v.tensor_scalar_mul(rs52[:, 0:NS2], rs52[:, 0:NS2], QMAX),
                   "wsc2")
            v.tensor_scalar_mul(sg[:, :], rowcat[:, 0:IT], 1.0 / QMAX)
            cx.inc(v.tensor_scalar_mul(su[:, :], rowcat[:, IT:2 * IT],
                                       1.0 / QMAX), "wsc2")
            cx.wait(v, "rb2")
            cx.vsync(v)
            v.scalar_tensor_tensor(sg[:, :], sg[:, :], sx_bc[:, 0:1],
                                   sg[:, :], ALU.mult, ALU.bypass)
            cx.inc(v.scalar_tensor_tensor(su[:, :], su[:, :], sx_bc[:, 0:1],
                                          su[:, :], ALU.mult, ALU.bypass),
                   "sev")
            cx.vsync(v)

        for t_, (w_dram, wqt, row) in enumerate(strips):
            pb = t_ % 2
            dq = dbq if pb == 0 else dbqB
            iq = i32b[:, :] if pb == 0 else i32c[:, :]
            wqn = wq_nat if pb == 0 else wq_natB

            @blk.sync
            def _(sync, w_dram=w_dram, row=row, pb=pb, t_=t_):
                if t_ >= 2:
                    cx.wait(sync, "wt0", b_wt0 + t_ - 1)
                cx.dma(sync.dma_start(
                    wsf[pb], w_dram.ap()[row * 128:(row + 1) * 128, :]),
                    "lwq2")

            @blk.scalar
            def _(s, pb=pb, t_=t_, dq=dq, iq=iq):
                cx.wait(s, "lwq2")
                if t_ == 0:
                    cx.wait(s, "wsc2", 1)
                if t_ >= 2:
                    cx.wait(s, "wrq", b_wrq + t_ - 1)
                s.activation(dq, wsf[pb], AF.Identity,
                             scale=rs52[:, t_:t_ + 1])
                cx.inc(s.activation(iq, wsf[pb], AF.Identity,
                                    scale=rs52[:, t_:t_ + 1]), "wt0")

            if t_ >= 1:
                @blk.vector
                def _(v, t_=t_):
                    pv = (t_ - 1) % 2
                    wqtp, rowp = strips[t_ - 1][1], strips[t_ - 1][2]
                    for jj in range(DT):
                        cx.wait(v, "wtt", b_wtt + (t_ - 1) * DT + jj + 1)
                        cx.inc(v.tensor_copy(
                            wqtp[:, jj * DI + rowp * 128:
                                 jj * DI + (rowp + 1) * 128],
                            ptb[jj % 2][:, 0:128]), "wtv")

            @blk.vector
            def _(v, pb=pb, t_=t_, dq=dq, iq=iq, wqn=wqn):
                cx.wait(v, "wt0", b_wt0 + t_ + 1)
                if t_ >= 2:
                    cx.wait(v, "wtt", b_wtt + (t_ - 1) * DT)
                v.scalar_tensor_tensor(c1q, dq, 0.0, iq,
                                       ALU.add, ALU.subtract)
                v.scalar_tensor_tensor(dq, c1q, -0.5, iq,
                                       ALU.is_lt, ALU.subtract)
                cx.inc(v.scalar_tensor_tensor(wqn, c1q, 0.5, dq,
                                              ALU.is_gt, ALU.subtract),
                       "wrq")

            @blk.tensor
            def _(t, t_=t_, wqn=wqn):
                cx.wait(t, "wrq", b_wrq + t_ + 1)
                base = cx.cnt["wtv"]
                for jj in range(DT):
                    if base + jj >= 2:
                        cx.wait(t, "wtv", base + jj - 1)
                    cx.inc(t.transpose(ptb[jj % 2][:, 0:128],
                                       wqn[:, jj * 128:(jj + 1) * 128],
                                       identb[:]), "wtt")

        @blk.vector
        def _(v):
            t_ = NS2
            wqtp, rowp = strips[t_ - 1][1], strips[t_ - 1][2]
            for jj in range(DT):
                cx.wait(v, "wtt", b_wtt + (t_ - 1) * DT + jj + 1)
                cx.inc(v.tensor_copy(
                    wqtp[:, jj * DI + rowp * 128:
                         jj * DI + (rowp + 1) * 128],
                    ptb[jj % 2][:, 0:128]), "wtv")

        @blk.tensor
        def _(t):
            cx.wait(t, "wtv")

    for c in range(NCH):
        with nc.Block() as blk:
            @blk.sync
            def _(sync):
                if c > 0:
                    cx.wait(sync, "ch3", c)
                cx.dma(sync.dma_start(
                    lstage.rearrange("p (a d) -> p a d", d=D),
                    ln2_hb.ap()[c * TC:(c + 1) * TC, :].rearrange(
                        "(a p) d -> p a d", p=128)), "ldl")

            @blk.scalar
            def _(s):
                cx.wait(s, "ldl")
                if c == 0:
                    cx.wait(s, "sev")
                for a in range(TT):
                    g = c * TT + a
                    if g >= 1:
                        cx.wait(s, "xqr", g)
                    s.activation(tq, lstage[:, a * D:(a + 1) * D],
                                 AF.Identity, scale=rsx_bc[:, 0:1])
                    cx.inc(s.activation(i32d[:, :],
                                        lstage[:, a * D:(a + 1) * D],
                                        AF.Identity, scale=rsx_bc[:, 0:1]),
                           "xqs")

            @blk.vector
            def _(v):
                cx.sem("xqtr")
                for a in range(TT):
                    g = c * TT + a
                    cx.wait(v, "xqs", g + 1)
                    if g >= 1:
                        cx.wait(v, "xqtr", g * DT)
                    v.scalar_tensor_tensor(dbq, tq, 0.0, i32d[:, :],
                                           ALU.add, ALU.subtract)
                    v.scalar_tensor_tensor(c1q, dbq, -0.5, i32d[:, :],
                                           ALU.is_lt, ALU.subtract)
                    v.scalar_tensor_tensor(tq, dbq, 0.5, c1q,
                                           ALU.is_gt, ALU.subtract)
                    cx.inc(v.tensor_scalar(xq_tm, tq, QMAX, -QMAX - 1.0,
                                           ALU.min, ALU.max), "xqr")
                    for k in range(DT):
                        gg = g * DT + k
                        cx.wait(v, "xqtr", gg + 1)
                        cx.inc(v.tensor_copy(
                            xqT[:, k * TC + a * 128: k * TC + (a + 1) * 128],
                            ptb[k % 2][:, 0:128]), "xqev")

            @blk.tensor
            def _(t):
                for a in range(TT):
                    g = c * TT + a
                    cx.wait(t, "xqr", g + 1)
                    for k in range(DT):
                        gg = g * DT + k
                        if gg >= 2:
                            cx.wait(t, "xqev", gg - 1)
                        cx.inc(t.transpose(ptb[k % 2][:, 0:128],
                                           xq_tm[:, k * 128:(k + 1) * 128],
                                           identb[:]), "xqtr")

            for it in range(IT):
                gi = c * IT + it
                pg = pmm[2 * (it % 2)]
                pu = pmm[2 * (it % 2) + 1]

                @blk.tensor
                def _(t, it=it, gi=gi, pg=pg, pu=pu):
                    if it == 0:
                        cx.wait(t, "xqev")
                    if gi >= 2:
                        cx.wait(t, "hev", gi - 1)
                    for pi, (pm, wqt) in enumerate(((pg, wqgT), (pu, wquT))):
                        first = True
                        for k in range(DT):
                            lhs = wqt[:, k * DI + it * 128: k * DI + (it + 1) * 128]
                            rhs = xqT[:, k * TC:(k + 1) * TC]
                            mi = t.matmul(pm[:, 0:512], lhs, rhs, start=first,
                                          stop=(k == DT - 1))
                            if pi == 1 and k == DT - 1:
                                cx.inc(mi, "mg")
                            first = False

                @blk.scalar
                def _(s, it=it, gi=gi, pg=pg):
                    cx.wait(s, "mg")
                    if gi >= 2:
                        cx.wait(s, "hev", gi - 1)
                    cx.inc(s.activation(silu[gi % 2], pg[:, 0:512], AF.Silu,
                                        scale=sg[:, it:it + 1]), "sil")

                @blk.vector
                def _(v, it=it, gi=gi, pu=pu):
                    cx.wait(v, "sil")
                    if gi >= 2:
                        cx.wait(v, "sth", 16 * (gi - 1))
                    v.scalar_tensor_tensor(hst[gi % 2], pu[:, 0:512],
                                           su[:, it:it + 1],
                                           silu[gi % 2], ALU.mult, ALU.mult)
                    cx.inc(v.reduce_max(hmx[:, gi:gi + 1], hst[gi % 2],
                                        axis=AX.X,
                                        apply_absolute_value=True), "hev")

                @blk.sync
                def _(sync, it=it, gi=gi, c=c):
                    cx.wait(sync, "hev")
                    cx.dma(sync.dma_start(
                        h_hb.ap()[it * 128:(it + 1) * 128, c * TC:(c + 1) * TC],
                        hst[gi % 2]), "sth")

            @blk.sync
            def _(sync):
                cx.wait(sync, "sth")
                cx.inc(sync.nop(), "ch3")

    with nc.Block() as blk:
        @blk.vector
        def _(v):
            cx.inc(v.reduce_max(amax[:], hmx[:, 0:NCH * IT], axis=AX.X), "hfm")

        @blk.tensor
        def _(t):
            cx.wait(t, "hfm")

    emit_allreduce_max(cc_mx2_in, cc_mx2_out, "2")
    emit_scale_prep(clipd_in, "2")

    AB.reset()
    wdnT = AB.alloc(IT * 1024)
    wq_nat4 = AB.alloc(DI)
    hq = [AB.alloc(IT * TC4) for _ in range(2)]
    AFR.reset()
    wstrip4 = AFR.alloc(DI)
    dbq4 = AFR.alloc(1024)
    c1q4 = AFR.alloc(1024)
    hs2 = [AFR.alloc(TC4) for _ in range(4)]
    x2stage = [AFR.alloc(TT4 * 1024) for _ in range(2)]
    outst = [AFR.alloc(TT4 * 1024) for _ in range(2)]
    tq4 = [AFR.alloc(TC4) for _ in range(4)]
    d4 = AFR.alloc(TC4)
    c4 = AFR.alloc(TC4)

    for nm in ("wseg", "wt0", "wtt", "wtv"):
        cx.sem(nm)
    with nc.Block() as blk:
        for jt in range(DT):
            b_wt0 = cx.cnt["wt0"]
            b_wseg = cx.cnt["wseg"]
            b_wtt = cx.cnt["wtt"]

            @blk.sync
            def _(sync, jt=jt, b_wt0=b_wt0):
                if jt >= 1:
                    cx.wait(sync, "wt0", b_wt0)
                cx.dma(sync.dma_start(wstrip4,
                                      wdn_in.ap()[jt * 128:(jt + 1) * 128, :]),
                       "lwd")

            @blk.vector
            def _(v, jt=jt, b_wt0=b_wt0):
                cx.wait(v, "lwd")
                v.reduce_max(rowm[:], wstrip4, axis=AX.X,
                             apply_absolute_value=True)
                cx.vsync(v)
                if jt >= 1:
                    cx.wait(v, "wt0", b_wt0)
                v.reciprocal(rs_w[:], rowm[:])
                cx.vsync(v)
                cx.inc(v.tensor_scalar_mul(rs_w[:], rs_w[:], QMAX), "wrs")
                cx.inc(v.tensor_scalar_mul(sw_col[:, jt:jt + 1], rowm[:],
                                           1.0 / QMAX), "wsc")

            @blk.scalar
            def _(s, b_wseg=b_wseg):
                cx.wait(s, "wrs")
                for seg in range(3):
                    lo = seg * 1024
                    hi = min(DI, lo + 1024)
                    if b_wseg + seg >= 1:
                        cx.wait(s, "wseg", b_wseg + seg)
                    s.activation(dbq4[:, 0:hi - lo], wstrip4[:, lo:hi],
                                 AF.Identity, scale=rs_w[:, 0:1])
                    cx.inc(s.activation(i32b[:, 0:hi - lo], wstrip4[:, lo:hi],
                                        AF.Identity, scale=rs_w[:, 0:1]), "wt0")

            @blk.vector
            def _(v, jt=jt, b_wtt=b_wtt):
                if jt >= 1:
                    cx.wait(v, "wtt", b_wtt)
                for seg in range(3):
                    lo = seg * 1024
                    hi = min(DI, lo + 1024)
                    w_ = hi - lo
                    cx.wait(v, "wt0", cx.cnt["wt0"] - (3 - 1 - seg))
                    v.scalar_tensor_tensor(c1q4[:, 0:w_], dbq4[:, 0:w_], 0.0,
                                           i32b[:, 0:w_], ALU.add,
                                           ALU.subtract)
                    v.scalar_tensor_tensor(dbq4[:, 0:w_], c1q4[:, 0:w_], -0.5,
                                           i32b[:, 0:w_], ALU.is_lt,
                                           ALU.subtract)
                    cx.inc(v.scalar_tensor_tensor(wq_nat4[:, lo:hi], c1q4[:, 0:w_],
                                                  0.5, dbq4[:, 0:w_], ALU.is_gt,
                                                  ALU.subtract), "wseg")
                cx.inc(v.memset(stat[:, 0:1], 0.0), "wrq")

            @blk.tensor
            def _(t):
                cx.wait(t, "wrq")
                base = cx.cnt["wtv"]
                for jj in range(IT):
                    if base + jj >= 2:
                        cx.wait(t, "wtv", base + jj - 1)
                    cx.inc(t.transpose(ptb[jj % 2][:, 0:128],
                                       wq_nat4[:, jj * 128:(jj + 1) * 128],
                                       identb[:]), "wtt")

            @blk.vector
            def _(v, jt=jt):
                for jj in range(IT):
                    cx.wait(v, "wtt", cx.cnt["wtt"] - (IT - 1 - jj))
                    cx.inc(v.tensor_copy(
                        wdnT[:, jj * 1024 + jt * 128: jj * 1024 + (jt + 1) * 128],
                        ptb[jj % 2][:, 0:128]), "wtv")

        @blk.tensor
        def _(t):
            cx.wait(t, "wtv")

    with nc.Block() as blk:
        @blk.tensor
        def _(t):
            cx.wait(t, "wsc")
            cx.sem("srev")
            base = cx.cnt["srev"]
            for jt in range(DT):
                if jt >= 2:
                    cx.wait(t, "srev", base + jt - 1)
                cx.inc(t.transpose(ptr[jt % 2][0:1, 0:128], sw_col[:, jt:jt + 1],
                                   ident[:]), "str")

        @blk.vector
        def _(v):
            for jt in range(DT):
                cx.wait(v, "str", cx.cnt["str"] - (DT - 1 - jt))
                cx.inc(v.tensor_copy(swd_row[0:1, jt * 128:(jt + 1) * 128],
                                     ptr[jt % 2][0:1, 0:128]), "srev")
            cx.wait(v, "rs")
            cx.vsync(v)
            cx.inc(v.tensor_scalar_mul(swd_row[0:1, :], swd_row[0:1, :],
                                       sxg[0:1, 0:1]), "srow")
            cx.vsync(v)

        @blk.tensor
        def _(t):
            cx.wait(t, "srow")
            for eb in range(2):
                cx.inc(t.matmul(pmm[eb][:, 0:512], onesr[0:1, :],
                                swd_row[0:1, eb * 512:(eb + 1) * 512],
                                start=True, stop=True), "sbc")

        @blk.vector
        def _(v):
            cx.wait(v, "sbc")
            for eb in range(2):
                cx.inc(v.tensor_copy(swd_bc[:, eb * 512:(eb + 1) * 512],
                                     pmm[eb][:, 0:512]), "sbv")

        @blk.tensor
        def _(t):
            cx.wait(t, "sbv")

    for nm in ("ldh", "ldx4", "hqs", "hqr", "md", "oev", "sto4"):
        cx.sem(nm)
    b_oev = cx.cnt["oev"]
    b_md = cx.cnt["md"]
    b_sto4 = cx.cnt["sto4"]
    b_hqs = cx.cnt["hqs"]
    b_hqr = cx.cnt["hqr"]
    with nc.Block() as blk:
        for c in range(NCH4):
            cb = c % 2

            @blk.sync
            def _(sync, c=c, cb=cb):
                if c >= 2:
                    cx.wait(sync, "oev", b_oev + 4 * (c - 1))
                    cx.wait(sync, "sto4", b_sto4 + 16 * (c - 1))
                cx.dma(sync.dma_start(
                    x2stage[cb].rearrange("p (a d) -> p a d", d=D),
                    x2_hb.ap()[c * TC4:(c + 1) * TC4, :].rearrange(
                        "(a p) d -> p a d", p=128)), "ldx4")

            for k in range(IT):
                gk = c * IT + k
                pb = gk % 4

                @blk.sync
                def _(sync, c=c, k=k, gk=gk, pb=pb):
                    if gk >= 4:
                        cx.wait(sync, "hqs", b_hqs + gk - 3)
                    cx.dma(sync.dma_start(
                        hs2[pb],
                        h_hb.ap()[k * 128:(k + 1) * 128,
                                  c * TC4:(c + 1) * TC4]), "ldh")

                @blk.scalar
                def _(s, c=c, k=k, gk=gk, pb=pb):
                    cx.wait(s, "ldh")
                    if c == 0 and k == 0:
                        cx.wait(s, "rb2")
                    if gk >= 4:
                        cx.wait(s, "hqr", b_hqr + gk - 3)
                    s.activation(tq4[pb], hs2[pb],
                                 AF.Identity, scale=rsx_bc[:, 0:1])
                    cx.inc(s.activation(
                        i32b[:, pb * TC4:(pb + 1) * TC4], hs2[pb],
                        AF.Identity, scale=rsx_bc[:, 0:1]), "hqs")

                @blk.vector
                def _(v, c=c, k=k, gk=gk, pb=pb, cb=cb):
                    i4 = i32b[:, pb * TC4:(pb + 1) * TC4]
                    cx.wait(v, "hqs", b_hqs + gk + 1)
                    if k == 0 and c >= 2:
                        cx.wait(v, "md", b_md + 4 * (c - 1))
                    v.scalar_tensor_tensor(d4, tq4[pb], 0.0, i4,
                                           ALU.add, ALU.subtract)
                    v.scalar_tensor_tensor(c4, d4, -0.5, i4,
                                           ALU.is_lt, ALU.subtract)
                    v.scalar_tensor_tensor(tq4[pb], d4, 0.5, c4, ALU.is_gt,
                                           ALU.subtract)
                    cx.inc(v.tensor_scalar(hq[cb][:, k * TC4:(k + 1) * TC4],
                                           tq4[pb],
                                           QMAX, -QMAX - 1.0, ALU.min,
                                           ALU.max), "hqr")

            for a in range(TT4):
                gt = c * TT4 + a

                @blk.tensor
                def _(t, c=c, a=a, gt=gt, cb=cb):
                    if a == 0:
                        cx.wait(t, "hqr")
                    for eb in range(2):
                        gg = 2 * gt + eb
                        if gg >= 2:
                            cx.wait(t, "oev", b_oev + gg - 1)
                        first = True
                        for k in range(IT):
                            lhs = hq[cb][:, k * TC4 + a * 128:
                                         k * TC4 + (a + 1) * 128]
                            rhs = wdnT[:, k * 1024 + eb * 512:
                                       k * 1024 + (eb + 1) * 512]
                            mi = t.matmul(pmm[eb][:, 0:512], lhs, rhs,
                                          start=first, stop=(k == IT - 1))
                            if k == IT - 1:
                                cx.inc(mi, "md")
                            first = False

                @blk.vector
                def _(v, c=c, a=a, gt=gt, cb=cb):
                    cx.wait(v, "ldx4", 16 * (c + 1))
                    x2row = x2stage[cb][:, a * D:(a + 1) * D]
                    orow = outst[cb][:, a * D:(a + 1) * D]
                    for eb in range(2):
                        cx.wait(v, "md", cx.cnt["md"] - (2 - 1 - eb))
                        sl = slice(eb * 512, (eb + 1) * 512)
                        v.scalar_tensor_tensor(orow[:, sl], pmm[eb][:, 0:512],
                                               0.0, swd_bc[:, sl],
                                               ALU.add, ALU.mult)
                        cx.inc(v.scalar_tensor_tensor(orow[:, sl],
                                                      x2row[:, sl], 0.0,
                                                      orow[:, sl],
                                                      ALU.add, ALU.add),
                               "oev")

            @blk.sync
            def _(sync, c=c, cb=cb):
                cx.wait(sync, "oev")
                cx.dma(sync.dma_start(
                    out_out.ap()[c * TC4:(c + 1) * TC4, :].rearrange(
                        "(a p) d -> p a d", p=128),
                    outst[cb].rearrange("p (a d) -> p a d", d=D)), "sto4")

    with nc.Block() as blk:
        @blk.sync
        def _(sync):
            cx.wait(sync, "sto4")
    stack.close()
    return nc


from concourse.bass_utils import run_bass_kernel_spmd

_CACHE = {}


def kernel(L0_attn_x, L0_attn_Wd, L0_attn_bd, L0_attn_a_log, L0_attn_b_param,
           L0_attn_Wo, L0_attn_bo, L0_ffn_clip_x, L0_ffn_clip_down,
           L0_ffn_w_gate, L0_ffn_w_up, L0_ffn_w_down):
    x = np.ascontiguousarray(np.asarray(L0_attn_x, dtype=np.float32))
    wd = np.ascontiguousarray(np.asarray(L0_attn_Wd, dtype=np.float32))
    bd = np.ascontiguousarray(np.asarray(L0_attn_bd, dtype=np.float32))
    alog = np.ascontiguousarray(np.asarray(L0_attn_a_log, dtype=np.float32))
    bpar = np.ascontiguousarray(np.asarray(L0_attn_b_param, dtype=np.float32))
    wo = np.ascontiguousarray(np.asarray(L0_attn_Wo, dtype=np.float32))
    bo = np.ascontiguousarray(np.asarray(L0_attn_bo, dtype=np.float32))
    clip_x = np.asarray(L0_ffn_clip_x, dtype=np.float32).reshape(1, 1)
    clip_dn = np.asarray(L0_ffn_clip_down, dtype=np.float32).reshape(1, 1)
    wg = np.ascontiguousarray(np.asarray(L0_ffn_w_gate, dtype=np.float32))
    wu = np.ascontiguousarray(np.asarray(L0_ffn_w_up, dtype=np.float32))
    wdn = np.ascontiguousarray(np.asarray(L0_ffn_w_down, dtype=np.float32))

    if "nc" not in _CACHE:
        _CACHE["nc"] = build_fused()
    nc = _CACHE["nc"]
    in_maps = []
    for i in range(NCORE):
        b, q = divmod(i, 4)
        gm = np.zeros((128, NCORE), dtype=np.float32)
        gm[:, b * 4:b * 4 + q] = 1.0
        in_maps.append({
            "x": np.ascontiguousarray(x[b, q * TPC:(q + 1) * TPC]),
            "wd": wd, "bd": bd, "alog": alog, "bpar": bpar, "wo": wo, "bo": bo,
            "clip_x": clip_x, "clip_down": clip_dn, "wg": wg, "wu": wu,
            "wdn": wdn, "gmask": gm})
    res = run_bass_kernel_spmd(nc, in_maps, core_ids=list(range(NCORE)))
    out = np.empty((B, T, D), dtype=np.float32)
    for i in range(NCORE):
        b, q = divmod(i, 4)
        out[b, q * TPC:(q + 1) * TPC] = res.results[i]["out"]
    return out


# version 14
# speedup vs baseline: 1.0783x; 1.0632x over previous
import contextlib

import numpy as np
import concourse.bass as bass
import concourse.mybir as mybir

F32 = mybir.dt.float32
BF16 = mybir.dt.bfloat16
I32 = mybir.dt.int32
AF = mybir.ActivationFunctionType
ALU = mybir.AluOpType
AX = mybir.AxisListType

NCORE = 8
B, T, D, S, DI = 2, 8192, 1024, 16, 2816
TPC = 2048
QMAX = 127.0
LN_EPS = 1e-5
DT = D // 128
IT = DI // 128
TC = 512
NCH = TPC // TC
TT = TC // 128
TC4 = 256
NCH4 = TPC // TC4
TT4 = TC4 // 128
SC = 1024
NSC = TPC // SC
SC2 = 512


class Ctx:
    def __init__(self, nc, stack):
        self.nc = nc
        self.stack = stack
        self.sems = {}
        self.cnt = {}

    def sem(self, name):
        if name not in self.sems:
            self.sems[name] = self.stack.enter_context(self.nc.semaphore(name))
            self.cnt[name] = 0
        return self.sems[name]

    def inc(self, inst, name, k=1):
        self.sem(name)
        self.cnt[name] += k
        inst.then_inc(self.sems[name], k)
        return inst

    def dma(self, inst, name):
        return self.inc(inst, name, 16)

    def wait(self, eng, name, val=None):
        self.sem(name)
        eng.wait_ge(self.sems[name], self.cnt[name] if val is None else val)

    def vsync(self, eng):
        self.sem("vself")
        self.cnt["vself"] += 1
        eng.engine_nop().then_inc(self.sems["vself"], 1)
        eng.wait_ge(self.sems["vself"], self.cnt["vself"])

    def sb(self, name, shape, dtype=F32):
        return self.stack.enter_context(self.nc.sbuf_tensor(name, shape, dtype))

    def ps(self, name, shape, dtype=F32):
        return self.stack.enter_context(self.nc.psum_tensor(name, shape, dtype))


class Arena:
    def __init__(self, cx, name, cols, dtype):
        self.t = cx.sb(name, [128, cols], dtype)
        self.cols = cols
        self.off = 0

    def reset(self):
        self.off = 0

    def alloc(self, cols):
        assert self.off + cols <= self.cols, (self.off, cols, self.cols)
        sl = self.t[:, self.off:self.off + cols]
        self.off += cols
        return sl


def build_fused():
    nc = bass.Bass("TRN2", target_bir_lowering=False, debug=False,
                   detect_race_conditions=False)
    x_in = nc.dram_tensor("x", [TPC, D], F32, kind="ExternalInput")
    wd_in = nc.dram_tensor("wd", [D, D], F32, kind="ExternalInput")
    bd_in = nc.dram_tensor("bd", [D], F32, kind="ExternalInput")
    alog_in = nc.dram_tensor("alog", [D, S], F32, kind="ExternalInput")
    bpar_in = nc.dram_tensor("bpar", [D, S], F32, kind="ExternalInput")
    wo_in = nc.dram_tensor("wo", [D, D], F32, kind="ExternalInput")
    bo_in = nc.dram_tensor("bo", [D], F32, kind="ExternalInput")
    clipx_in = nc.dram_tensor("clip_x", [1, 1], F32, kind="ExternalInput")
    clipd_in = nc.dram_tensor("clip_down", [1, 1], F32, kind="ExternalInput")
    wg_in = nc.dram_tensor("wg", [DI, D], F32, kind="ExternalInput")
    wu_in = nc.dram_tensor("wu", [DI, D], F32, kind="ExternalInput")
    wdn_in = nc.dram_tensor("wdn", [D, DI], F32, kind="ExternalInput")
    gmask_in = nc.dram_tensor("gmask", [128, NCORE], F32, kind="ExternalInput")
    out_out = nc.dram_tensor("out", [TPC, D], F32, kind="ExternalOutput")

    delta_hb = nc.dram_tensor("delta_hb", [D, TPC], F32, kind="Internal")
    v_hb = nc.dram_tensor("v_hb", [D, TPC], F32, kind="Internal")
    x2_hb = nc.dram_tensor("x2_hb", [TPC, D], F32, kind="Internal")
    ln2_hb = nc.dram_tensor("ln2_hb", [TPC, D], F32, kind="Internal")
    h_hb = nc.dram_tensor("h_hb", [DI, TPC], F32, kind="Internal")
    cum_hb = nc.dram_tensor("cum_hb", [D, TPC], F32, kind="Internal")
    y0_hb = nc.dram_tensor("y0_hb", [D, TPC], F32, kind="Internal")
    cc_sum_in = nc.dram_tensor("cc_sum_in", [128, 2 * DT * S], F32, kind="Internal")
    cc_sum_out = nc.dram_tensor("cc_sum_out", [NCORE * 128, 2 * DT * S], F32,
                                kind="Internal")
    cc_mx_in = nc.dram_tensor("cc_mx_in", [1, 4], F32, kind="Internal")
    cc_mx_out = nc.dram_tensor("cc_mx_out", [1, 4], F32, kind="Internal")
    cc_mx2_in = nc.dram_tensor("cc_mx2_in", [1, 4], F32, kind="Internal")
    cc_mx2_out = nc.dram_tensor("cc_mx2_out", [1, 4], F32, kind="Internal")

    stack = contextlib.ExitStack()
    cx = Ctx(nc, stack)
    rg = [list(range(NCORE))]

    ident = cx.sb("ident", [128, 128], F32)
    identb = cx.sb("identb", [128, 128], BF16)
    apos = cx.sb("apos", [128, DT * S], F32)
    nega = cx.sb("nega", [128, DT * S], F32)
    bpar = cx.sb("bpar_sb", [128, DT * S], F32)
    t0s = cx.sb("t0s", [128, DT * S], F32)
    t1s = cx.sb("t1s", [128, DT * S], F32)
    bd_sb = cx.sb("bd_sb", [128, DT], F32)
    bd_row = cx.sb("bd_row", [1, D], F32)
    bo_row = cx.sb("bo_row", [1, D], F32)
    onesr = cx.sb("onesr", [1, 128], F32)
    dsum = cx.sb("dsum", [128, DT], F32)
    carry = cx.sb("carry", [128, DT * S], F32)
    aend = cx.sb("aend", [128, DT * S], F32)
    sin = cx.sb("sin", [128, DT * S], F32)
    cc = cx.sb("cc", [128, DT * S], F32)
    gmask = cx.sb("gmask_sb", [128, NCORE], F32)
    summ = cx.sb("summ", [128, 2 * DT * S], F32)
    stat = cx.sb("stat", [128, 4], F32)
    amax = cx.sb("amax", [128, 1], F32)
    amaxt = cx.sb("amaxt", [128, 1], F32)
    fin = cx.sb("fin", [1, 128], F32)
    mx4 = cx.sb("mx4", [1, 4], F32)
    clip_sb = cx.sb("clip_sb", [1, 1], F32)
    sxg = cx.sb("sxg", [1, 1], F32)
    rsxg = cx.sb("rsxg", [1, 1], F32)
    rsx_bc = cx.sb("rsx_bc", [128, 1], F32)
    sx_bc = cx.sb("sx_bc", [128, 1], F32)
    rowm = cx.sb("rowm", [128, 1], F32)
    rs_w = cx.sb("rs_w", [128, 1], F32)
    sg = cx.sb("sg", [128, IT], F32)
    su = cx.sb("su", [128, IT], F32)
    sw_col = cx.sb("sw_col", [128, DT], F32)
    swd_row = cx.sb("swd_row", [1, D], F32)
    swd_bc = cx.sb("swd_bc", [128, D], F32)

    AB = Arena(cx, "arena_b", 53504, BF16)
    AFR = Arena(cx, "arena_f", 16384, F32)
    i32b = cx.sb("i32b", [128, 1024], I32)
    i32c = cx.sb("i32c", [128, 1024], I32)
    i32d = cx.sb("i32d", [128, 1024], I32)

    ptr = [cx.ps(f"ptr{i}", [128, 128], F32) for i in range(2)]
    ptb = [cx.ps(f"ptb{i}", [128, 128], BF16) for i in range(2)]
    pmm = [cx.ps(f"pmm{i}", [128, 512], F32) for i in range(4)]

    with nc.Block() as blk:
        @blk.gpsimd
        def _(g):
            cx.inc(g.iota(ident[:], pattern=[[1, 128]], base=0,
                          channel_multiplier=-1,
                          allow_small_or_imprecise_dtypes=True), "idg0")
            cx.wait(g, "idg0")
            cx.inc(g.tensor_single_scalar(ident[:], ident[:], 0.0, ALU.is_equal),
                   "idg")

        @blk.vector
        def _(v):
            cx.wait(v, "idg")
            cx.inc(v.tensor_copy(identb[:], ident[:]), "idb")

        @blk.sync
        def _(sync):
            cx.dma(sync.dma_start(
                t0s.ap().rearrange("p (j s) -> p j s", s=S),
                alog_in.ap().rearrange("(j p) s -> p j s", p=128)), "ld0")
            cx.dma(sync.dma_start(
                bpar.ap().rearrange("p (j s) -> p j s", s=S),
                bpar_in.ap().rearrange("(j p) s -> p j s", p=128)), "ld0")
            cx.dma(sync.dma_start(
                bd_row[0:1, :], bd_in.ap().rearrange("(a d) -> a d", a=1)), "ld0")
            cx.dma(sync.dma_start(
                bo_row[0:1, :], bo_in.ap().rearrange("(a d) -> a d", a=1)), "ld0")
            cx.dma(sync.dma_start(gmask[:, :], gmask_in.ap()), "ld0")

        @blk.scalar
        def _(s):
            cx.wait(s, "ld0")
            s.activation(t1s[:], t0s[:], AF.Abs)
            s.activation(t1s[:], t1s[:], AF.Exp, scale=-1.0)
            cx.inc(s.activation(t1s[:], t1s[:], AF.Ln, bias=1.0), "sps")

        @blk.vector
        def _(v):
            cx.wait(v, "sps")
            v.tensor_scalar_max(apos[:], t0s[:], 0.0)
            cx.vsync(v)
            v.tensor_add(apos[:], apos[:], t1s[:])
            cx.vsync(v)
            v.tensor_scalar_add(apos[:], apos[:], 1e-4)
            cx.vsync(v)
            v.tensor_scalar_mul(nega[:], apos[:], -1.0)
            cx.vsync(v)
            v.memset(onesr[0:1, :], 1.0)
            v.memset(carry[:], 0.0)
            v.memset(dsum[:], 0.0)
            cx.inc(v.memset(amax[:], 0.0), "ini")

        @blk.tensor
        def _(t):
            cx.wait(t, "ini")
            cx.wait(t, "idg")
            cx.sem("bdev")
            for j in range(DT):
                if j >= 2:
                    cx.wait(t, "bdev", j - 1)
                cx.inc(t.transpose(ptr[j % 2][:, 0:1],
                                   bd_row[0:1, j * 128:(j + 1) * 128],
                                   ident[0:1, 0:1]), "bdtr")

        @blk.vector
        def _(v):
            for j in range(DT):
                cx.wait(v, "bdtr", j + 1)
                cx.inc(v.tensor_copy(bd_sb[:, j:j + 1], ptr[j % 2][:, 0:1]), "bdev")

        @blk.tensor
        def _(t):
            cx.wait(t, "bdev")

    wT_h = AB.alloc(DT * 1024)
    wT_l = AB.alloc(DT * 1024)
    wnat_h = AB.alloc(1024)
    wnat_l = AB.alloc(1024)
    AB_base_p1 = AB.off

    def emit_wsplit(blk, w_dram, tag, wstage):
        cx.sem("wev")
        cx.sem("vsp")
        cx.sem("wtr")
        b_vsp = cx.cnt["vsp"]
        b_wtr = cx.cnt["wtr"]
        b_wev = cx.cnt["wev"]
        if True:
            @blk.sync
            def _(sync):
                for j in range(DT):
                    if j >= 1:
                        cx.wait(sync, "vsp", b_vsp + j)
                    cx.dma(sync.dma_start(wstage,
                                          w_dram.ap()[j * 128:(j + 1) * 128, :]),
                           "ldw")

            @blk.vector
            def _(v):
                for j in range(DT):
                    cx.wait(v, "ldw", (cx.cnt["ldw"] - 16 * (DT - 1 - j)))
                    if j >= 1:
                        cx.wait(v, "wtr", b_wtr + 2 * DT * j)
                    v.tensor_copy(wnat_h, wstage)
                    cx.inc(v.tensor_tensor(wnat_l, wstage, wnat_h, ALU.subtract),
                           "vsp")
                    for k in range(DT):
                        cx.wait(v, "wtr", b_wtr + 2 * DT * j + 2 * (k + 1))
                        v.tensor_copy(
                            wT_h[:, k * 1024 + j * 128: k * 1024 + (j + 1) * 128],
                            ptb[0][:, 0:128])
                        cx.inc(v.tensor_copy(
                            wT_l[:, k * 1024 + j * 128: k * 1024 + (j + 1) * 128],
                            ptb[1][:, 0:128]), "wev")

            @blk.tensor
            def _(t):
                for j in range(DT):
                    cx.wait(t, "vsp", b_vsp + j + 1)
                    for k in range(DT):
                        if DT * j + k >= 1:
                            cx.wait(t, "wev", b_wev + DT * j + k)
                        t.transpose(ptb[0][:, 0:128],
                                    wnat_h[:, k * 128:(k + 1) * 128], identb[:])
                        cx.inc(t.transpose(ptb[1][:, 0:128],
                                           wnat_l[:, k * 128:(k + 1) * 128],
                                           identb[:]), "wtr", 2)

    AFR.reset()
    wstage = AFR.alloc(1024)
    xstage = AFR.alloc(TT * 1024)
    xln_tm = AFR.alloc(TT * 1024)
    xlnT = AFR.alloc(DT * TC)
    sqs = AFR.alloc(1024)
    dstage = AFR.alloc(TC)
    vstage = AFR.alloc(TC)
    sp0 = AFR.alloc(TC)
    sp1 = sqs[:, 0:TC]
    lstats1 = AFR.alloc(8 * TT)
    xh_off = AB.off
    xh = AB.alloc(DT * TC)
    xl = AB.alloc(DT * TC)

    with nc.Block() as blk:
        emit_wsplit(blk, wd_in, "d", wstage)

    def emit_ln2(blk, xsrc, lnout, sqs_, stats, lnmxb, mxbase, ready_sem,
                 ready_vals):
        ms = stats[:, 0:TT]
        vs = stats[:, TT:2 * TT]
        t1 = stats[:, 2 * TT:3 * TT]
        t2 = stats[:, 3 * TT:4 * TT]
        rr = stats[:, 4 * TT:5 * TT]
        nb = stats[:, 5 * TT:6 * TT]
        for nm in ("lsq", "lst", "lnq", "lns", "lnr", "lnd", "lnx"):
            cx.sem(nm)
        b_lsq = cx.cnt["lsq"]
        b_lst = cx.cnt["lst"]
        b_lnr = cx.cnt["lnr"]
        b_lnd = cx.cnt["lnd"]

        @blk.scalar
        def _(s):
            for a in range(TT):
                cx.wait(s, ready_sem, ready_vals[a])
                if b_lst + a >= 1:
                    cx.wait(s, "lst", b_lst + a)
                cx.inc(s.activation(sqs_, xsrc[:, a * D:(a + 1) * D],
                                    AF.Square), "lsq")

        @blk.vector
        def _(v):
            for a in range(TT):
                cx.wait(v, "lsq", b_lsq + a + 1)
                v.reduce_sum(ms[:, a:a + 1], xsrc[:, a * D:(a + 1) * D],
                             axis=AX.X)
                cx.inc(v.reduce_sum(vs[:, a:a + 1], sqs_, axis=AX.X), "lst")
            cx.vsync(v)
            v.tensor_scalar_mul(ms, ms, 1.0 / D)
            v.tensor_scalar_mul(vs, vs, 1.0 / D)
            cx.vsync(v)
            v.tensor_tensor(t1, ms, ms, ALU.mult)
            cx.vsync(v)
            v.tensor_tensor(vs, vs, t1, ALU.subtract)
            cx.vsync(v)
            cx.inc(v.tensor_scalar_add(t1, vs, LN_EPS), "lnq")

        @blk.scalar
        def _(s):
            cx.wait(s, "lnq")
            cx.inc(s.activation(t2, t1, AF.Sqrt), "lns")

        @blk.vector
        def _(v):
            cx.wait(v, "lns")
            v.reciprocal(rr, t2)
            cx.vsync(v)
            v.tensor_tensor(t2, rr, rr, ALU.mult)
            cx.vsync(v)
            v.tensor_tensor(t2, t1, t2, ALU.mult)
            cx.vsync(v)
            v.tensor_scalar(t2, t2, -0.5, 1.5, ALU.mult, ALU.add)
            cx.vsync(v)
            v.tensor_tensor(rr, rr, t2, ALU.mult)
            cx.vsync(v)
            v.tensor_tensor(nb, ms, rr, ALU.mult)
            cx.vsync(v)
            cx.inc(v.tensor_scalar_mul(nb, nb, -1.0), "lnr")
            cx.vsync(v)

        @blk.scalar
        def _(s):
            cx.wait(s, "lnr", b_lnr + 1)
            for a in range(TT):
                cx.inc(s.activation(lnout[:, a * D:(a + 1) * D],
                                    xsrc[:, a * D:(a + 1) * D], AF.Identity,
                                    bias=nb[:, a:a + 1],
                                    scale=rr[:, a:a + 1]), "lnd")

        if lnmxb is not None:
            @blk.vector
            def _(v):
                for a in range(TT):
                    cx.wait(v, "lnd", b_lnd + a + 1)
                    cx.inc(v.reduce_max(lnmxb[:, mxbase + a:mxbase + a + 1],
                                        lnout[:, a * D:(a + 1) * D],
                                        axis=AX.X,
                                        apply_absolute_value=True), "lnx")


    for c in range(NCH):
        with nc.Block() as blk:
            @blk.sync
            def _(sync):
                cx.dma(sync.dma_start(
                    xstage.rearrange("p (a d) -> p a d", d=D),
                    x_in.ap()[c * TC:(c + 1) * TC, :].rearrange(
                        "(a p) d -> p a d", p=128)), "ldx")

            emit_ln2(blk, xstage, xln_tm, sqs, lstats1, None, 0, "ldx",
                     [cx.cnt["ldx"]] * TT)

            @blk.tensor
            def _(t):
                cx.sem("xev")
                cx.wait(t, "lnd")
                base = cx.cnt["xev"]
                for k in range(DT):
                    for a in range(TT):
                        g = k * TT + a
                        if base + g >= 2:
                            cx.wait(t, "xev", base + g - 1)
                        cx.inc(t.transpose(
                            ptr[g % 2][:, 0:128],
                            xln_tm[:, a * D + k * 128: a * D + (k + 1) * 128],
                            ident[:]), "xtr")

            @blk.vector
            def _(v):
                base = cx.cnt["xtr"] - DT * TT
                for k in range(DT):
                    for a in range(TT):
                        g = k * TT + a
                        cx.wait(v, "xtr", base + g + 1)
                        cx.inc(v.tensor_copy(
                            xlnT[:, k * TC + a * 128: k * TC + (a + 1) * 128],
                            ptr[g % 2][:, 0:128]), "xev")
                for k in range(DT):
                    v.tensor_copy(xh[:, k * TC:(k + 1) * TC],
                                  xlnT[:, k * TC:(k + 1) * TC])
                    cx.inc(v.tensor_tensor(xl[:, k * TC:(k + 1) * TC],
                                           xlnT[:, k * TC:(k + 1) * TC],
                                           xh[:, k * TC:(k + 1) * TC],
                                           ALU.subtract), "xsp")

            for et in range(DT):
                g = c * DT + et
                pm = pmm[et % 2]

                @blk.tensor
                def _(t, et=et, g=g, pm=pm):
                    if et == 0:
                        cx.wait(t, "xsp")
                    if g >= 2:
                        cx.wait(t, "spev", g - 1)
                    first = True
                    for k in range(DT):
                        lh = wT_h[:, k * 1024 + et * 128: k * 1024 + (et + 1) * 128]
                        ll = wT_l[:, k * 1024 + et * 128: k * 1024 + (et + 1) * 128]
                        rh = xh[:, k * TC:(k + 1) * TC]
                        rl = xl[:, k * TC:(k + 1) * TC]
                        for lhs, rhs in ((lh, rh), (lh, rl), (ll, rh)):
                            last = (k == DT - 1) and (lhs is ll)
                            mi = t.matmul(pm[:, :], lhs, rhs, start=first, stop=last)
                            if last:
                                cx.inc(mi, "mm")
                            first = False

                @blk.scalar
                def _(s, et=et, g=g, pm=pm):
                    cx.wait(s, "mm")
                    if g >= 1:
                        cx.wait(s, "spev", g)
                    bcol = bd_sb[:, et:et + 1]
                    s.activation(sp0, pm[:, :], AF.Abs, bias=bcol)
                    s.activation(sp1, sp0, AF.Exp, scale=-1.0)
                    cx.inc(s.activation(xstage[:, 0:TC], sp1, AF.Ln, bias=1.0),
                           "spl")

                @blk.vector
                def _(v, et=et, g=g, pm=pm):
                    cx.wait(v, "spl")
                    if g >= 1:
                        cx.wait(v, "sto1", 32 * g)
                    bcol = bd_sb[:, et:et + 1]
                    v.scalar_tensor_tensor(dstage, pm[:, :], bcol, sp0,
                                           ALU.add, ALU.add)
                    v.scalar_tensor_tensor(dstage, dstage, 0.5,
                                           xstage[:, 0:TC],
                                           ALU.mult, ALU.add)
                    v.scalar_tensor_tensor(vstage, dstage, 0.0,
                                           xlnT[:, et * TC:(et + 1) * TC],
                                           ALU.add, ALU.mult)
                    v.reduce_sum(amaxt[:], dstage, axis=AX.X)
                    cx.vsync(v)
                    cx.inc(v.tensor_add(dsum[:, et:et + 1], dsum[:, et:et + 1],
                                        amaxt[:]), "spev")

                @blk.sync
                def _(sync, et=et, c=c):
                    cx.wait(sync, "spev")
                    cx.dma(sync.dma_start(
                        delta_hb.ap()[et * 128:(et + 1) * 128, c * TC:(c + 1) * TC],
                        dstage), "sto1")
                    cx.dma(sync.dma_start(
                        v_hb.ap()[et * 128:(et + 1) * 128, c * TC:(c + 1) * TC],
                        vstage), "sto1")

    AFR.reset()
    dj = AFR.alloc(TPC)
    vjb = AFR.alloc(TPC)
    cumj = AFR.alloc(TPC)
    y0j = AFR.alloc(TPC)
    abz = [AFR.alloc(TPC) for _ in range(2)]
    sgz = [AFR.alloc(TPC) for _ in range(2)]

    with nc.Block() as blk:
        @blk.sync
        def _(sync):
            cx.wait(sync, "sto1")

        @blk.vector
        def _(v):
            v.memset(swd_bc[:, :], 1.0)

    for nm in ("zse", "zscs", "zdone", "y0st"):
        cx.sem(nm)
    b_zse = cx.cnt["zse"]
    b_zscs = cx.cnt["zscs"]
    b_zd = cx.cnt["zdone"]
    b_y0 = cx.cnt["y0st"]
    with nc.Block() as blk:
        for j in range(DT):
            @blk.sync
            def _(sync, j=j):
                if j >= 1:
                    cx.wait(sync, "zdone", b_zd + j)
                    cx.dma(sync.dma_start(
                        y0_hb.ap()[(j - 1) * 128:j * 128, :], y0j), "y0st")
                    cx.dma(sync.dma_start(
                        cum_hb.ap()[(j - 1) * 128:j * 128, :], cumj), "y0st")
                    cx.wait(sync, "zse", b_zse + S * j)
                cx.dma(sync.dma_start(
                    dj, delta_hb.ap()[j * 128:(j + 1) * 128, :]), "ldz")
                cx.dma(sync.dma_start(
                    vjb, v_hb.ap()[j * 128:(j + 1) * 128, :]), "ldz")

            @blk.scalar
            def _(s, j=j):
                cx.wait(s, "ldz")
                for si in range(S):
                    gs = j * S + si
                    if gs >= 2:
                        cx.wait(s, "zscs", b_zscs + gs - 1)
                    cx.inc(s.activation(
                        abz[si % 2], dj, AF.Exp,
                        scale=nega[:, j * S + si:j * S + si + 1]), "zse")

            @blk.vector
            def _(v, j=j):
                cx.wait(v, "ldz")
                if j >= 1:
                    cx.wait(v, "y0st", b_y0 + 32 * j)
                half = TPC // 2
                v.tensor_tensor_scan(cumj[:, 0:half], swd_bc[:, 0:half],
                                     dj[:, 0:half], 0.0, ALU.mult, ALU.add)
                for si in range(S):
                    if si >= 2:
                        v.tensor_copy(
                            carry[:, j * S + si - 2:j * S + si - 1],
                            sgz[si % 2][:, TPC - 1:TPC])
                    cx.wait(v, "zse", b_zse + j * S + si + 1)
                    cx.inc(v.tensor_tensor_scan(
                        sgz[si % 2], abz[si % 2], vjb, 0.0,
                        ALU.mult, ALU.add), "zscs")
                    if si == 0:
                        v.tensor_tensor_scan(
                            cumj[:, half:TPC], swd_bc[:, 0:half],
                            dj[:, half:TPC], cumj[:, half - 1:half],
                            ALU.mult, ALU.add)
                        v.scalar_tensor_tensor(
                            y0j, sgz[0], bpar[:, j * S:j * S + 1], sgz[0],
                            ALU.mult, ALU.bypass)
                    else:
                        v.scalar_tensor_tensor(
                            y0j, sgz[si % 2],
                            bpar[:, j * S + si:j * S + si + 1], y0j,
                            ALU.mult, ALU.add)
                cx.vsync(v)
                v.tensor_copy(carry[:, j * S + S - 2:j * S + S - 1],
                              sgz[0][:, TPC - 1:TPC])
                cx.inc(v.tensor_copy(carry[:, j * S + S - 1:j * S + S],
                                     sgz[1][:, TPC - 1:TPC]), "zdone")

        @blk.sync
        def _(sync):
            cx.wait(sync, "zdone", b_zd + DT)
            cx.dma(sync.dma_start(
                y0_hb.ap()[(DT - 1) * 128:DT * 128, :], y0j), "y0st")
            cx.dma(sync.dma_start(
                cum_hb.ap()[(DT - 1) * 128:DT * 128, :], cumj), "y0st")
            cx.wait(sync, "y0st")

    with nc.Block() as blk:
        @blk.vector
        def _(v):
            cx.wait(v, "zdone")
            for j in range(DT):
                cx.inc(v.tensor_scalar_mul(t0s[:, j * S:(j + 1) * S],
                                           nega[:, j * S:(j + 1) * S],
                                           dsum[:, j:j + 1]), "ae0")
            cx.vsync(v)

        @blk.scalar
        def _(s):
            cx.wait(s, "ae0")
            cx.inc(s.activation(aend[:], t0s[:], AF.Exp), "ae1")

        @blk.sync
        def _(sync):
            cx.wait(sync, "ae1")
            cx.dma(sync.dma_start(cc_sum_in.ap()[:, 0:DT * S], carry[:, :]), "cci")
            cx.dma(sync.dma_start(cc_sum_in.ap()[:, DT * S:], aend[:, :]), "cci")

        @blk.gpsimd
        def _(g):
            g.wait_ge(cx.sems["cci"], cx.cnt["cci"])
            cx.inc(g.collective_compute(
                "AllGather", ALU.bypass, replica_groups=rg,
                ins=[cc_sum_in.ap()], outs=[cc_sum_out.ap()]), "cc1")

        @blk.vector
        def _(v):
            cx.wait(v, "cc1")
            cx.inc(v.memset(sin[:], 0.0), "sini")

    for p in range(NCORE):
        with nc.Block() as blk:
            @blk.sync
            def _(sync):
                if p == 0:
                    cx.wait(sync, "sini")
                cx.dma(sync.dma_start(
                    summ[:, :], cc_sum_out.ap()[p * 128:(p + 1) * 128, :]), "lds")

            @blk.vector
            def _(v):
                cx.wait(v, "lds")
                gcol = gmask[:, p:p + 1]
                v.tensor_scalar_mul(t0s[:], summ[:, DT * S:], gcol)
                v.tensor_scalar(amaxt[:], gcol, -1.0, 1.0, ALU.mult, ALU.add)
                cx.vsync(v)
                v.tensor_scalar_add(t0s[:], t0s[:], amaxt[:, 0:1])
                cx.vsync(v)
                v.tensor_mul(sin[:], sin[:], t0s[:])
                v.tensor_scalar_mul(t1s[:], summ[:, 0:DT * S], gcol)
                cx.vsync(v)
                cx.inc(v.tensor_add(sin[:], sin[:], t1s[:]), "scomp")
                cx.vsync(v)

    with nc.Block() as blk:
        @blk.vector
        def _(v):
            cx.wait(v, "scomp")
            cx.inc(v.tensor_tensor(cc[:], bpar[:], sin[:], ALU.mult), "ccd")

    AFR.reset()
    cu2 = [AFR.alloc(TPC) for _ in range(2)]
    yy = [AFR.alloc(TPC) for _ in range(2)]
    pfac = [AFR.alloc(TPC) for _ in range(2)]
    wstage2 = AFR.alloc(1024)
    AB.off = xh_off
    yh = AB.alloc(DT * TPC)
    yl = AB.alloc(DT * TPC)

    for nm in ("pfe", "yco", "ysp2", "ldy"):
        cx.sem(nm)
    b_pfe = cx.cnt["pfe"]
    b_yco = cx.cnt["yco"]
    b_ysp2 = cx.cnt["ysp2"]
    with nc.Block() as blk:
        emit_wsplit(blk, wo_in, "o", wstage2)
        for j in range(DT):
            jb = j % 2

            @blk.sync
            def _(sync, j=j, jb=jb):
                if j >= 2:
                    cx.wait(sync, "pfe", b_pfe + S * (j - 1))
                    cx.wait(sync, "ysp2", b_ysp2 + j - 1)
                cx.dma(sync.dma_start(
                    cu2[jb], cum_hb.ap()[j * 128:(j + 1) * 128, :]), "ldy")
                cx.dma(sync.dma_start(
                    yy[jb], y0_hb.ap()[j * 128:(j + 1) * 128, :]), "ldy")

            @blk.scalar
            def _(s, j=j, jb=jb):
                cx.wait(s, "ldy", 32 * (j + 1))
                if j == 0:
                    cx.wait(s, "ccd")
                for si in range(S):
                    gs = j * S + si
                    if gs >= 2:
                        cx.wait(s, "yco", b_yco + gs - 1)
                    cx.inc(s.activation(
                        pfac[si % 2], cu2[jb], AF.Exp,
                        scale=nega[:, j * S + si:j * S + si + 1]), "pfe")

            @blk.vector
            def _(v, j=j, jb=jb):
                for si in range(S):
                    cx.wait(v, "pfe", b_pfe + j * S + si + 1)
                    cx.inc(v.scalar_tensor_tensor(
                        yy[jb], pfac[si % 2],
                        cc[:, j * S + si:j * S + si + 1], yy[jb],
                        ALU.mult, ALU.add), "yco")
                v.tensor_copy(yh[:, j * TPC:(j + 1) * TPC], yy[jb])
                cx.inc(v.scalar_tensor_tensor(
                    yl[:, j * TPC:(j + 1) * TPC], yy[jb], 0.0,
                    yh[:, j * TPC:(j + 1) * TPC],
                    ALU.add, ALU.subtract), "ysp2")

    AFR.reset()
    x2s = AFR.alloc(TT * 1024)
    ln2s = AFR.alloc(TT * 1024)
    sqs2 = AFR.alloc(1024)
    lstats = AFR.alloc(8 * TT)
    lnmx = AFR.alloc(NCH * TT)

    for c in range(NCH):
        with nc.Block() as blk:
            @blk.sync
            def _(sync):
                cx.dma(sync.dma_start(
                    x2s.rearrange("p (a d) -> p a d", d=D),
                    x_in.ap()[c * TC:(c + 1) * TC, :].rearrange(
                        "(a p) d -> p a d", p=128)), "ldx2")

            for a in range(TT):
                gt = c * TT + a

                @blk.tensor
                def _(t, a=a, gt=gt):
                    for eb in range(2):
                        gg = 2 * gt + eb
                        if gg >= 2:
                            cx.wait(t, "x2ev", gg - 1)
                        t.matmul(pmm[eb][:, 0:512], onesr[0:1, :],
                                 bo_row[0:1, eb * 512:(eb + 1) * 512],
                                 start=True, stop=False)
                        for k in range(DT):
                            lh = yh[:, k * TPC + c * TC + a * 128:
                                    k * TPC + c * TC + (a + 1) * 128]
                            ll = yl[:, k * TPC + c * TC + a * 128:
                                    k * TPC + c * TC + (a + 1) * 128]
                            rh = wT_h[:, k * 1024 + eb * 512: k * 1024 + (eb + 1) * 512]
                            rl = wT_l[:, k * 1024 + eb * 512: k * 1024 + (eb + 1) * 512]
                            for lhs, rhs in ((lh, rh), (lh, rl), (ll, rh)):
                                last = (k == DT - 1) and (lhs is ll)
                                mi = t.matmul(pmm[eb][:, 0:512], lhs, rhs,
                                              start=False, stop=last)
                                if last:
                                    cx.inc(mi, "mmo")

                @blk.vector
                def _(v, a=a):
                    cx.wait(v, "ldx2")
                    xrow = x2s[:, a * D:(a + 1) * D]
                    for eb in range(2):
                        cx.wait(v, "mmo", cx.cnt["mmo"] - (2 - 1 - eb))
                        sl = slice(eb * 512, (eb + 1) * 512)
                        cx.inc(v.scalar_tensor_tensor(
                            xrow[:, sl], pmm[eb][:, 0:512], 0.0, xrow[:, sl],
                            ALU.add, ALU.add), "x2ev")

            x2ev_now = cx.cnt["x2ev"]
            emit_ln2(blk, x2s, ln2s, sqs2, lstats, lnmx, c * TT,
                     "x2ev", [x2ev_now - 2 * (TT - 1 - a) for a in range(TT)])

            @blk.sync
            def _(sync):
                cx.wait(sync, "lnd")
                cx.dma(sync.dma_start(
                    x2_hb.ap()[c * TC:(c + 1) * TC, :].rearrange(
                        "(a p) d -> p a d", p=128),
                    x2s.rearrange("p (a d) -> p a d", d=D)), "sto2")
                cx.dma(sync.dma_start(
                    ln2_hb.ap()[c * TC:(c + 1) * TC, :].rearrange(
                        "(a p) d -> p a d", p=128),
                    ln2s.rearrange("p (a d) -> p a d", d=D)), "sto2")
                cx.wait(sync, "sto2")

    with nc.Block() as blk:
        @blk.vector
        def _(v):
            cx.wait(v, "lnx")
            cx.inc(v.reduce_max(amax[:], lnmx[:, 0:NCH * TT], axis=AX.X),
                   "lfm")

        @blk.tensor
        def _(t):
            cx.wait(t, "lfm")

    def emit_allreduce_max(cc_in_d, cc_out_d, tag):
        with nc.Block() as blk:
            @blk.tensor
            def _(t):
                cx.inc(t.transpose(ptr[0][0:1, 0:128], amax[:, 0:1], ident[:]),
                       "ftr")

            @blk.vector
            def _(v):
                cx.wait(v, "ftr")
                v.tensor_copy(fin[0:1, :], ptr[0][0:1, 0:128])
                cx.vsync(v)
                v.reduce_max(mx4[0:1, 0:1], fin[0:1, :], axis=AX.X)
                cx.vsync(v)
                v.tensor_copy(mx4[0:1, 1:2], mx4[0:1, 0:1])
                v.tensor_copy(mx4[0:1, 2:3], mx4[0:1, 0:1])
                cx.inc(v.tensor_copy(mx4[0:1, 3:4], mx4[0:1, 0:1]), "fmx")

            @blk.sync
            def _(sync):
                cx.wait(sync, "fmx")
                cx.dma(sync.dma_start(cc_in_d.ap(), mx4[0:1, :]), "ccm")

            @blk.gpsimd
            def _(g):
                g.wait_ge(cx.sems["ccm"], cx.cnt["ccm"])
                cx.inc(g.collective_compute(
                    "AllReduce", ALU.max, replica_groups=rg,
                    ins=[cc_in_d.ap()], outs=[cc_out_d.ap()]), "ccr")

            @blk.sync
            def _(sync):
                cx.wait(sync, "ccr")
                cx.dma(sync.dma_start(mx4[0:1, :], cc_out_d.ap()), "ccl")

    def emit_scale_prep(clip_dram, tag):
        with nc.Block() as blk:
            @blk.sync
            def _(sync):
                cx.dma(sync.dma_start(clip_sb[0:1, :], clip_dram.ap()),
                       "ldc")

            @blk.scalar
            def _(s):
                cx.wait(s, "ldc")
                cx.inc(s.activation(sxg[0:1, :], clip_sb[0:1, :], AF.Sigmoid),
                       "sg")

            @blk.vector
            def _(v):
                cx.wait(v, "sg")
                cx.wait(v, "ccl")
                v.tensor_mul(sxg[0:1, :], sxg[0:1, :], mx4[0:1, 0:1])
                cx.vsync(v)
                v.tensor_scalar_mul(sxg[0:1, :], sxg[0:1, :], 1.0 / QMAX)
                cx.vsync(v)
                cx.inc(v.reciprocal(rsxg[0:1, :], sxg[0:1, :]), "rs")
                cx.vsync(v)

            @blk.tensor
            def _(t):
                cx.wait(t, "rs")
                t.matmul(ptr[0][:, 0:1], onesr[0:1, :], rsxg[0:1, 0:1],
                         start=True, stop=True)
                cx.inc(t.matmul(ptr[1][:, 0:1], onesr[0:1, :], sxg[0:1, 0:1],
                                start=True, stop=True), "rb")

            @blk.vector
            def _(v):
                cx.wait(v, "rb")
                v.tensor_copy(rsx_bc[:, 0:1], ptr[0][:, 0:1])
                cx.inc(v.tensor_copy(sx_bc[:, 0:1], ptr[1][:, 0:1]), "rb2")

            @blk.tensor
            def _(t):
                cx.wait(t, "rb2")

    emit_allreduce_max(cc_mx_in, cc_mx_out, "1")
    emit_scale_prep(clipx_in, "1")

    AB.reset()
    wqgT = AB.alloc(DT * DI)
    wquT = AB.alloc(DT * DI)
    wq_nat = AB.alloc(1024)
    wq_natB = AB.alloc(1024)
    xq_tm = AB.alloc(1024)
    xqT = AB.alloc(DT * TC)
    AFR.reset()
    wsf = [AFR.alloc(1024) for _ in range(2)]
    lstage = AFR.alloc(TT * 1024)
    tq = AFR.alloc(1024)
    dbq = AFR.alloc(1024)
    dbqB = AFR.alloc(1024)
    c1q = AFR.alloc(1024)
    silu = [AFR.alloc(TC) for _ in range(2)]
    hst = [AFR.alloc(TC) for _ in range(2)]
    hmx = AFR.alloc(NCH * IT)
    rowcat = AFR.alloc(2 * IT)
    rs52 = AFR.alloc(2 * IT)

    strips = [(wg_in, wqgT, it) for it in range(IT)] + \
             [(wu_in, wquT, it) for it in range(IT)]
    NS2 = len(strips)
    for nm in ("rmx", "lwq", "lwq2", "wt0", "wrq", "wtt", "wtv", "wsc2",
               "sev"):
        cx.sem(nm)
    b_rmx = cx.cnt["rmx"]
    b_wt0 = cx.cnt["wt0"]
    b_wrq = cx.cnt["wrq"]
    b_wtt = cx.cnt["wtt"]
    with nc.Block() as blk:
        for t_, (w_dram, wqt, row) in enumerate(strips):
            pb = t_ % 2

            @blk.sync
            def _(sync, w_dram=w_dram, row=row, pb=pb, t_=t_):
                if t_ >= 2:
                    cx.wait(sync, "rmx", b_rmx + t_ - 1)
                cx.dma(sync.dma_start(
                    wsf[pb], w_dram.ap()[row * 128:(row + 1) * 128, :]),
                    "lwq")

            @blk.vector
            def _(v, pb=pb, t_=t_):
                cx.wait(v, "lwq")
                cx.inc(v.reduce_max(rowcat[:, t_:t_ + 1], wsf[pb], axis=AX.X,
                                    apply_absolute_value=True), "rmx")

        @blk.vector
        def _(v):
            cx.wait(v, "rmx")
            cx.vsync(v)
            v.reciprocal(rs52[:, 0:NS2], rowcat[:, 0:NS2])
            cx.vsync(v)
            cx.inc(v.tensor_scalar_mul(rs52[:, 0:NS2], rs52[:, 0:NS2], QMAX),
                   "wsc2")
            v.tensor_scalar_mul(sg[:, :], rowcat[:, 0:IT], 1.0 / QMAX)
            cx.inc(v.tensor_scalar_mul(su[:, :], rowcat[:, IT:2 * IT],
                                       1.0 / QMAX), "wsc2")
            cx.wait(v, "rb2")
            cx.vsync(v)
            v.scalar_tensor_tensor(sg[:, :], sg[:, :], sx_bc[:, 0:1],
                                   sg[:, :], ALU.mult, ALU.bypass)
            cx.inc(v.scalar_tensor_tensor(su[:, :], su[:, :], sx_bc[:, 0:1],
                                          su[:, :], ALU.mult, ALU.bypass),
                   "sev")
            cx.vsync(v)

        for t_, (w_dram, wqt, row) in enumerate(strips):
            pb = t_ % 2
            dq = dbq if pb == 0 else dbqB
            iq = i32b[:, :] if pb == 0 else i32c[:, :]
            wqn = wq_nat if pb == 0 else wq_natB

            @blk.sync
            def _(sync, w_dram=w_dram, row=row, pb=pb, t_=t_):
                if t_ >= 2:
                    cx.wait(sync, "wt0", b_wt0 + t_ - 1)
                cx.dma(sync.dma_start(
                    wsf[pb], w_dram.ap()[row * 128:(row + 1) * 128, :]),
                    "lwq2")

            @blk.scalar
            def _(s, pb=pb, t_=t_, dq=dq, iq=iq):
                cx.wait(s, "lwq2")
                if t_ == 0:
                    cx.wait(s, "wsc2", 1)
                if t_ >= 2:
                    cx.wait(s, "wrq", b_wrq + t_ - 1)
                s.activation(dq, wsf[pb], AF.Identity,
                             scale=rs52[:, t_:t_ + 1])
                cx.inc(s.activation(iq, wsf[pb], AF.Identity,
                                    scale=rs52[:, t_:t_ + 1]), "wt0")

            if t_ >= 1:
                @blk.vector
                def _(v, t_=t_):
                    pv = (t_ - 1) % 2
                    wqtp, rowp = strips[t_ - 1][1], strips[t_ - 1][2]
                    for jj in range(DT):
                        cx.wait(v, "wtt", b_wtt + (t_ - 1) * DT + jj + 1)
                        cx.inc(v.tensor_copy(
                            wqtp[:, jj * DI + rowp * 128:
                                 jj * DI + (rowp + 1) * 128],
                            ptb[jj % 2][:, 0:128]), "wtv")

            @blk.vector
            def _(v, pb=pb, t_=t_, dq=dq, iq=iq, wqn=wqn):
                cx.wait(v, "wt0", b_wt0 + t_ + 1)
                if t_ >= 2:
                    cx.wait(v, "wtt", b_wtt + (t_ - 1) * DT)
                v.scalar_tensor_tensor(c1q, dq, 0.0, iq,
                                       ALU.add, ALU.subtract)
                v.scalar_tensor_tensor(dq, c1q, -0.5, iq,
                                       ALU.is_lt, ALU.subtract)
                cx.inc(v.scalar_tensor_tensor(wqn, c1q, 0.5, dq,
                                              ALU.is_gt, ALU.subtract),
                       "wrq")

            @blk.tensor
            def _(t, t_=t_, wqn=wqn):
                cx.wait(t, "wrq", b_wrq + t_ + 1)
                base = cx.cnt["wtv"]
                for jj in range(DT):
                    if base + jj >= 2:
                        cx.wait(t, "wtv", base + jj - 1)
                    cx.inc(t.transpose(ptb[jj % 2][:, 0:128],
                                       wqn[:, jj * 128:(jj + 1) * 128],
                                       identb[:]), "wtt")

        @blk.vector
        def _(v):
            t_ = NS2
            wqtp, rowp = strips[t_ - 1][1], strips[t_ - 1][2]
            for jj in range(DT):
                cx.wait(v, "wtt", b_wtt + (t_ - 1) * DT + jj + 1)
                cx.inc(v.tensor_copy(
                    wqtp[:, jj * DI + rowp * 128:
                         jj * DI + (rowp + 1) * 128],
                    ptb[jj % 2][:, 0:128]), "wtv")

        @blk.tensor
        def _(t):
            cx.wait(t, "wtv")

    for c in range(NCH):
        with nc.Block() as blk:
            @blk.sync
            def _(sync):
                if c > 0:
                    cx.wait(sync, "ch3", c)
                cx.dma(sync.dma_start(
                    lstage.rearrange("p (a d) -> p a d", d=D),
                    ln2_hb.ap()[c * TC:(c + 1) * TC, :].rearrange(
                        "(a p) d -> p a d", p=128)), "ldl")

            @blk.scalar
            def _(s):
                cx.wait(s, "ldl")
                if c == 0:
                    cx.wait(s, "sev")
                for a in range(TT):
                    g = c * TT + a
                    if g >= 1:
                        cx.wait(s, "xqr", g)
                    s.activation(tq, lstage[:, a * D:(a + 1) * D],
                                 AF.Identity, scale=rsx_bc[:, 0:1])
                    cx.inc(s.activation(i32d[:, :],
                                        lstage[:, a * D:(a + 1) * D],
                                        AF.Identity, scale=rsx_bc[:, 0:1]),
                           "xqs")

            @blk.vector
            def _(v):
                cx.sem("xqtr")
                for a in range(TT):
                    g = c * TT + a
                    cx.wait(v, "xqs", g + 1)
                    if g >= 1:
                        cx.wait(v, "xqtr", g * DT)
                    v.scalar_tensor_tensor(dbq, tq, 0.0, i32d[:, :],
                                           ALU.add, ALU.subtract)
                    v.scalar_tensor_tensor(c1q, dbq, -0.5, i32d[:, :],
                                           ALU.is_lt, ALU.subtract)
                    v.scalar_tensor_tensor(tq, dbq, 0.5, c1q,
                                           ALU.is_gt, ALU.subtract)
                    cx.inc(v.tensor_scalar(xq_tm, tq, QMAX, -QMAX - 1.0,
                                           ALU.min, ALU.max), "xqr")
                    for k in range(DT):
                        gg = g * DT + k
                        cx.wait(v, "xqtr", gg + 1)
                        cx.inc(v.tensor_copy(
                            xqT[:, k * TC + a * 128: k * TC + (a + 1) * 128],
                            ptb[k % 2][:, 0:128]), "xqev")

            @blk.tensor
            def _(t):
                for a in range(TT):
                    g = c * TT + a
                    cx.wait(t, "xqr", g + 1)
                    for k in range(DT):
                        gg = g * DT + k
                        if gg >= 2:
                            cx.wait(t, "xqev", gg - 1)
                        cx.inc(t.transpose(ptb[k % 2][:, 0:128],
                                           xq_tm[:, k * 128:(k + 1) * 128],
                                           identb[:]), "xqtr")

            for it in range(IT):
                gi = c * IT + it
                pg = pmm[2 * (it % 2)]
                pu = pmm[2 * (it % 2) + 1]

                @blk.tensor
                def _(t, it=it, gi=gi, pg=pg, pu=pu):
                    if it == 0:
                        cx.wait(t, "xqev")
                    if gi >= 2:
                        cx.wait(t, "hev", gi - 1)
                    for pi, (pm, wqt) in enumerate(((pg, wqgT), (pu, wquT))):
                        first = True
                        for k in range(DT):
                            lhs = wqt[:, k * DI + it * 128: k * DI + (it + 1) * 128]
                            rhs = xqT[:, k * TC:(k + 1) * TC]
                            mi = t.matmul(pm[:, 0:512], lhs, rhs, start=first,
                                          stop=(k == DT - 1))
                            if pi == 1 and k == DT - 1:
                                cx.inc(mi, "mg")
                            first = False

                @blk.scalar
                def _(s, it=it, gi=gi, pg=pg):
                    cx.wait(s, "mg")
                    if gi >= 2:
                        cx.wait(s, "hev", gi - 1)
                    cx.inc(s.activation(silu[gi % 2], pg[:, 0:512], AF.Silu,
                                        scale=sg[:, it:it + 1]), "sil")

                @blk.vector
                def _(v, it=it, gi=gi, pu=pu):
                    cx.wait(v, "sil")
                    if gi >= 2:
                        cx.wait(v, "sth", 16 * (gi - 1))
                    v.scalar_tensor_tensor(hst[gi % 2], pu[:, 0:512],
                                           su[:, it:it + 1],
                                           silu[gi % 2], ALU.mult, ALU.mult)
                    cx.inc(v.reduce_max(hmx[:, gi:gi + 1], hst[gi % 2],
                                        axis=AX.X,
                                        apply_absolute_value=True), "hev")

                @blk.sync
                def _(sync, it=it, gi=gi, c=c):
                    cx.wait(sync, "hev")
                    cx.dma(sync.dma_start(
                        h_hb.ap()[it * 128:(it + 1) * 128, c * TC:(c + 1) * TC],
                        hst[gi % 2]), "sth")

            @blk.sync
            def _(sync):
                cx.wait(sync, "sth")
                cx.inc(sync.nop(), "ch3")

    with nc.Block() as blk:
        @blk.vector
        def _(v):
            cx.inc(v.reduce_max(amax[:], hmx[:, 0:NCH * IT], axis=AX.X), "hfm")

        @blk.tensor
        def _(t):
            cx.wait(t, "hfm")

    emit_allreduce_max(cc_mx2_in, cc_mx2_out, "2")
    emit_scale_prep(clipd_in, "2")

    AB.reset()
    wdnT = AB.alloc(IT * 1024)
    wq_nat4 = AB.alloc(DI)
    hq = AB.alloc(IT * TC4)
    AFR.reset()
    wstrip4 = AFR.alloc(DI)
    dbq4 = AFR.alloc(1024)
    c1q4 = AFR.alloc(1024)
    hstage = AFR.alloc(IT * TC4)
    x2stage = AFR.alloc(TT4 * 1024)
    outst = AFR.alloc(TT4 * 1024)
    tq4 = [AFR.alloc(TC4) for _ in range(4)]
    d4 = AFR.alloc(TC4)
    c4 = AFR.alloc(TC4)

    for nm in ("wseg", "wt0", "wtt", "wtv"):
        cx.sem(nm)
    with nc.Block() as blk:
        for jt in range(DT):
            b_wt0 = cx.cnt["wt0"]
            b_wseg = cx.cnt["wseg"]
            b_wtt = cx.cnt["wtt"]

            @blk.sync
            def _(sync, jt=jt, b_wt0=b_wt0):
                if jt >= 1:
                    cx.wait(sync, "wt0", b_wt0)
                cx.dma(sync.dma_start(wstrip4,
                                      wdn_in.ap()[jt * 128:(jt + 1) * 128, :]),
                       "lwd")

            @blk.vector
            def _(v, jt=jt, b_wt0=b_wt0):
                cx.wait(v, "lwd")
                v.reduce_max(rowm[:], wstrip4, axis=AX.X,
                             apply_absolute_value=True)
                cx.vsync(v)
                if jt >= 1:
                    cx.wait(v, "wt0", b_wt0)
                v.reciprocal(rs_w[:], rowm[:])
                cx.vsync(v)
                cx.inc(v.tensor_scalar_mul(rs_w[:], rs_w[:], QMAX), "wrs")
                cx.inc(v.tensor_scalar_mul(sw_col[:, jt:jt + 1], rowm[:],
                                           1.0 / QMAX), "wsc")

            @blk.scalar
            def _(s, b_wseg=b_wseg):
                cx.wait(s, "wrs")
                for seg in range(3):
                    lo = seg * 1024
                    hi = min(DI, lo + 1024)
                    if b_wseg + seg >= 1:
                        cx.wait(s, "wseg", b_wseg + seg)
                    s.activation(dbq4[:, 0:hi - lo], wstrip4[:, lo:hi],
                                 AF.Identity, scale=rs_w[:, 0:1])
                    cx.inc(s.activation(i32b[:, 0:hi - lo], wstrip4[:, lo:hi],
                                        AF.Identity, scale=rs_w[:, 0:1]), "wt0")

            @blk.vector
            def _(v, jt=jt, b_wtt=b_wtt):
                if jt >= 1:
                    cx.wait(v, "wtt", b_wtt)
                for seg in range(3):
                    lo = seg * 1024
                    hi = min(DI, lo + 1024)
                    w_ = hi - lo
                    cx.wait(v, "wt0", cx.cnt["wt0"] - (3 - 1 - seg))
                    v.scalar_tensor_tensor(c1q4[:, 0:w_], dbq4[:, 0:w_], 0.0,
                                           i32b[:, 0:w_], ALU.add,
                                           ALU.subtract)
                    v.scalar_tensor_tensor(dbq4[:, 0:w_], c1q4[:, 0:w_], -0.5,
                                           i32b[:, 0:w_], ALU.is_lt,
                                           ALU.subtract)
                    cx.inc(v.scalar_tensor_tensor(wq_nat4[:, lo:hi], c1q4[:, 0:w_],
                                                  0.5, dbq4[:, 0:w_], ALU.is_gt,
                                                  ALU.subtract), "wseg")
                cx.inc(v.memset(stat[:, 0:1], 0.0), "wrq")

            @blk.tensor
            def _(t):
                cx.wait(t, "wrq")
                base = cx.cnt["wtv"]
                for jj in range(IT):
                    if base + jj >= 2:
                        cx.wait(t, "wtv", base + jj - 1)
                    cx.inc(t.transpose(ptb[jj % 2][:, 0:128],
                                       wq_nat4[:, jj * 128:(jj + 1) * 128],
                                       identb[:]), "wtt")

            @blk.vector
            def _(v, jt=jt):
                for jj in range(IT):
                    cx.wait(v, "wtt", cx.cnt["wtt"] - (IT - 1 - jj))
                    cx.inc(v.tensor_copy(
                        wdnT[:, jj * 1024 + jt * 128: jj * 1024 + (jt + 1) * 128],
                        ptb[jj % 2][:, 0:128]), "wtv")

        @blk.tensor
        def _(t):
            cx.wait(t, "wtv")

    with nc.Block() as blk:
        @blk.tensor
        def _(t):
            cx.wait(t, "wsc")
            cx.sem("srev")
            base = cx.cnt["srev"]
            for jt in range(DT):
                if jt >= 2:
                    cx.wait(t, "srev", base + jt - 1)
                cx.inc(t.transpose(ptr[jt % 2][0:1, 0:128], sw_col[:, jt:jt + 1],
                                   ident[:]), "str")

        @blk.vector
        def _(v):
            for jt in range(DT):
                cx.wait(v, "str", cx.cnt["str"] - (DT - 1 - jt))
                cx.inc(v.tensor_copy(swd_row[0:1, jt * 128:(jt + 1) * 128],
                                     ptr[jt % 2][0:1, 0:128]), "srev")
            cx.wait(v, "rs")
            cx.vsync(v)
            cx.inc(v.tensor_scalar_mul(swd_row[0:1, :], swd_row[0:1, :],
                                       sxg[0:1, 0:1]), "srow")
            cx.vsync(v)

        @blk.tensor
        def _(t):
            cx.wait(t, "srow")
            for eb in range(2):
                cx.inc(t.matmul(pmm[eb][:, 0:512], onesr[0:1, :],
                                swd_row[0:1, eb * 512:(eb + 1) * 512],
                                start=True, stop=True), "sbc")

        @blk.vector
        def _(v):
            cx.wait(v, "sbc")
            for eb in range(2):
                cx.inc(v.tensor_copy(swd_bc[:, eb * 512:(eb + 1) * 512],
                                     pmm[eb][:, 0:512]), "sbv")

        @blk.tensor
        def _(t):
            cx.wait(t, "sbv")

    for nm in ("ldh", "ldx4", "hqs", "hqr", "md", "oev", "sto4"):
        cx.sem(nm)
    for c in range(NCH4):
        b_hqs = cx.cnt["hqs"]
        b_hqr = cx.cnt["hqr"]
        b_md = cx.cnt["md"]
        with nc.Block() as blk:
            @blk.sync
            def _(sync, c=c):
                cx.dma(sync.dma_start(
                    hstage.rearrange("p (k t) -> p k t", t=TC4),
                    h_hb.ap()[:, c * TC4:(c + 1) * TC4].rearrange(
                        "(k p) t -> p k t", p=128)), "ldh")
                cx.dma(sync.dma_start(
                    x2stage.rearrange("p (a d) -> p a d", d=D),
                    x2_hb.ap()[c * TC4:(c + 1) * TC4, :].rearrange(
                        "(a p) d -> p a d", p=128)), "ldx4")

            @blk.scalar
            def _(s, c=c, b_hqr=b_hqr):
                cx.wait(s, "ldh")
                if c == 0:
                    cx.wait(s, "rb2")
                for k in range(IT):
                    pb = k % 4
                    if k >= 4:
                        cx.wait(s, "hqr", b_hqr + k - 3)
                    s.activation(tq4[pb], hstage[:, k * TC4:(k + 1) * TC4],
                                 AF.Identity, scale=rsx_bc[:, 0:1])
                    cx.inc(s.activation(
                        i32b[:, pb * TC4:(pb + 1) * TC4],
                        hstage[:, k * TC4:(k + 1) * TC4],
                        AF.Identity, scale=rsx_bc[:, 0:1]), "hqs")

            @blk.vector
            def _(v, c=c, b_hqs=b_hqs):
                for k in range(IT):
                    pb = k % 4
                    i4 = i32b[:, pb * TC4:(pb + 1) * TC4]
                    cx.wait(v, "hqs", b_hqs + k + 1)
                    v.scalar_tensor_tensor(d4, tq4[pb], 0.0, i4,
                                           ALU.add, ALU.subtract)
                    v.scalar_tensor_tensor(c4, d4, -0.5, i4,
                                           ALU.is_lt, ALU.subtract)
                    v.scalar_tensor_tensor(tq4[pb], d4, 0.5, c4, ALU.is_gt,
                                           ALU.subtract)
                    cx.inc(v.tensor_scalar(hq[:, k * TC4:(k + 1) * TC4],
                                           tq4[pb],
                                           QMAX, -QMAX - 1.0, ALU.min,
                                           ALU.max), "hqr")

            @blk.tensor
            def _(t, c=c, b_hqr=b_hqr):
                for k in range(IT):
                    cx.wait(t, "hqr", b_hqr + k + 1)
                    for a in range(TT4):
                        for eb in range(2):
                            lhs = hq[:, k * TC4 + a * 128:
                                     k * TC4 + (a + 1) * 128]
                            rhs = wdnT[:, k * 1024 + eb * 512:
                                       k * 1024 + (eb + 1) * 512]
                            mi = t.matmul(pmm[2 * a + eb][:, 0:512], lhs, rhs,
                                          start=(k == 0), stop=(k == IT - 1))
                            if k == IT - 1:
                                cx.inc(mi, "md")

            @blk.vector
            def _(v, c=c, b_md=b_md):
                cx.wait(v, "ldx4")
                if c >= 1:
                    cx.wait(v, "sto4", 16 * c)
                for a in range(TT4):
                    x2row = x2stage[:, a * D:(a + 1) * D]
                    orow = outst[:, a * D:(a + 1) * D]
                    for eb in range(2):
                        cx.wait(v, "md", b_md + 2 * a + eb + 1)
                        sl = slice(eb * 512, (eb + 1) * 512)
                        v.scalar_tensor_tensor(orow[:, sl],
                                               pmm[2 * a + eb][:, 0:512],
                                               0.0, swd_bc[:, sl],
                                               ALU.add, ALU.mult)
                        cx.inc(v.scalar_tensor_tensor(orow[:, sl],
                                                      x2row[:, sl], 0.0,
                                                      orow[:, sl],
                                                      ALU.add, ALU.add),
                               "oev")

            @blk.sync
            def _(sync, c=c):
                cx.wait(sync, "oev")
                cx.dma(sync.dma_start(
                    out_out.ap()[c * TC4:(c + 1) * TC4, :].rearrange(
                        "(a p) d -> p a d", p=128),
                    outst.rearrange("p (a d) -> p a d", d=D)), "sto4")

    with nc.Block() as blk:
        @blk.sync
        def _(sync):
            cx.wait(sync, "sto4")
    stack.close()
    return nc


from concourse.bass_utils import run_bass_kernel_spmd

_CACHE = {}


def kernel(L0_attn_x, L0_attn_Wd, L0_attn_bd, L0_attn_a_log, L0_attn_b_param,
           L0_attn_Wo, L0_attn_bo, L0_ffn_clip_x, L0_ffn_clip_down,
           L0_ffn_w_gate, L0_ffn_w_up, L0_ffn_w_down):
    x = np.ascontiguousarray(np.asarray(L0_attn_x, dtype=np.float32))
    wd = np.ascontiguousarray(np.asarray(L0_attn_Wd, dtype=np.float32))
    bd = np.ascontiguousarray(np.asarray(L0_attn_bd, dtype=np.float32))
    alog = np.ascontiguousarray(np.asarray(L0_attn_a_log, dtype=np.float32))
    bpar = np.ascontiguousarray(np.asarray(L0_attn_b_param, dtype=np.float32))
    wo = np.ascontiguousarray(np.asarray(L0_attn_Wo, dtype=np.float32))
    bo = np.ascontiguousarray(np.asarray(L0_attn_bo, dtype=np.float32))
    clip_x = np.asarray(L0_ffn_clip_x, dtype=np.float32).reshape(1, 1)
    clip_dn = np.asarray(L0_ffn_clip_down, dtype=np.float32).reshape(1, 1)
    wg = np.ascontiguousarray(np.asarray(L0_ffn_w_gate, dtype=np.float32))
    wu = np.ascontiguousarray(np.asarray(L0_ffn_w_up, dtype=np.float32))
    wdn = np.ascontiguousarray(np.asarray(L0_ffn_w_down, dtype=np.float32))

    if "nc" not in _CACHE:
        _CACHE["nc"] = build_fused()
    nc = _CACHE["nc"]
    in_maps = []
    for i in range(NCORE):
        b, q = divmod(i, 4)
        gm = np.zeros((128, NCORE), dtype=np.float32)
        gm[:, b * 4:b * 4 + q] = 1.0
        in_maps.append({
            "x": np.ascontiguousarray(x[b, q * TPC:(q + 1) * TPC]),
            "wd": wd, "bd": bd, "alog": alog, "bpar": bpar, "wo": wo, "bo": bo,
            "clip_x": clip_x, "clip_down": clip_dn, "wg": wg, "wu": wu,
            "wdn": wdn, "gmask": gm})
    res = run_bass_kernel_spmd(nc, in_maps, core_ids=list(range(NCORE)))
    out = np.empty((B, T, D), dtype=np.float32)
    for i in range(NCORE):
        b, q = divmod(i, 4)
        out[b, q * TPC:(q + 1) * TPC] = res.results[i]["out"]
    return out


# version 15 (final)
# speedup vs baseline: 1.0825x; 1.0039x over previous
import contextlib

import numpy as np
import concourse.bass as bass
import concourse.mybir as mybir

F32 = mybir.dt.float32
BF16 = mybir.dt.bfloat16
I32 = mybir.dt.int32
AF = mybir.ActivationFunctionType
ALU = mybir.AluOpType
AX = mybir.AxisListType

NCORE = 8
B, T, D, S, DI = 2, 8192, 1024, 16, 2816
TPC = 2048
QMAX = 127.0
LN_EPS = 1e-5
DT = D // 128
IT = DI // 128
TC = 512
NCH = TPC // TC
TT = TC // 128
TC4 = 256
NCH4 = TPC // TC4
TT4 = TC4 // 128
SC = 1024
NSC = TPC // SC
SC2 = 512


class Ctx:
    def __init__(self, nc, stack):
        self.nc = nc
        self.stack = stack
        self.sems = {}
        self.cnt = {}

    def sem(self, name):
        if name not in self.sems:
            self.sems[name] = self.stack.enter_context(self.nc.semaphore(name))
            self.cnt[name] = 0
        return self.sems[name]

    def inc(self, inst, name, k=1):
        self.sem(name)
        self.cnt[name] += k
        inst.then_inc(self.sems[name], k)
        return inst

    def dma(self, inst, name):
        return self.inc(inst, name, 16)

    def wait(self, eng, name, val=None):
        self.sem(name)
        eng.wait_ge(self.sems[name], self.cnt[name] if val is None else val)

    def vsync(self, eng):
        self.sem("vself")
        self.cnt["vself"] += 1
        eng.engine_nop().then_inc(self.sems["vself"], 1)
        eng.wait_ge(self.sems["vself"], self.cnt["vself"])

    def sb(self, name, shape, dtype=F32):
        return self.stack.enter_context(self.nc.sbuf_tensor(name, shape, dtype))

    def ps(self, name, shape, dtype=F32):
        return self.stack.enter_context(self.nc.psum_tensor(name, shape, dtype))


class Arena:
    def __init__(self, cx, name, cols, dtype):
        self.t = cx.sb(name, [128, cols], dtype)
        self.cols = cols
        self.off = 0

    def reset(self):
        self.off = 0

    def alloc(self, cols):
        assert self.off + cols <= self.cols, (self.off, cols, self.cols)
        sl = self.t[:, self.off:self.off + cols]
        self.off += cols
        return sl


def build_fused():
    nc = bass.Bass("TRN2", target_bir_lowering=False, debug=False,
                   detect_race_conditions=False)
    x_in = nc.dram_tensor("x", [TPC, D], F32, kind="ExternalInput")
    wd_in = nc.dram_tensor("wd", [D, D], F32, kind="ExternalInput")
    bd_in = nc.dram_tensor("bd", [D], F32, kind="ExternalInput")
    alog_in = nc.dram_tensor("alog", [D, S], F32, kind="ExternalInput")
    bpar_in = nc.dram_tensor("bpar", [D, S], F32, kind="ExternalInput")
    wo_in = nc.dram_tensor("wo", [D, D], F32, kind="ExternalInput")
    bo_in = nc.dram_tensor("bo", [D], F32, kind="ExternalInput")
    clipx_in = nc.dram_tensor("clip_x", [1, 1], F32, kind="ExternalInput")
    clipd_in = nc.dram_tensor("clip_down", [1, 1], F32, kind="ExternalInput")
    wg_in = nc.dram_tensor("wg", [DI, D], F32, kind="ExternalInput")
    wu_in = nc.dram_tensor("wu", [DI, D], F32, kind="ExternalInput")
    wdn_in = nc.dram_tensor("wdn", [D, DI], F32, kind="ExternalInput")
    gmask_in = nc.dram_tensor("gmask", [128, NCORE], F32, kind="ExternalInput")
    out_out = nc.dram_tensor("out", [TPC, D], F32, kind="ExternalOutput")

    delta_hb = nc.dram_tensor("delta_hb", [D, TPC], F32, kind="Internal")
    v_hb = nc.dram_tensor("v_hb", [D, TPC], F32, kind="Internal")
    x2_hb = nc.dram_tensor("x2_hb", [TPC, D], F32, kind="Internal")
    ln2_hb = nc.dram_tensor("ln2_hb", [TPC, D], F32, kind="Internal")
    h_hb = nc.dram_tensor("h_hb", [DI, TPC], F32, kind="Internal")
    cum_hb = nc.dram_tensor("cum_hb", [D, TPC], F32, kind="Internal")
    y0_hb = nc.dram_tensor("y0_hb", [D, TPC], F32, kind="Internal")
    cc_sum_in = nc.dram_tensor("cc_sum_in", [128, 2 * DT * S], F32, kind="Internal")
    cc_sum_out = nc.dram_tensor("cc_sum_out", [NCORE * 128, 2 * DT * S], F32,
                                kind="Internal")
    cc_mx_in = nc.dram_tensor("cc_mx_in", [1, 4], F32, kind="Internal")
    cc_mx_out = nc.dram_tensor("cc_mx_out", [1, 4], F32, kind="Internal")
    cc_mx2_in = nc.dram_tensor("cc_mx2_in", [1, 4], F32, kind="Internal")
    cc_mx2_out = nc.dram_tensor("cc_mx2_out", [1, 4], F32, kind="Internal")

    stack = contextlib.ExitStack()
    cx = Ctx(nc, stack)
    rg = [list(range(NCORE))]

    ident = cx.sb("ident", [128, 128], F32)
    identb = cx.sb("identb", [128, 128], BF16)
    apos = cx.sb("apos", [128, DT * S], F32)
    nega = cx.sb("nega", [128, DT * S], F32)
    bpar = cx.sb("bpar_sb", [128, DT * S], F32)
    t0s = cx.sb("t0s", [128, DT * S], F32)
    t1s = cx.sb("t1s", [128, DT * S], F32)
    bd_sb = cx.sb("bd_sb", [128, DT], F32)
    bd_row = cx.sb("bd_row", [1, D], F32)
    bo_row = cx.sb("bo_row", [1, D], F32)
    onesr = cx.sb("onesr", [1, 128], F32)
    dsum = cx.sb("dsum", [128, DT], F32)
    carry = cx.sb("carry", [128, DT * S], F32)
    aend = cx.sb("aend", [128, DT * S], F32)
    sin = cx.sb("sin", [128, DT * S], F32)
    cc = cx.sb("cc", [128, DT * S], F32)
    gmask = cx.sb("gmask_sb", [128, NCORE], F32)
    summ = cx.sb("summ", [128, 2 * DT * S], F32)
    stat = cx.sb("stat", [128, 4], F32)
    amax = cx.sb("amax", [128, 1], F32)
    amaxt = cx.sb("amaxt", [128, 1], F32)
    fin = cx.sb("fin", [1, 128], F32)
    mx4 = cx.sb("mx4", [1, 4], F32)
    clip_sb = cx.sb("clip_sb", [1, 1], F32)
    sxg = cx.sb("sxg", [1, 1], F32)
    rsxg = cx.sb("rsxg", [1, 1], F32)
    rsx_bc = cx.sb("rsx_bc", [128, 1], F32)
    sx_bc = cx.sb("sx_bc", [128, 1], F32)
    rowm = cx.sb("rowm", [128, 1], F32)
    rs_w = cx.sb("rs_w", [128, 1], F32)
    sg = cx.sb("sg", [128, IT], F32)
    su = cx.sb("su", [128, IT], F32)
    sw_col = cx.sb("sw_col", [128, DT], F32)
    swd_row = cx.sb("swd_row", [1, D], F32)
    swd_bc = cx.sb("swd_bc", [128, D], F32)

    AB = Arena(cx, "arena_b", 53504, BF16)
    AFR = Arena(cx, "arena_f", 16384, F32)
    i32b = cx.sb("i32b", [128, 1024], I32)
    i32c = cx.sb("i32c", [128, 1024], I32)
    i32d = cx.sb("i32d", [128, 1024], I32)

    ptr = [cx.ps(f"ptr{i}", [128, 128], F32) for i in range(2)]
    ptb = [cx.ps(f"ptb{i}", [128, 128], BF16) for i in range(2)]
    pmm = [cx.ps(f"pmm{i}", [128, 512], F32) for i in range(4)]

    with nc.Block() as blk:
        @blk.gpsimd
        def _(g):
            cx.inc(g.iota(ident[:], pattern=[[1, 128]], base=0,
                          channel_multiplier=-1,
                          allow_small_or_imprecise_dtypes=True), "idg0")
            cx.wait(g, "idg0")
            cx.inc(g.tensor_single_scalar(ident[:], ident[:], 0.0, ALU.is_equal),
                   "idg")

        @blk.vector
        def _(v):
            cx.wait(v, "idg")
            cx.inc(v.tensor_copy(identb[:], ident[:]), "idb")

        @blk.sync
        def _(sync):
            cx.dma(sync.dma_start(
                t0s.ap().rearrange("p (j s) -> p j s", s=S),
                alog_in.ap().rearrange("(j p) s -> p j s", p=128)), "ld0")
            cx.dma(sync.dma_start(
                bpar.ap().rearrange("p (j s) -> p j s", s=S),
                bpar_in.ap().rearrange("(j p) s -> p j s", p=128)), "ld0")
            cx.dma(sync.dma_start(
                bd_row[0:1, :], bd_in.ap().rearrange("(a d) -> a d", a=1)), "ld0")
            cx.dma(sync.dma_start(
                bo_row[0:1, :], bo_in.ap().rearrange("(a d) -> a d", a=1)), "ld0")
            cx.dma(sync.dma_start(gmask[:, :], gmask_in.ap()), "ld0")

        @blk.scalar
        def _(s):
            cx.wait(s, "ld0")
            s.activation(t1s[:], t0s[:], AF.Abs)
            s.activation(t1s[:], t1s[:], AF.Exp, scale=-1.0)
            cx.inc(s.activation(t1s[:], t1s[:], AF.Ln, bias=1.0), "sps")

        @blk.vector
        def _(v):
            cx.wait(v, "sps")
            v.tensor_scalar_max(apos[:], t0s[:], 0.0)
            cx.vsync(v)
            v.tensor_add(apos[:], apos[:], t1s[:])
            cx.vsync(v)
            v.tensor_scalar_add(apos[:], apos[:], 1e-4)
            cx.vsync(v)
            v.tensor_scalar_mul(nega[:], apos[:], -1.0)
            cx.vsync(v)
            v.memset(onesr[0:1, :], 1.0)
            v.memset(carry[:], 0.0)
            v.memset(dsum[:], 0.0)
            cx.inc(v.memset(amax[:], 0.0), "ini")

        @blk.tensor
        def _(t):
            cx.wait(t, "ini")
            cx.wait(t, "idg")
            cx.sem("bdev")
            for j in range(DT):
                if j >= 2:
                    cx.wait(t, "bdev", j - 1)
                cx.inc(t.transpose(ptr[j % 2][:, 0:1],
                                   bd_row[0:1, j * 128:(j + 1) * 128],
                                   ident[0:1, 0:1]), "bdtr")

        @blk.vector
        def _(v):
            for j in range(DT):
                cx.wait(v, "bdtr", j + 1)
                cx.inc(v.tensor_copy(bd_sb[:, j:j + 1], ptr[j % 2][:, 0:1]), "bdev")

        @blk.tensor
        def _(t):
            cx.wait(t, "bdev")

    wT_h = AB.alloc(DT * 1024)
    wT_l = AB.alloc(DT * 1024)
    wnat_h = AB.alloc(1024)
    wnat_l = AB.alloc(1024)
    AB_base_p1 = AB.off

    def emit_wsplit(blk, w_dram, tag, wstage):
        cx.sem("wev")
        cx.sem("vsp")
        cx.sem("wtr")
        b_vsp = cx.cnt["vsp"]
        b_wtr = cx.cnt["wtr"]
        b_wev = cx.cnt["wev"]
        if True:
            @blk.sync
            def _(sync):
                for j in range(DT):
                    if j >= 1:
                        cx.wait(sync, "vsp", b_vsp + j)
                    cx.dma(sync.dma_start(wstage,
                                          w_dram.ap()[j * 128:(j + 1) * 128, :]),
                           "ldw")

            @blk.vector
            def _(v):
                for j in range(DT):
                    cx.wait(v, "ldw", (cx.cnt["ldw"] - 16 * (DT - 1 - j)))
                    if j >= 1:
                        cx.wait(v, "wtr", b_wtr + 2 * DT * j)
                    v.tensor_copy(wnat_h, wstage)
                    cx.inc(v.tensor_tensor(wnat_l, wstage, wnat_h, ALU.subtract),
                           "vsp")
                    for k in range(DT):
                        cx.wait(v, "wtr", b_wtr + 2 * DT * j + 2 * (k + 1))
                        v.tensor_copy(
                            wT_h[:, k * 1024 + j * 128: k * 1024 + (j + 1) * 128],
                            ptb[0][:, 0:128])
                        cx.inc(v.tensor_copy(
                            wT_l[:, k * 1024 + j * 128: k * 1024 + (j + 1) * 128],
                            ptb[1][:, 0:128]), "wev")

            @blk.tensor
            def _(t):
                for j in range(DT):
                    cx.wait(t, "vsp", b_vsp + j + 1)
                    for k in range(DT):
                        if DT * j + k >= 1:
                            cx.wait(t, "wev", b_wev + DT * j + k)
                        t.transpose(ptb[0][:, 0:128],
                                    wnat_h[:, k * 128:(k + 1) * 128], identb[:])
                        cx.inc(t.transpose(ptb[1][:, 0:128],
                                           wnat_l[:, k * 128:(k + 1) * 128],
                                           identb[:]), "wtr", 2)

    AFR.reset()
    wstage = AFR.alloc(1024)
    xstage = AFR.alloc(TT * 1024)
    xln_tm = AFR.alloc(TT * 1024)
    xlnT = AFR.alloc(DT * TC)
    sqs = AFR.alloc(1024)
    dstage = AFR.alloc(TC)
    vstage = AFR.alloc(TC)
    sp0 = AFR.alloc(TC)
    sp1 = sqs[:, 0:TC]
    lstats1 = AFR.alloc(8 * TT)
    xh_off = AB.off
    xh = AB.alloc(DT * TC)
    xl = AB.alloc(DT * TC)

    def emit_ln2(blk, xsrc, lnout, sqs_, stats, lnmxb, mxbase, ready_sem,
                 ready_vals):
        ms = stats[:, 0:TT]
        vs = stats[:, TT:2 * TT]
        t1 = stats[:, 2 * TT:3 * TT]
        t2 = stats[:, 3 * TT:4 * TT]
        rr = stats[:, 4 * TT:5 * TT]
        nb = stats[:, 5 * TT:6 * TT]
        for nm in ("lsq", "lst", "lnq", "lns", "lnr", "lnd", "lnx"):
            cx.sem(nm)
        b_lsq = cx.cnt["lsq"]
        b_lst = cx.cnt["lst"]
        b_lnr = cx.cnt["lnr"]
        b_lnd = cx.cnt["lnd"]

        @blk.scalar
        def _(s):
            for a in range(TT):
                cx.wait(s, ready_sem, ready_vals[a])
                if b_lst + a >= 1:
                    cx.wait(s, "lst", b_lst + a)
                cx.inc(s.activation(sqs_, xsrc[:, a * D:(a + 1) * D],
                                    AF.Square), "lsq")

        @blk.vector
        def _(v):
            for a in range(TT):
                cx.wait(v, "lsq", b_lsq + a + 1)
                v.reduce_sum(ms[:, a:a + 1], xsrc[:, a * D:(a + 1) * D],
                             axis=AX.X)
                cx.inc(v.reduce_sum(vs[:, a:a + 1], sqs_, axis=AX.X), "lst")
            cx.vsync(v)
            v.tensor_scalar_mul(ms, ms, 1.0 / D)
            v.tensor_scalar_mul(vs, vs, 1.0 / D)
            cx.vsync(v)
            v.tensor_tensor(t1, ms, ms, ALU.mult)
            cx.vsync(v)
            v.tensor_tensor(vs, vs, t1, ALU.subtract)
            cx.vsync(v)
            cx.inc(v.tensor_scalar_add(t1, vs, LN_EPS), "lnq")

        @blk.scalar
        def _(s):
            cx.wait(s, "lnq")
            cx.inc(s.activation(t2, t1, AF.Sqrt), "lns")

        @blk.vector
        def _(v):
            cx.wait(v, "lns")
            v.reciprocal(rr, t2)
            cx.vsync(v)
            v.tensor_tensor(t2, rr, rr, ALU.mult)
            cx.vsync(v)
            v.tensor_tensor(t2, t1, t2, ALU.mult)
            cx.vsync(v)
            v.tensor_scalar(t2, t2, -0.5, 1.5, ALU.mult, ALU.add)
            cx.vsync(v)
            v.tensor_tensor(rr, rr, t2, ALU.mult)
            cx.vsync(v)
            v.tensor_tensor(nb, ms, rr, ALU.mult)
            cx.vsync(v)
            cx.inc(v.tensor_scalar_mul(nb, nb, -1.0), "lnr")
            cx.vsync(v)

        @blk.scalar
        def _(s):
            cx.wait(s, "lnr", b_lnr + 1)
            for a in range(TT):
                cx.inc(s.activation(lnout[:, a * D:(a + 1) * D],
                                    xsrc[:, a * D:(a + 1) * D], AF.Identity,
                                    bias=nb[:, a:a + 1],
                                    scale=rr[:, a:a + 1]), "lnd")

        if lnmxb is not None:
            @blk.vector
            def _(v):
                for a in range(TT):
                    cx.wait(v, "lnd", b_lnd + a + 1)
                    cx.inc(v.reduce_max(lnmxb[:, mxbase + a:mxbase + a + 1],
                                        lnout[:, a * D:(a + 1) * D],
                                        axis=AX.X,
                                        apply_absolute_value=True), "lnx")


    for c in range(NCH):
        b_wev1 = cx.cnt["wev"] if "wev" in cx.cnt else 0
        with nc.Block() as blk:
            @blk.sync
            def _(sync):
                cx.dma(sync.dma_start(
                    xstage.rearrange("p (a d) -> p a d", d=D),
                    x_in.ap()[c * TC:(c + 1) * TC, :].rearrange(
                        "(a p) d -> p a d", p=128)), "ldx")

            if c == 0:
                emit_wsplit(blk, wd_in, "d", wstage)

            emit_ln2(blk, xstage, xln_tm, sqs, lstats1, None, 0, "ldx",
                     [cx.cnt["ldx"]] * TT)

            @blk.tensor
            def _(t):
                cx.sem("xev")
                cx.wait(t, "lnd")
                base = cx.cnt["xev"]
                for k in range(DT):
                    for a in range(TT):
                        g = k * TT + a
                        if base + g >= 2:
                            cx.wait(t, "xev", base + g - 1)
                        cx.inc(t.transpose(
                            ptr[g % 2][:, 0:128],
                            xln_tm[:, a * D + k * 128: a * D + (k + 1) * 128],
                            ident[:]), "xtr")

            @blk.vector
            def _(v):
                base = cx.cnt["xtr"] - DT * TT
                for k in range(DT):
                    for a in range(TT):
                        g = k * TT + a
                        cx.wait(v, "xtr", base + g + 1)
                        cx.inc(v.tensor_copy(
                            xlnT[:, k * TC + a * 128: k * TC + (a + 1) * 128],
                            ptr[g % 2][:, 0:128]), "xev")
                for k in range(DT):
                    v.tensor_copy(xh[:, k * TC:(k + 1) * TC],
                                  xlnT[:, k * TC:(k + 1) * TC])
                    cx.inc(v.tensor_tensor(xl[:, k * TC:(k + 1) * TC],
                                           xlnT[:, k * TC:(k + 1) * TC],
                                           xh[:, k * TC:(k + 1) * TC],
                                           ALU.subtract), "xsp")

            for et in range(DT):
                g = c * DT + et
                pm = pmm[et % 2]

                @blk.tensor
                def _(t, et=et, g=g, pm=pm):
                    if et == 0:
                        cx.wait(t, "xsp")
                    if c == 0:
                        cx.wait(t, "wev", b_wev1 + DT * (et + 1))
                    if g >= 2:
                        cx.wait(t, "spev", g - 1)
                    first = True
                    for k in range(DT):
                        lh = wT_h[:, k * 1024 + et * 128: k * 1024 + (et + 1) * 128]
                        ll = wT_l[:, k * 1024 + et * 128: k * 1024 + (et + 1) * 128]
                        rh = xh[:, k * TC:(k + 1) * TC]
                        rl = xl[:, k * TC:(k + 1) * TC]
                        for lhs, rhs in ((lh, rh), (lh, rl), (ll, rh)):
                            last = (k == DT - 1) and (lhs is ll)
                            mi = t.matmul(pm[:, :], lhs, rhs, start=first, stop=last)
                            if last:
                                cx.inc(mi, "mm")
                            first = False

                @blk.scalar
                def _(s, et=et, g=g, pm=pm):
                    cx.wait(s, "mm")
                    if g >= 1:
                        cx.wait(s, "spev", g)
                    bcol = bd_sb[:, et:et + 1]
                    s.activation(sp0, pm[:, :], AF.Abs, bias=bcol)
                    s.activation(sp1, sp0, AF.Exp, scale=-1.0)
                    cx.inc(s.activation(xstage[:, 0:TC], sp1, AF.Ln, bias=1.0),
                           "spl")

                @blk.vector
                def _(v, et=et, g=g, pm=pm):
                    cx.wait(v, "spl")
                    if g >= 1:
                        cx.wait(v, "sto1", 32 * g)
                    bcol = bd_sb[:, et:et + 1]
                    v.scalar_tensor_tensor(dstage, pm[:, :], bcol, sp0,
                                           ALU.add, ALU.add)
                    v.scalar_tensor_tensor(dstage, dstage, 0.5,
                                           xstage[:, 0:TC],
                                           ALU.mult, ALU.add)
                    v.scalar_tensor_tensor(vstage, dstage, 0.0,
                                           xlnT[:, et * TC:(et + 1) * TC],
                                           ALU.add, ALU.mult)
                    v.reduce_sum(amaxt[:], dstage, axis=AX.X)
                    cx.vsync(v)
                    cx.inc(v.tensor_add(dsum[:, et:et + 1], dsum[:, et:et + 1],
                                        amaxt[:]), "spev")

                @blk.sync
                def _(sync, et=et, c=c):
                    cx.wait(sync, "spev")
                    cx.dma(sync.dma_start(
                        delta_hb.ap()[et * 128:(et + 1) * 128, c * TC:(c + 1) * TC],
                        dstage), "sto1")
                    cx.dma(sync.dma_start(
                        v_hb.ap()[et * 128:(et + 1) * 128, c * TC:(c + 1) * TC],
                        vstage), "sto1")

    AFR.reset()
    dj = AFR.alloc(TPC)
    vjb = AFR.alloc(TPC)
    cumj = AFR.alloc(TPC)
    y0j = AFR.alloc(TPC)
    abz = [AFR.alloc(TPC) for _ in range(2)]
    sgz = [AFR.alloc(TPC) for _ in range(2)]

    with nc.Block() as blk:
        @blk.sync
        def _(sync):
            cx.wait(sync, "sto1")

        @blk.vector
        def _(v):
            v.memset(swd_bc[:, :], 1.0)

    for nm in ("zse", "zscs", "zdone", "y0st"):
        cx.sem(nm)
    b_zse = cx.cnt["zse"]
    b_zscs = cx.cnt["zscs"]
    b_zd = cx.cnt["zdone"]
    b_y0 = cx.cnt["y0st"]
    with nc.Block() as blk:
        for j in range(DT):
            @blk.sync
            def _(sync, j=j):
                if j >= 1:
                    cx.wait(sync, "zdone", b_zd + j)
                    cx.dma(sync.dma_start(
                        y0_hb.ap()[(j - 1) * 128:j * 128, :], y0j), "y0st")
                    cx.dma(sync.dma_start(
                        cum_hb.ap()[(j - 1) * 128:j * 128, :], cumj), "y0st")
                    cx.wait(sync, "zse", b_zse + S * j)
                cx.dma(sync.dma_start(
                    dj, delta_hb.ap()[j * 128:(j + 1) * 128, :]), "ldz")
                cx.dma(sync.dma_start(
                    vjb, v_hb.ap()[j * 128:(j + 1) * 128, :]), "ldz")

            @blk.scalar
            def _(s, j=j):
                cx.wait(s, "ldz")
                for si in range(S):
                    gs = j * S + si
                    if gs >= 2:
                        cx.wait(s, "zscs", b_zscs + gs - 1)
                    cx.inc(s.activation(
                        abz[si % 2], dj, AF.Exp,
                        scale=nega[:, j * S + si:j * S + si + 1]), "zse")

            @blk.vector
            def _(v, j=j):
                cx.wait(v, "ldz")
                if j >= 1:
                    cx.wait(v, "y0st", b_y0 + 32 * j)
                half = TPC // 2
                v.tensor_tensor_scan(cumj[:, 0:half], swd_bc[:, 0:half],
                                     dj[:, 0:half], 0.0, ALU.mult, ALU.add)
                for si in range(S):
                    if si >= 2:
                        v.tensor_copy(
                            carry[:, j * S + si - 2:j * S + si - 1],
                            sgz[si % 2][:, TPC - 1:TPC])
                    cx.wait(v, "zse", b_zse + j * S + si + 1)
                    cx.inc(v.tensor_tensor_scan(
                        sgz[si % 2], abz[si % 2], vjb, 0.0,
                        ALU.mult, ALU.add), "zscs")
                    if si == 0:
                        v.tensor_tensor_scan(
                            cumj[:, half:TPC], swd_bc[:, 0:half],
                            dj[:, half:TPC], cumj[:, half - 1:half],
                            ALU.mult, ALU.add)
                        v.scalar_tensor_tensor(
                            y0j, sgz[0], bpar[:, j * S:j * S + 1], sgz[0],
                            ALU.mult, ALU.bypass)
                    else:
                        v.scalar_tensor_tensor(
                            y0j, sgz[si % 2],
                            bpar[:, j * S + si:j * S + si + 1], y0j,
                            ALU.mult, ALU.add)
                cx.vsync(v)
                v.tensor_copy(carry[:, j * S + S - 2:j * S + S - 1],
                              sgz[0][:, TPC - 1:TPC])
                cx.inc(v.tensor_copy(carry[:, j * S + S - 1:j * S + S],
                                     sgz[1][:, TPC - 1:TPC]), "zdone")

        @blk.sync
        def _(sync):
            cx.wait(sync, "zdone", b_zd + DT)
            cx.dma(sync.dma_start(
                y0_hb.ap()[(DT - 1) * 128:DT * 128, :], y0j), "y0st")
            cx.dma(sync.dma_start(
                cum_hb.ap()[(DT - 1) * 128:DT * 128, :], cumj), "y0st")
            cx.wait(sync, "y0st")

    with nc.Block() as blk:
        @blk.vector
        def _(v):
            cx.wait(v, "zdone")
            for j in range(DT):
                cx.inc(v.tensor_scalar_mul(t0s[:, j * S:(j + 1) * S],
                                           nega[:, j * S:(j + 1) * S],
                                           dsum[:, j:j + 1]), "ae0")
            cx.vsync(v)

        @blk.scalar
        def _(s):
            cx.wait(s, "ae0")
            cx.inc(s.activation(aend[:], t0s[:], AF.Exp), "ae1")

        @blk.sync
        def _(sync):
            cx.wait(sync, "ae1")
            cx.dma(sync.dma_start(cc_sum_in.ap()[:, 0:DT * S], carry[:, :]), "cci")
            cx.dma(sync.dma_start(cc_sum_in.ap()[:, DT * S:], aend[:, :]), "cci")

        @blk.gpsimd
        def _(g):
            g.wait_ge(cx.sems["cci"], cx.cnt["cci"])
            cx.inc(g.collective_compute(
                "AllGather", ALU.bypass, replica_groups=rg,
                ins=[cc_sum_in.ap()], outs=[cc_sum_out.ap()]), "cc1")

        @blk.vector
        def _(v):
            cx.wait(v, "cc1")
            cx.inc(v.memset(sin[:], 0.0), "sini")

    for p in range(NCORE):
        with nc.Block() as blk:
            @blk.sync
            def _(sync):
                if p == 0:
                    cx.wait(sync, "sini")
                cx.dma(sync.dma_start(
                    summ[:, :], cc_sum_out.ap()[p * 128:(p + 1) * 128, :]), "lds")

            @blk.vector
            def _(v):
                cx.wait(v, "lds")
                gcol = gmask[:, p:p + 1]
                v.tensor_scalar_mul(t0s[:], summ[:, DT * S:], gcol)
                v.tensor_scalar(amaxt[:], gcol, -1.0, 1.0, ALU.mult, ALU.add)
                cx.vsync(v)
                v.tensor_scalar_add(t0s[:], t0s[:], amaxt[:, 0:1])
                cx.vsync(v)
                v.tensor_mul(sin[:], sin[:], t0s[:])
                v.tensor_scalar_mul(t1s[:], summ[:, 0:DT * S], gcol)
                cx.vsync(v)
                cx.inc(v.tensor_add(sin[:], sin[:], t1s[:]), "scomp")
                cx.vsync(v)

    with nc.Block() as blk:
        @blk.vector
        def _(v):
            cx.wait(v, "scomp")
            cx.inc(v.tensor_tensor(cc[:], bpar[:], sin[:], ALU.mult), "ccd")

    AFR.reset()
    cu2 = [AFR.alloc(TPC) for _ in range(2)]
    yy = [AFR.alloc(TPC) for _ in range(2)]
    pfac = [AFR.alloc(TPC) for _ in range(2)]
    wstage2 = AFR.alloc(1024)
    AB.off = xh_off
    yh = AB.alloc(DT * TPC)
    yl = AB.alloc(DT * TPC)

    for nm in ("pfe", "yco", "ysp2", "ldy"):
        cx.sem(nm)
    b_pfe = cx.cnt["pfe"]
    b_yco = cx.cnt["yco"]
    b_ysp2 = cx.cnt["ysp2"]
    with nc.Block() as blk:
        emit_wsplit(blk, wo_in, "o", wstage2)
        for j in range(DT):
            jb = j % 2

            @blk.sync
            def _(sync, j=j, jb=jb):
                if j >= 2:
                    cx.wait(sync, "pfe", b_pfe + S * (j - 1))
                    cx.wait(sync, "ysp2", b_ysp2 + j - 1)
                cx.dma(sync.dma_start(
                    cu2[jb], cum_hb.ap()[j * 128:(j + 1) * 128, :]), "ldy")
                cx.dma(sync.dma_start(
                    yy[jb], y0_hb.ap()[j * 128:(j + 1) * 128, :]), "ldy")

            @blk.scalar
            def _(s, j=j, jb=jb):
                cx.wait(s, "ldy", 32 * (j + 1))
                if j == 0:
                    cx.wait(s, "ccd")
                for si in range(S):
                    gs = j * S + si
                    if gs >= 2:
                        cx.wait(s, "yco", b_yco + gs - 1)
                    cx.inc(s.activation(
                        pfac[si % 2], cu2[jb], AF.Exp,
                        scale=nega[:, j * S + si:j * S + si + 1]), "pfe")

            @blk.vector
            def _(v, j=j, jb=jb):
                for si in range(S):
                    cx.wait(v, "pfe", b_pfe + j * S + si + 1)
                    cx.inc(v.scalar_tensor_tensor(
                        yy[jb], pfac[si % 2],
                        cc[:, j * S + si:j * S + si + 1], yy[jb],
                        ALU.mult, ALU.add), "yco")
                v.tensor_copy(yh[:, j * TPC:(j + 1) * TPC], yy[jb])
                cx.inc(v.scalar_tensor_tensor(
                    yl[:, j * TPC:(j + 1) * TPC], yy[jb], 0.0,
                    yh[:, j * TPC:(j + 1) * TPC],
                    ALU.add, ALU.subtract), "ysp2")

    AFR.reset()
    x2s = AFR.alloc(TT * 1024)
    ln2s = AFR.alloc(TT * 1024)
    sqs2 = AFR.alloc(1024)
    lstats = AFR.alloc(8 * TT)
    lnmx = AFR.alloc(NCH * TT)

    for c in range(NCH):
        with nc.Block() as blk:
            @blk.sync
            def _(sync):
                cx.dma(sync.dma_start(
                    x2s.rearrange("p (a d) -> p a d", d=D),
                    x_in.ap()[c * TC:(c + 1) * TC, :].rearrange(
                        "(a p) d -> p a d", p=128)), "ldx2")

            for a in range(TT):
                gt = c * TT + a

                @blk.tensor
                def _(t, a=a, gt=gt):
                    for eb in range(2):
                        gg = 2 * gt + eb
                        if gg >= 2:
                            cx.wait(t, "x2ev", gg - 1)
                        t.matmul(pmm[eb][:, 0:512], onesr[0:1, :],
                                 bo_row[0:1, eb * 512:(eb + 1) * 512],
                                 start=True, stop=False)
                        for k in range(DT):
                            lh = yh[:, k * TPC + c * TC + a * 128:
                                    k * TPC + c * TC + (a + 1) * 128]
                            ll = yl[:, k * TPC + c * TC + a * 128:
                                    k * TPC + c * TC + (a + 1) * 128]
                            rh = wT_h[:, k * 1024 + eb * 512: k * 1024 + (eb + 1) * 512]
                            rl = wT_l[:, k * 1024 + eb * 512: k * 1024 + (eb + 1) * 512]
                            for lhs, rhs in ((lh, rh), (lh, rl), (ll, rh)):
                                last = (k == DT - 1) and (lhs is ll)
                                mi = t.matmul(pmm[eb][:, 0:512], lhs, rhs,
                                              start=False, stop=last)
                                if last:
                                    cx.inc(mi, "mmo")

                @blk.vector
                def _(v, a=a):
                    cx.wait(v, "ldx2")
                    xrow = x2s[:, a * D:(a + 1) * D]
                    for eb in range(2):
                        cx.wait(v, "mmo", cx.cnt["mmo"] - (2 - 1 - eb))
                        sl = slice(eb * 512, (eb + 1) * 512)
                        cx.inc(v.scalar_tensor_tensor(
                            xrow[:, sl], pmm[eb][:, 0:512], 0.0, xrow[:, sl],
                            ALU.add, ALU.add), "x2ev")

            x2ev_now = cx.cnt["x2ev"]
            emit_ln2(blk, x2s, ln2s, sqs2, lstats, lnmx, c * TT,
                     "x2ev", [x2ev_now - 2 * (TT - 1 - a) for a in range(TT)])

            @blk.sync
            def _(sync):
                cx.wait(sync, "lnd")
                cx.dma(sync.dma_start(
                    x2_hb.ap()[c * TC:(c + 1) * TC, :].rearrange(
                        "(a p) d -> p a d", p=128),
                    x2s.rearrange("p (a d) -> p a d", d=D)), "sto2")
                cx.dma(sync.dma_start(
                    ln2_hb.ap()[c * TC:(c + 1) * TC, :].rearrange(
                        "(a p) d -> p a d", p=128),
                    ln2s.rearrange("p (a d) -> p a d", d=D)), "sto2")
                cx.wait(sync, "sto2")

    with nc.Block() as blk:
        @blk.vector
        def _(v):
            cx.wait(v, "lnx")
            cx.inc(v.reduce_max(amax[:], lnmx[:, 0:NCH * TT], axis=AX.X),
                   "lfm")

        @blk.tensor
        def _(t):
            cx.wait(t, "lfm")

    def emit_allreduce_max(cc_in_d, cc_out_d, tag):
        with nc.Block() as blk:
            @blk.tensor
            def _(t):
                cx.inc(t.transpose(ptr[0][0:1, 0:128], amax[:, 0:1], ident[:]),
                       "ftr")

            @blk.vector
            def _(v):
                cx.wait(v, "ftr")
                v.tensor_copy(fin[0:1, :], ptr[0][0:1, 0:128])
                cx.vsync(v)
                v.reduce_max(mx4[0:1, 0:1], fin[0:1, :], axis=AX.X)
                cx.vsync(v)
                v.tensor_copy(mx4[0:1, 1:2], mx4[0:1, 0:1])
                v.tensor_copy(mx4[0:1, 2:3], mx4[0:1, 0:1])
                cx.inc(v.tensor_copy(mx4[0:1, 3:4], mx4[0:1, 0:1]), "fmx")

            @blk.sync
            def _(sync):
                cx.wait(sync, "fmx")
                cx.dma(sync.dma_start(cc_in_d.ap(), mx4[0:1, :]), "ccm")

            @blk.gpsimd
            def _(g):
                g.wait_ge(cx.sems["ccm"], cx.cnt["ccm"])
                cx.inc(g.collective_compute(
                    "AllReduce", ALU.max, replica_groups=rg,
                    ins=[cc_in_d.ap()], outs=[cc_out_d.ap()]), "ccr")

            @blk.sync
            def _(sync):
                cx.wait(sync, "ccr")
                cx.dma(sync.dma_start(mx4[0:1, :], cc_out_d.ap()), "ccl")

    def emit_scale_prep(clip_dram, tag):
        with nc.Block() as blk:
            @blk.sync
            def _(sync):
                cx.dma(sync.dma_start(clip_sb[0:1, :], clip_dram.ap()),
                       "ldc")

            @blk.scalar
            def _(s):
                cx.wait(s, "ldc")
                cx.inc(s.activation(sxg[0:1, :], clip_sb[0:1, :], AF.Sigmoid),
                       "sg")

            @blk.vector
            def _(v):
                cx.wait(v, "sg")
                cx.wait(v, "ccl")
                v.tensor_mul(sxg[0:1, :], sxg[0:1, :], mx4[0:1, 0:1])
                cx.vsync(v)
                v.tensor_scalar_mul(sxg[0:1, :], sxg[0:1, :], 1.0 / QMAX)
                cx.vsync(v)
                cx.inc(v.reciprocal(rsxg[0:1, :], sxg[0:1, :]), "rs")
                cx.vsync(v)

            @blk.tensor
            def _(t):
                cx.wait(t, "rs")
                t.matmul(ptr[0][:, 0:1], onesr[0:1, :], rsxg[0:1, 0:1],
                         start=True, stop=True)
                cx.inc(t.matmul(ptr[1][:, 0:1], onesr[0:1, :], sxg[0:1, 0:1],
                                start=True, stop=True), "rb")

            @blk.vector
            def _(v):
                cx.wait(v, "rb")
                v.tensor_copy(rsx_bc[:, 0:1], ptr[0][:, 0:1])
                cx.inc(v.tensor_copy(sx_bc[:, 0:1], ptr[1][:, 0:1]), "rb2")

            @blk.tensor
            def _(t):
                cx.wait(t, "rb2")

    emit_allreduce_max(cc_mx_in, cc_mx_out, "1")
    emit_scale_prep(clipx_in, "1")

    AB.reset()
    wqgT = AB.alloc(DT * DI)
    wquT = AB.alloc(DT * DI)
    wq_nat = AB.alloc(1024)
    wq_natB = AB.alloc(1024)
    xq_tm = AB.alloc(1024)
    xqT = AB.alloc(DT * TC)
    AFR.reset()
    wsf = [AFR.alloc(1024) for _ in range(2)]
    lstage = AFR.alloc(TT * 1024)
    tq = AFR.alloc(1024)
    dbq = AFR.alloc(1024)
    dbqB = AFR.alloc(1024)
    c1q = AFR.alloc(1024)
    silu = [AFR.alloc(TC) for _ in range(2)]
    hst = [AFR.alloc(TC) for _ in range(2)]
    hmx = AFR.alloc(NCH * IT)
    rowcat = AFR.alloc(2 * IT)
    rs52 = AFR.alloc(2 * IT)

    strips = [(wg_in, wqgT, it) for it in range(IT)] + \
             [(wu_in, wquT, it) for it in range(IT)]
    NS2 = len(strips)
    for nm in ("rmx", "lwq", "lwq2", "wt0", "wrq", "wtt", "wtv", "wsc2",
               "sev"):
        cx.sem(nm)
    b_rmx = cx.cnt["rmx"]
    b_wt0 = cx.cnt["wt0"]
    b_wrq = cx.cnt["wrq"]
    b_wtt = cx.cnt["wtt"]
    with nc.Block() as blk:
        for t_, (w_dram, wqt, row) in enumerate(strips):
            pb = t_ % 2

            @blk.sync
            def _(sync, w_dram=w_dram, row=row, pb=pb, t_=t_):
                if t_ >= 2:
                    cx.wait(sync, "rmx", b_rmx + t_ - 1)
                cx.dma(sync.dma_start(
                    wsf[pb], w_dram.ap()[row * 128:(row + 1) * 128, :]),
                    "lwq")

            @blk.vector
            def _(v, pb=pb, t_=t_):
                cx.wait(v, "lwq")
                cx.inc(v.reduce_max(rowcat[:, t_:t_ + 1], wsf[pb], axis=AX.X,
                                    apply_absolute_value=True), "rmx")

        @blk.vector
        def _(v):
            cx.wait(v, "rmx")
            cx.vsync(v)
            v.reciprocal(rs52[:, 0:NS2], rowcat[:, 0:NS2])
            cx.vsync(v)
            cx.inc(v.tensor_scalar_mul(rs52[:, 0:NS2], rs52[:, 0:NS2], QMAX),
                   "wsc2")
            v.tensor_scalar_mul(sg[:, :], rowcat[:, 0:IT], 1.0 / QMAX)
            cx.inc(v.tensor_scalar_mul(su[:, :], rowcat[:, IT:2 * IT],
                                       1.0 / QMAX), "wsc2")
            cx.wait(v, "rb2")
            cx.vsync(v)
            v.scalar_tensor_tensor(sg[:, :], sg[:, :], sx_bc[:, 0:1],
                                   sg[:, :], ALU.mult, ALU.bypass)
            cx.inc(v.scalar_tensor_tensor(su[:, :], su[:, :], sx_bc[:, 0:1],
                                          su[:, :], ALU.mult, ALU.bypass),
                   "sev")
            cx.vsync(v)

        for t_, (w_dram, wqt, row) in enumerate(strips):
            pb = t_ % 2
            dq = dbq if pb == 0 else dbqB
            iq = i32b[:, :] if pb == 0 else i32c[:, :]
            wqn = wq_nat if pb == 0 else wq_natB

            @blk.sync
            def _(sync, w_dram=w_dram, row=row, pb=pb, t_=t_):
                if t_ >= 2:
                    cx.wait(sync, "wt0", b_wt0 + t_ - 1)
                cx.dma(sync.dma_start(
                    wsf[pb], w_dram.ap()[row * 128:(row + 1) * 128, :]),
                    "lwq2")

            @blk.scalar
            def _(s, pb=pb, t_=t_, dq=dq, iq=iq):
                cx.wait(s, "lwq2")
                if t_ == 0:
                    cx.wait(s, "wsc2", 1)
                if t_ >= 2:
                    cx.wait(s, "wrq", b_wrq + t_ - 1)
                s.activation(dq, wsf[pb], AF.Identity,
                             scale=rs52[:, t_:t_ + 1])
                cx.inc(s.activation(iq, wsf[pb], AF.Identity,
                                    scale=rs52[:, t_:t_ + 1]), "wt0")

            if t_ >= 1:
                @blk.vector
                def _(v, t_=t_):
                    pv = (t_ - 1) % 2
                    wqtp, rowp = strips[t_ - 1][1], strips[t_ - 1][2]
                    for jj in range(DT):
                        cx.wait(v, "wtt", b_wtt + (t_ - 1) * DT + jj + 1)
                        cx.inc(v.tensor_copy(
                            wqtp[:, jj * DI + rowp * 128:
                                 jj * DI + (rowp + 1) * 128],
                            ptb[jj % 2][:, 0:128]), "wtv")

            @blk.vector
            def _(v, pb=pb, t_=t_, dq=dq, iq=iq, wqn=wqn):
                cx.wait(v, "wt0", b_wt0 + t_ + 1)
                if t_ >= 2:
                    cx.wait(v, "wtt", b_wtt + (t_ - 1) * DT)
                v.scalar_tensor_tensor(c1q, dq, 0.0, iq,
                                       ALU.add, ALU.subtract)
                v.scalar_tensor_tensor(dq, c1q, -0.5, iq,
                                       ALU.is_lt, ALU.subtract)
                cx.inc(v.scalar_tensor_tensor(wqn, c1q, 0.5, dq,
                                              ALU.is_gt, ALU.subtract),
                       "wrq")

            @blk.tensor
            def _(t, t_=t_, wqn=wqn):
                cx.wait(t, "wrq", b_wrq + t_ + 1)
                base = cx.cnt["wtv"]
                for jj in range(DT):
                    if base + jj >= 2:
                        cx.wait(t, "wtv", base + jj - 1)
                    cx.inc(t.transpose(ptb[jj % 2][:, 0:128],
                                       wqn[:, jj * 128:(jj + 1) * 128],
                                       identb[:]), "wtt")

        @blk.vector
        def _(v):
            t_ = NS2
            wqtp, rowp = strips[t_ - 1][1], strips[t_ - 1][2]
            for jj in range(DT):
                cx.wait(v, "wtt", b_wtt + (t_ - 1) * DT + jj + 1)
                cx.inc(v.tensor_copy(
                    wqtp[:, jj * DI + rowp * 128:
                         jj * DI + (rowp + 1) * 128],
                    ptb[jj % 2][:, 0:128]), "wtv")

        @blk.tensor
        def _(t):
            cx.wait(t, "wtv")

    for c in range(NCH):
        with nc.Block() as blk:
            @blk.sync
            def _(sync):
                if c > 0:
                    cx.wait(sync, "ch3", c)
                cx.dma(sync.dma_start(
                    lstage.rearrange("p (a d) -> p a d", d=D),
                    ln2_hb.ap()[c * TC:(c + 1) * TC, :].rearrange(
                        "(a p) d -> p a d", p=128)), "ldl")

            @blk.scalar
            def _(s):
                cx.wait(s, "ldl")
                if c == 0:
                    cx.wait(s, "sev")
                for a in range(TT):
                    g = c * TT + a
                    if g >= 1:
                        cx.wait(s, "xqr", g)
                    s.activation(tq, lstage[:, a * D:(a + 1) * D],
                                 AF.Identity, scale=rsx_bc[:, 0:1])
                    cx.inc(s.activation(i32d[:, :],
                                        lstage[:, a * D:(a + 1) * D],
                                        AF.Identity, scale=rsx_bc[:, 0:1]),
                           "xqs")

            @blk.vector
            def _(v):
                cx.sem("xqtr")
                for a in range(TT):
                    g = c * TT + a
                    cx.wait(v, "xqs", g + 1)
                    if g >= 1:
                        cx.wait(v, "xqtr", g * DT)
                    v.scalar_tensor_tensor(dbq, tq, 0.0, i32d[:, :],
                                           ALU.add, ALU.subtract)
                    v.scalar_tensor_tensor(c1q, dbq, -0.5, i32d[:, :],
                                           ALU.is_lt, ALU.subtract)
                    v.scalar_tensor_tensor(tq, dbq, 0.5, c1q,
                                           ALU.is_gt, ALU.subtract)
                    cx.inc(v.tensor_scalar(xq_tm, tq, QMAX, -QMAX - 1.0,
                                           ALU.min, ALU.max), "xqr")
                    for k in range(DT):
                        gg = g * DT + k
                        cx.wait(v, "xqtr", gg + 1)
                        cx.inc(v.tensor_copy(
                            xqT[:, k * TC + a * 128: k * TC + (a + 1) * 128],
                            ptb[k % 2][:, 0:128]), "xqev")

            @blk.tensor
            def _(t):
                for a in range(TT):
                    g = c * TT + a
                    cx.wait(t, "xqr", g + 1)
                    for k in range(DT):
                        gg = g * DT + k
                        if gg >= 2:
                            cx.wait(t, "xqev", gg - 1)
                        cx.inc(t.transpose(ptb[k % 2][:, 0:128],
                                           xq_tm[:, k * 128:(k + 1) * 128],
                                           identb[:]), "xqtr")

            for it in range(IT):
                gi = c * IT + it
                pg = pmm[2 * (it % 2)]
                pu = pmm[2 * (it % 2) + 1]

                @blk.tensor
                def _(t, it=it, gi=gi, pg=pg, pu=pu):
                    if it == 0:
                        cx.wait(t, "xqev")
                    if gi >= 2:
                        cx.wait(t, "hev", gi - 1)
                    for pi, (pm, wqt) in enumerate(((pg, wqgT), (pu, wquT))):
                        first = True
                        for k in range(DT):
                            lhs = wqt[:, k * DI + it * 128: k * DI + (it + 1) * 128]
                            rhs = xqT[:, k * TC:(k + 1) * TC]
                            mi = t.matmul(pm[:, 0:512], lhs, rhs, start=first,
                                          stop=(k == DT - 1))
                            if pi == 1 and k == DT - 1:
                                cx.inc(mi, "mg")
                            first = False

                @blk.scalar
                def _(s, it=it, gi=gi, pg=pg):
                    cx.wait(s, "mg")
                    if gi >= 2:
                        cx.wait(s, "hev", gi - 1)
                    cx.inc(s.activation(silu[gi % 2], pg[:, 0:512], AF.Silu,
                                        scale=sg[:, it:it + 1]), "sil")

                @blk.vector
                def _(v, it=it, gi=gi, pu=pu):
                    cx.wait(v, "sil")
                    if gi >= 2:
                        cx.wait(v, "sth", 16 * (gi - 1))
                    v.scalar_tensor_tensor(hst[gi % 2], pu[:, 0:512],
                                           su[:, it:it + 1],
                                           silu[gi % 2], ALU.mult, ALU.mult)
                    cx.inc(v.reduce_max(hmx[:, gi:gi + 1], hst[gi % 2],
                                        axis=AX.X,
                                        apply_absolute_value=True), "hev")

                @blk.sync
                def _(sync, it=it, gi=gi, c=c):
                    cx.wait(sync, "hev")
                    cx.dma(sync.dma_start(
                        h_hb.ap()[it * 128:(it + 1) * 128, c * TC:(c + 1) * TC],
                        hst[gi % 2]), "sth")

            @blk.sync
            def _(sync):
                cx.wait(sync, "sth")
                cx.inc(sync.nop(), "ch3")

    with nc.Block() as blk:
        @blk.vector
        def _(v):
            cx.inc(v.reduce_max(amax[:], hmx[:, 0:NCH * IT], axis=AX.X), "hfm")

        @blk.tensor
        def _(t):
            cx.wait(t, "hfm")

    emit_allreduce_max(cc_mx2_in, cc_mx2_out, "2")
    emit_scale_prep(clipd_in, "2")

    AB.reset()
    wdnT = AB.alloc(IT * 1024)
    wq_nat4 = AB.alloc(DI)
    hq = AB.alloc(IT * TC4)
    AFR.reset()
    wstrip4 = AFR.alloc(DI)
    dbq4 = AFR.alloc(1024)
    c1q4 = AFR.alloc(1024)
    hstage = AFR.alloc(IT * TC4)
    x2stage = AFR.alloc(TT4 * 1024)
    outst = AFR.alloc(TT4 * 1024)
    tq4 = [AFR.alloc(TC4) for _ in range(4)]
    d4 = AFR.alloc(TC4)
    c4 = AFR.alloc(TC4)

    for nm in ("wseg", "wt0", "wtt", "wtv"):
        cx.sem(nm)
    with nc.Block() as blk:
        for jt in range(DT):
            b_wt0 = cx.cnt["wt0"]
            b_wseg = cx.cnt["wseg"]
            b_wtt = cx.cnt["wtt"]

            @blk.sync
            def _(sync, jt=jt, b_wt0=b_wt0):
                if jt >= 1:
                    cx.wait(sync, "wt0", b_wt0)
                cx.dma(sync.dma_start(wstrip4,
                                      wdn_in.ap()[jt * 128:(jt + 1) * 128, :]),
                       "lwd")

            @blk.vector
            def _(v, jt=jt, b_wt0=b_wt0):
                cx.wait(v, "lwd")
                v.reduce_max(rowm[:], wstrip4, axis=AX.X,
                             apply_absolute_value=True)
                cx.vsync(v)
                if jt >= 1:
                    cx.wait(v, "wt0", b_wt0)
                v.reciprocal(rs_w[:], rowm[:])
                cx.vsync(v)
                cx.inc(v.tensor_scalar_mul(rs_w[:], rs_w[:], QMAX), "wrs")
                cx.inc(v.tensor_scalar_mul(sw_col[:, jt:jt + 1], rowm[:],
                                           1.0 / QMAX), "wsc")

            @blk.scalar
            def _(s, b_wseg=b_wseg):
                cx.wait(s, "wrs")
                for seg in range(3):
                    lo = seg * 1024
                    hi = min(DI, lo + 1024)
                    if b_wseg + seg >= 1:
                        cx.wait(s, "wseg", b_wseg + seg)
                    s.activation(dbq4[:, 0:hi - lo], wstrip4[:, lo:hi],
                                 AF.Identity, scale=rs_w[:, 0:1])
                    cx.inc(s.activation(i32b[:, 0:hi - lo], wstrip4[:, lo:hi],
                                        AF.Identity, scale=rs_w[:, 0:1]), "wt0")

            @blk.vector
            def _(v, jt=jt, b_wtt=b_wtt):
                if jt >= 1:
                    cx.wait(v, "wtt", b_wtt)
                for seg in range(3):
                    lo = seg * 1024
                    hi = min(DI, lo + 1024)
                    w_ = hi - lo
                    cx.wait(v, "wt0", cx.cnt["wt0"] - (3 - 1 - seg))
                    v.scalar_tensor_tensor(c1q4[:, 0:w_], dbq4[:, 0:w_], 0.0,
                                           i32b[:, 0:w_], ALU.add,
                                           ALU.subtract)
                    v.scalar_tensor_tensor(dbq4[:, 0:w_], c1q4[:, 0:w_], -0.5,
                                           i32b[:, 0:w_], ALU.is_lt,
                                           ALU.subtract)
                    cx.inc(v.scalar_tensor_tensor(wq_nat4[:, lo:hi], c1q4[:, 0:w_],
                                                  0.5, dbq4[:, 0:w_], ALU.is_gt,
                                                  ALU.subtract), "wseg")
                cx.inc(v.memset(stat[:, 0:1], 0.0), "wrq")

            @blk.tensor
            def _(t):
                cx.wait(t, "wrq")
                base = cx.cnt["wtv"]
                for jj in range(IT):
                    if base + jj >= 2:
                        cx.wait(t, "wtv", base + jj - 1)
                    cx.inc(t.transpose(ptb[jj % 2][:, 0:128],
                                       wq_nat4[:, jj * 128:(jj + 1) * 128],
                                       identb[:]), "wtt")

            @blk.vector
            def _(v, jt=jt):
                for jj in range(IT):
                    cx.wait(v, "wtt", cx.cnt["wtt"] - (IT - 1 - jj))
                    cx.inc(v.tensor_copy(
                        wdnT[:, jj * 1024 + jt * 128: jj * 1024 + (jt + 1) * 128],
                        ptb[jj % 2][:, 0:128]), "wtv")

        @blk.tensor
        def _(t):
            cx.wait(t, "wtv")

    with nc.Block() as blk:
        @blk.tensor
        def _(t):
            cx.wait(t, "wsc")
            cx.sem("srev")
            base = cx.cnt["srev"]
            for jt in range(DT):
                if jt >= 2:
                    cx.wait(t, "srev", base + jt - 1)
                cx.inc(t.transpose(ptr[jt % 2][0:1, 0:128], sw_col[:, jt:jt + 1],
                                   ident[:]), "str")

        @blk.vector
        def _(v):
            for jt in range(DT):
                cx.wait(v, "str", cx.cnt["str"] - (DT - 1 - jt))
                cx.inc(v.tensor_copy(swd_row[0:1, jt * 128:(jt + 1) * 128],
                                     ptr[jt % 2][0:1, 0:128]), "srev")
            cx.wait(v, "rs")
            cx.vsync(v)
            cx.inc(v.tensor_scalar_mul(swd_row[0:1, :], swd_row[0:1, :],
                                       sxg[0:1, 0:1]), "srow")
            cx.vsync(v)

        @blk.tensor
        def _(t):
            cx.wait(t, "srow")
            for eb in range(2):
                cx.inc(t.matmul(pmm[eb][:, 0:512], onesr[0:1, :],
                                swd_row[0:1, eb * 512:(eb + 1) * 512],
                                start=True, stop=True), "sbc")

        @blk.vector
        def _(v):
            cx.wait(v, "sbc")
            for eb in range(2):
                cx.inc(v.tensor_copy(swd_bc[:, eb * 512:(eb + 1) * 512],
                                     pmm[eb][:, 0:512]), "sbv")

        @blk.tensor
        def _(t):
            cx.wait(t, "sbv")

    for nm in ("ldh", "ldx4", "hqs", "hqr", "md", "oev", "sto4"):
        cx.sem(nm)
    for c in range(NCH4):
        b_hqs = cx.cnt["hqs"]
        b_hqr = cx.cnt["hqr"]
        b_md = cx.cnt["md"]
        with nc.Block() as blk:
            @blk.sync
            def _(sync, c=c):
                cx.dma(sync.dma_start(
                    hstage.rearrange("p (k t) -> p k t", t=TC4),
                    h_hb.ap()[:, c * TC4:(c + 1) * TC4].rearrange(
                        "(k p) t -> p k t", p=128)), "ldh")
                cx.dma(sync.dma_start(
                    x2stage.rearrange("p (a d) -> p a d", d=D),
                    x2_hb.ap()[c * TC4:(c + 1) * TC4, :].rearrange(
                        "(a p) d -> p a d", p=128)), "ldx4")

            @blk.scalar
            def _(s, c=c, b_hqr=b_hqr):
                cx.wait(s, "ldh")
                if c == 0:
                    cx.wait(s, "rb2")
                for k in range(IT):
                    pb = k % 4
                    if k >= 4:
                        cx.wait(s, "hqr", b_hqr + k - 3)
                    s.activation(tq4[pb], hstage[:, k * TC4:(k + 1) * TC4],
                                 AF.Identity, scale=rsx_bc[:, 0:1])
                    cx.inc(s.activation(
                        i32b[:, pb * TC4:(pb + 1) * TC4],
                        hstage[:, k * TC4:(k + 1) * TC4],
                        AF.Identity, scale=rsx_bc[:, 0:1]), "hqs")

            @blk.vector
            def _(v, c=c, b_hqs=b_hqs):
                for k in range(IT):
                    pb = k % 4
                    i4 = i32b[:, pb * TC4:(pb + 1) * TC4]
                    cx.wait(v, "hqs", b_hqs + k + 1)
                    v.scalar_tensor_tensor(d4, tq4[pb], 0.0, i4,
                                           ALU.add, ALU.subtract)
                    v.scalar_tensor_tensor(c4, d4, -0.5, i4,
                                           ALU.is_lt, ALU.subtract)
                    v.scalar_tensor_tensor(tq4[pb], d4, 0.5, c4, ALU.is_gt,
                                           ALU.subtract)
                    cx.inc(v.tensor_scalar(hq[:, k * TC4:(k + 1) * TC4],
                                           tq4[pb],
                                           QMAX, -QMAX - 1.0, ALU.min,
                                           ALU.max), "hqr")

            @blk.tensor
            def _(t, c=c, b_hqr=b_hqr):
                for k in range(IT):
                    cx.wait(t, "hqr", b_hqr + k + 1)
                    for a in range(TT4):
                        for eb in range(2):
                            lhs = hq[:, k * TC4 + a * 128:
                                     k * TC4 + (a + 1) * 128]
                            rhs = wdnT[:, k * 1024 + eb * 512:
                                       k * 1024 + (eb + 1) * 512]
                            mi = t.matmul(pmm[2 * a + eb][:, 0:512], lhs, rhs,
                                          start=(k == 0), stop=(k == IT - 1))
                            if k == IT - 1:
                                cx.inc(mi, "md")

            @blk.vector
            def _(v, c=c, b_md=b_md):
                cx.wait(v, "ldx4")
                if c >= 1:
                    cx.wait(v, "sto4", 16 * c)
                for a in range(TT4):
                    x2row = x2stage[:, a * D:(a + 1) * D]
                    orow = outst[:, a * D:(a + 1) * D]
                    for eb in range(2):
                        cx.wait(v, "md", b_md + 2 * a + eb + 1)
                        sl = slice(eb * 512, (eb + 1) * 512)
                        v.scalar_tensor_tensor(orow[:, sl],
                                               pmm[2 * a + eb][:, 0:512],
                                               0.0, swd_bc[:, sl],
                                               ALU.add, ALU.mult)
                        cx.inc(v.scalar_tensor_tensor(orow[:, sl],
                                                      x2row[:, sl], 0.0,
                                                      orow[:, sl],
                                                      ALU.add, ALU.add),
                               "oev")

            @blk.sync
            def _(sync, c=c):
                cx.wait(sync, "oev")
                cx.dma(sync.dma_start(
                    out_out.ap()[c * TC4:(c + 1) * TC4, :].rearrange(
                        "(a p) d -> p a d", p=128),
                    outst.rearrange("p (a d) -> p a d", d=D)), "sto4")

    with nc.Block() as blk:
        @blk.sync
        def _(sync):
            cx.wait(sync, "sto4")
    stack.close()
    return nc


from concourse.bass_utils import run_bass_kernel_spmd

_CACHE = {}


def kernel(L0_attn_x, L0_attn_Wd, L0_attn_bd, L0_attn_a_log, L0_attn_b_param,
           L0_attn_Wo, L0_attn_bo, L0_ffn_clip_x, L0_ffn_clip_down,
           L0_ffn_w_gate, L0_ffn_w_up, L0_ffn_w_down):
    x = np.ascontiguousarray(np.asarray(L0_attn_x, dtype=np.float32))
    wd = np.ascontiguousarray(np.asarray(L0_attn_Wd, dtype=np.float32))
    bd = np.ascontiguousarray(np.asarray(L0_attn_bd, dtype=np.float32))
    alog = np.ascontiguousarray(np.asarray(L0_attn_a_log, dtype=np.float32))
    bpar = np.ascontiguousarray(np.asarray(L0_attn_b_param, dtype=np.float32))
    wo = np.ascontiguousarray(np.asarray(L0_attn_Wo, dtype=np.float32))
    bo = np.ascontiguousarray(np.asarray(L0_attn_bo, dtype=np.float32))
    clip_x = np.asarray(L0_ffn_clip_x, dtype=np.float32).reshape(1, 1)
    clip_dn = np.asarray(L0_ffn_clip_down, dtype=np.float32).reshape(1, 1)
    wg = np.ascontiguousarray(np.asarray(L0_ffn_w_gate, dtype=np.float32))
    wu = np.ascontiguousarray(np.asarray(L0_ffn_w_up, dtype=np.float32))
    wdn = np.ascontiguousarray(np.asarray(L0_ffn_w_down, dtype=np.float32))

    if "nc" not in _CACHE:
        _CACHE["nc"] = build_fused()
    nc = _CACHE["nc"]
    in_maps = []
    for i in range(NCORE):
        b, q = divmod(i, 4)
        gm = np.zeros((128, NCORE), dtype=np.float32)
        gm[:, b * 4:b * 4 + q] = 1.0
        in_maps.append({
            "x": np.ascontiguousarray(x[b, q * TPC:(q + 1) * TPC]),
            "wd": wd, "bd": bd, "alog": alog, "bpar": bpar, "wo": wo, "bo": bo,
            "clip_x": clip_x, "clip_down": clip_dn, "wg": wg, "wu": wu,
            "wdn": wdn, "gmask": gm})
    res = run_bass_kernel_spmd(nc, in_maps, core_ids=list(range(NCORE)))
    out = np.empty((B, T, D), dtype=np.float32)
    for i in range(NCORE):
        b, q = divmod(i, 4)
        out[b, q * TPC:(q + 1) * TPC] = res.results[i]["out"]
    return out
